# version 01
# baseline (speedup 1.0000x reference)
import contextlib
import math
import numpy as np
import ml_dtypes

import concourse.bass as bass
import concourse.mybir as mybir
from concourse.bass_utils import run_bass_kernel_spmd

F32 = mybir.dt.float32
F32R = mybir.dt.float32r
BF16 = mybir.dt.bfloat16
AF = mybir.ActivationFunctionType
ALU = mybir.AluOpType
AXX = mybir.AxisListType.X
BF = ml_dtypes.bfloat16

DM = 4096
TOK = 1024
TQ = 1027
TKV = 1664
KC = 32
NEG = -1e30
FCONSTS = (1e-5, -0.5, 0.5, 2.0)

ENGINES = ("sync", "scalar", "vector", "tensor", "gpsimd")


class Sched:
    def __init__(self):
        self.steps = {e: [] for e in ENGINES}

    def on(self, eng, fn):
        self.steps[eng].append(fn)

    def emit(self, nc):
        with nc.Block() as block:
            for name in ENGINES:
                def mk(nm):
                    def body(eng):
                        for fn in self.steps[nm]:
                            fn(eng)
                    return body
                getattr(block, name)(mk(name))


class Sem:
    def __init__(self, ctx, nc, name):
        self.h = ctx.enter_context(nc.semaphore(name))
        self.n = 0

    def next(self, k=1):
        self.n += k
        return self.n


class Prog:

    def __init__(self, nc, ctx, sched):
        self.nc = nc
        self.sched = sched
        self.ld = Sem(ctx, nc, "s_ld")
        self.st = Sem(ctx, nc, "s_st")
        self.mm = Sem(ctx, nc, "s_mm")
        self.dv = Sem(ctx, nc, "s_dv")
        self.ac = Sem(ctx, nc, "s_ac")
        self.po = Sem(ctx, nc, "s_po")
        self.ph = Sem(ctx, nc, "s_ph")
        self.phk = 0

    def _wrap(self, fn, waits, incs, drain=False):
        def g(eng):
            for (s, v) in waits:
                eng.wait_ge(s.h, v)
            if drain:
                eng.drain()
            inst = fn(eng)
            for (s, k) in incs:
                inst.then_inc(s.h, k)
        return g

    def tensor(self, fn, waits=()):
        v = self.mm.next()
        self.sched.on("tensor", self._wrap(fn, waits, [(self.mm, 1)]))
        return v

    def vector(self, fn, waits=(), ninc=1):
        v = self.dv.next(ninc)
        self.sched.on("vector", self._wrap(fn, waits, [(self.dv, ninc)], drain=True))
        return v

    def scalar(self, fn, waits=(), ninc=1):
        v = self.ac.next(ninc)
        self.sched.on("scalar", self._wrap(fn, waits, [(self.ac, ninc)], drain=True))
        return v

    def pool(self, fn, waits=(), ninc=1):
        v = self.po.next(ninc)
        self.sched.on("gpsimd", self._wrap(fn, waits, [(self.po, ninc)], drain=True))
        return v

    def gdma(self, dst, src, waits=()):
        v = self.ld.next(16)
        def fn(g):
            return g.dma_start(dst, src)
        self.sched.on("gpsimd", self._wrap(fn, waits, [(self.ld, 16)]))
        return v

    def sdma(self, dst, src, waits=()):
        v = self.st.next(16)
        def fn(s):
            return s.dma_start(dst, src)
        self.sched.on("sync", self._wrap(fn, waits, [(self.st, 16)]))
        return v

    def barrier(self):
        self.phk += 1
        tgt = 5 * self.phk
        self.ph.n = tgt
        for e in ENGINES:
            def fn(eng, tgt=tgt):
                eng.sem_inc(self.ph.h, 1)
                eng.wait_ge(self.ph.h, tgt)
            self.sched.on(e, fn)


def ts(i, n):
    return slice(i * n, (i + 1) * n)



def build_lA(debug=False):
    nc = bass.Bass("TRN2", target_bir_lowering=False)
    x0T = nc.dram_tensor("x0T", [DM, TKV], F32, kind="ExternalInput")
    wqT = nc.dram_tensor("wqT", [DM, DM], F32, kind="ExternalInput")
    wkT = nc.dram_tensor("wkT", [DM, 1024], F32, kind="ExternalInput")
    wvT = nc.dram_tensor("wvT", [DM, 1024], F32, kind="ExternalInput")
    woT = nc.dram_tensor("woT", [DM, DM], F32, kind="ExternalInput")
    rwT = nc.dram_tensor("rwT", [DM, 8], F32, kind="ExternalInput")
    rbrow = nc.dram_tensor("rbrow", [1, 8], F32, kind="ExternalInput")
    AallT = nc.dram_tensor("AallT", [DM, 128], F32, kind="ExternalInput")
    BmF = nc.dram_tensor("BmF", [128, DM], F32, kind="ExternalInput")
    i128b = nc.dram_tensor("i128b", [128, 128], BF16, kind="ExternalInput")
    onescol = nc.dram_tensor("onescol", [128, 1], BF16, kind="ExternalInput")
    onesrow = nc.dram_tensor("onesrow", [1, 512], BF16, kind="ExternalInput")
    negones = nc.dram_tensor("negones", [1, 128], F32, kind="ExternalInput")
    onesrow_f = nc.dram_tensor("onesrow_f", [1, 128], F32, kind="ExternalInput")
    maskSelf = nc.dram_tensor("maskSelf", [128, 128], BF16, kind="ExternalInput")
    maskFar = nc.dram_tensor("maskFar", [128, 128], BF16, kind="ExternalInput")
    fullmaskT = nc.dram_tensor("fullmaskT", [1, 13 * 128], BF16, kind="ExternalInput")
    maskHalo = nc.dram_tensor("maskHalo", [128, 15], F32, kind="ExternalInput")
    cntQ = nc.dram_tensor("cntQ", [1, 1024], F32, kind="ExternalInput")
    cntH = nc.dram_tensor("cntH", [1, 3], F32, kind="ExternalInput")
    fconsts = nc.dram_tensor("fconsts", [128, 4], F32, kind="ExternalInput")
    i8exp = nc.dram_tensor("i8exp", [8, 128], BF16, kind="ExternalInput")

    x1T = nc.dram_tensor("x1T", [DM, TQ], BF16, kind="ExternalOutput")

    xn0_d = nc.dram_tensor("xn0_d", [DM, TKV], BF16)
    qT_d = nc.dram_tensor("qT_d", [DM, TQ], BF16)
    moe_d = nc.dram_tensor("moe_d", [DM, 1152], BF16)

    sched = Sched()
    NT4 = [(0, 512), (512, 512), (1024, 512), (1536, 128)]
    NTQ = [(0, 512), (512, 512), (1024, 3)]

    with contextlib.ExitStack() as ctx:
        E = ctx.enter_context
        P = Prog(nc, ctx, sched)

        i128_sb = E(nc.sbuf_tensor("i128_sb", [128, 128], BF16))
        ones_c = E(nc.sbuf_tensor("ones_c", [128, 1], BF16))
        ones_r = E(nc.sbuf_tensor("ones_r", [1, 512], BF16))
        nones_r = E(nc.sbuf_tensor("nones_r", [1, 128], F32R))
        nones_f = E(nc.sbuf_tensor("nones_f", [1, 128], F32))
        ones_rf = E(nc.sbuf_tensor("ones_rf", [1, 128], F32R))
        mself_sb = E(nc.sbuf_tensor("mself_sb", [128, 128], BF16))
        mfar_sb = E(nc.sbuf_tensor("mfar_sb", [128, 128], BF16))
        fmask_sb = E(nc.sbuf_tensor("fmask_sb", [1, 13 * 128], BF16))
        mhalo_sb = E(nc.sbuf_tensor("mhalo_sb", [128, 15], F32))
        cntq_sb = E(nc.sbuf_tensor("cntq_sb", [1, 1024], F32))
        cnth_sb = E(nc.sbuf_tensor("cnth_sb", [1, 3], F32))
        qmax2 = E(nc.sbuf_tensor("qmax2", [1, 32], F32))
        kmax2 = E(nc.sbuf_tensor("kmax2", [1, 8], F32))
        negC_cols = E(nc.sbuf_tensor("negC_cols", [128, 32], F32))
        negC_row = E(nc.sbuf_tensor("negC_row", [1, 32], F32R))
        encrow = E(nc.sbuf_tensor("encrow", [1, 32], F32))
        fc_sb = E(nc.sbuf_tensor("fc_sb", [128, 4], F32))
        i8e_sb = E(nc.sbuf_tensor("i8e_sb", [8, 128], BF16))
        for i, val in enumerate(FCONSTS):
            nc.const_aps.register(F32, val, fc_sb[:, i:i + 1])

        for dst, src in [(i128_sb, i128b), (ones_c, onescol), (ones_r, onesrow),
                         (nones_r, negones), (ones_rf, onesrow_f),
                         (mself_sb, maskSelf), (mfar_sb, maskFar),
                         (fmask_sb, fullmaskT), (mhalo_sb, maskHalo),
                         (cntq_sb, cntQ), (cnth_sb, cntH), (fc_sb, fconsts),
                         (i8e_sb, i8exp), (nones_f, negones)]:
            P.gdma(dst[:], src[:, :])

        with contextlib.ExitStack() as p1:
            Ep = p1.enter_context
            x0slab = Ep(nc.sbuf_tensor("x0slab", [128, 2 * TKV], BF16))
            sqslab = Ep(nc.sbuf_tensor("sqslab", [128, 2 * TKV], BF16))
            rows = Ep(nc.sbuf_tensor("rows", [1, 4 * TKV], F32))
            rstd_r = Ep(nc.sbuf_tensor("rstd_r", [1, TKV], F32R))
            negmr_r = Ep(nc.sbuf_tensor("negmr_r", [1, TKV], F32R))
            rstd_b = Ep(nc.sbuf_tensor("rstd_b", [128, TKV], F32))
            negmr_b = Ep(nc.sbuf_tensor("negmr_b", [128, TKV], F32))

            mean_r = rows[:, 0:TKV]
            m2_r = rows[:, TKV:2 * TKV]
            var_r = rows[:, 2 * TKV:3 * TKV]
            lnv_r = rows[:, 3 * TKV:4 * TKV]

            with contextlib.ExitStack() as p1s:
                msum_p = p1s.enter_context(nc.psum_tensor("msum_p", [1, TKV], F32))
                sqsum_p = p1s.enter_context(nc.psum_tensor("sqsum_p", [1, TKV], F32))

                mm_hist = {}
                for kc in range(KC):
                    buf = x0slab[:, ts(kc % 2, TKV)]
                    sq = sqslab[:, ts(kc % 2, TKV)]
                    w = [(P.mm, mm_hist[kc - 2])] if kc >= 2 else []
                    v_ld = P.gdma(buf, x0T[ts(kc, 128), :], waits=w)
                    v_sq = P.scalar(lambda s, buf=buf, sq=sq: s.activation(sq, buf, AF.Square),
                                    waits=[(P.ld, v_ld)] + w)
                    def f(t, kc=kc, buf=buf, sq=sq):
                        for (o, n) in NT4:
                            t.matmul(msum_p[:, o:o + n], ones_c[:], buf[:, o:o + n],
                                     start=(kc == 0), stop=(kc == KC - 1))
                        inst = None
                        for (o, n) in NT4:
                            inst = t.matmul(sqsum_p[:, o:o + n], ones_c[:], sq[:, o:o + n],
                                            start=(kc == 0), stop=(kc == KC - 1))
                        return inst
                    mm_hist[kc] = P.tensor(f, waits=[(P.ld, v_ld), (P.ac, v_sq)])
                v_stats = P.mm.n

                def f(v):
                    v.tensor_scalar(out=mean_r, in0=msum_p[:], scalar1=1.0 / DM, scalar2=None, op0=ALU.mult)
                    v.drain()
                    v.tensor_tensor(m2_r, mean_r, mean_r, op=ALU.mult)
                    v.drain()
                    return v.scalar_tensor_tensor(out=var_r, in0=sqsum_p[:], scalar=1.0 / DM, in1=m2_r,
                                                  op0=ALU.mult, op1=ALU.subtract)
                v_rows = P.vector(f, waits=[(P.mm, v_stats)])
                def f(s):
                    s.activation(lnv_r, var_r, AF.Ln, bias=1e-5)
                    s.drain()
                    return s.activation(rstd_r[:], lnv_r, AF.Exp, scale=-0.5)
                v_rstd = P.scalar(f, waits=[(P.dv, v_rows)])
                v_negmr = P.vector(
                    lambda v: v.scalar_tensor_tensor(out=negmr_r[:], in0=mean_r, scalar=-1.0,
                                                     in1=rstd_r[:], op0=ALU.mult, op1=ALU.mult),
                    waits=[(P.ac, v_rstd)])

            with contextlib.ExitStack() as p1b:
                bc_p = p1b.enter_context(nc.psum_tensor("bc_p", [128, 512], F32))
                prev_bc = None
                for row, bcast in [(rstd_r, rstd_b), (negmr_r, negmr_b)]:
                    for (o, n) in NT4:
                        wb_ = [(P.dv, prev_bc)] if prev_bc is not None else []
                        v_b = P.tensor(
                            lambda t, row=row, o=o, n=n: t.matmul(
                                bc_p[:, 0:n], ones_rf[:, :], row[:, o:o + n].bitcast(F32R),
                                start=True, stop=True),
                            waits=[(P.dv, v_negmr)] + wb_)
                        prev_bc = P.vector(
                            lambda v, bcast=bcast, o=o, n=n: v.tensor_copy(bcast[:, o:o + n], bc_p[:, 0:n]),
                            waits=[(P.mm, v_b)])
                v_bc = P.dv.n

                dv_hist = {}
                st_hist = {}
                for kc in range(KC):
                    buf = x0slab[:, ts(kc % 2, TKV)]
                    sq = sqslab[:, ts(kc % 2, TKV)]
                    w = [(P.dv, dv_hist[kc - 2]), (P.st, st_hist[kc - 2])] if kc >= 2 else [(P.mm, v_stats)]
                    v_ld = P.gdma(buf, x0T[ts(kc, 128), :], waits=w)
                    def f(v, buf=buf, sq=sq):
                        v.tensor_tensor(sq, buf, rstd_b[:], op=ALU.mult)
                        v.drain()
                        return v.tensor_tensor(buf, sq, negmr_b[:], op=ALU.add)
                    dv_hist[kc] = P.vector(f, waits=[(P.ld, v_ld), (P.dv, v_bc)])
                    st_hist[kc] = P.sdma(xn0_d[ts(kc, 128), :], buf, waits=[(P.dv, dv_hist[kc])])

        P.barrier()

        kT_sb = E(nc.sbuf_tensor("kT_sb", [128, 8 * TKV], BF16))
        v_sb = E(nc.sbuf_tensor("v_sb", [128, 13 * 1024], BF16))

        with contextlib.ExitStack() as p2:
            Ep = p2.enter_context
            xn0_sb = Ep(nc.sbuf_tensor("xn0_sb", [128, KC * TKV], BF16))
            oslab = Ep(nc.sbuf_tensor("oslab", [128, TKV], BF16))
            sqs = Ep(nc.sbuf_tensor("sqs", [128, TKV], BF16))

            for kc in range(KC):
                P.sdma(xn0_sb[:, ts(kc, TKV)], xn0_d[ts(kc, 128), :])
            v_xn0 = P.st.n

            p2w = contextlib.ExitStack()
            wbuf = p2w.enter_context(nc.sbuf_tensor("wbuf", [128, 2 * KC * 128], BF16))
            out_p = p2w.enter_context(nc.psum_tensor("out_p", [128, TKV], F32))
            tp_p = p2w.enter_context(nc.psum_tensor("tp_p", [128, 128], BF16))
            kweight_hist = {}
            prev_red = None
            for hk in range(8):
                wb = wbuf[:, ts(hk % 2, KC * 128)]
                w = [(P.mm, kweight_hist[hk - 2])] if hk >= 2 else []
                v_w = P.gdma(wb.rearrange("p (c n) -> p c n", n=128),
                             wkT[:, ts(hk, 128)].rearrange("(c p) n -> p c n", p=128), waits=w)
                waits = [(P.ld, v_w), (P.st, v_xn0)]
                if prev_red is not None:
                    waits.append((P.dv, prev_red))
                def f(t, wb=wb):
                    inst = None
                    for kc in range(KC):
                        for (o, n) in NT4:
                            inst = t.matmul(out_p[:, o:o + n], wb[:, ts(kc, 128)],
                                            xn0_sb[:, kc * TKV + o: kc * TKV + o + n],
                                            start=(kc == 0), stop=(kc == KC - 1))
                    return inst
                kweight_hist[hk] = v_k = P.tensor(f, waits=waits)
                v_ev = P.vector(lambda v, hk=hk: v.tensor_copy(kT_sb[:, ts(hk, TKV)], out_p[:]),
                                waits=[(P.mm, v_k)])
                v_sq = P.scalar(lambda s, hk=hk: s.activation(sqs[:], kT_sb[:, ts(hk, TKV)], AF.Square),
                                waits=[(P.dv, v_ev)])
                def f(t):
                    inst = None
                    for (o, n) in NT4:
                        inst = t.matmul(out_p[0:1, o:o + n], ones_c[:], sqs[:, o:o + n],
                                        start=True, stop=True)
                    return inst
                v_sqm = P.tensor(f, waits=[(P.ac, v_sq), (P.dv, v_ev)])
                prev_red = P.vector(
                    lambda v, hk=hk: v.reduce_max(kmax2[:, hk:hk + 1], out_p[0:1, :], axis=AXX),
                    waits=[(P.mm, v_sqm)])

            vweight_hist = {}
            prev_tp = None
            for hv in range(8):
                wb = wbuf[:, ts(hv % 2, KC * 128)]
                w = [(P.mm, vweight_hist[hv - 2])] if hv >= 2 else [(P.mm, kweight_hist[7])]
                v_w = P.gdma(wb.rearrange("p (c n) -> p c n", n=128),
                             wvT[:, ts(hv, 128)].rearrange("(c p) n -> p c n", p=128), waits=w)
                waits = [(P.ld, v_w)]
                if prev_red is not None:
                    waits.append((P.dv, prev_red))
                    prev_red = None
                def f(t, wb=wb):
                    inst = None
                    for kc in range(KC):
                        for (o, n) in NT4:
                            inst = t.matmul(out_p[:, o:o + n], wb[:, ts(kc, 128)],
                                            xn0_sb[:, kc * TKV + o: kc * TKV + o + n],
                                            start=(kc == 0), stop=(kc == KC - 1))
                    return inst
                vweight_hist[hv] = v_m = P.tensor(f, waits=waits)
                v_ev = P.vector(lambda v: v.tensor_copy(oslab[:], out_p[:]), waits=[(P.mm, v_m)])
                for tc in range(13):
                    wtp = [(P.dv, prev_tp)] if prev_tp is not None else []
                    v_t = P.tensor(
                        lambda t, tc=tc: t.transpose(tp_p[:], oslab[:, ts(tc, 128)], i128_sb[:]),
                        waits=[(P.dv, v_ev)] + wtp)
                    prev_tp = P.vector(
                        lambda v, hv=hv, tc=tc: v.tensor_copy(
                            v_sb[:, tc * 1024 + hv * 128: tc * 1024 + (hv + 1) * 128], tp_p[:]),
                        waits=[(P.mm, v_t)])

            qweight_hist = {}
            prev_evq = None
            for hq in range(KC):
                wb = wbuf[:, ts(hq % 2, KC * 128)]
                w = [(P.mm, qweight_hist[hq - 2])] if hq >= 2 else [(P.mm, vweight_hist[7])]
                v_w = P.gdma(wb.rearrange("p (c n) -> p c n", n=128),
                             wqT[:, ts(hq, 128)].rearrange("(c p) n -> p c n", p=128), waits=w)
                waits = [(P.ld, v_w)]
                if prev_tp is not None:
                    waits.append((P.dv, prev_tp))
                    prev_tp = None
                if prev_evq is not None:
                    waits.append((P.dv, prev_evq))
                def f(t, wb=wb):
                    inst = None
                    for kc in range(KC):
                        for (o, n) in NTQ:
                            inst = t.matmul(out_p[:, o:o + n], wb[:, ts(kc, 128)],
                                            xn0_sb[:, kc * TKV + 637 + o: kc * TKV + 637 + o + n],
                                            start=(kc == 0), stop=(kc == KC - 1))
                    return inst
                qweight_hist[hq] = v_m = P.tensor(f, waits=waits)
                v_ev = P.vector(lambda v: v.tensor_scalar(out=oslab[:, 0:TQ], in0=out_p[:, 0:TQ],
                                                          scalar1=1.0 / math.sqrt(128.0),
                                                          scalar2=None, op0=ALU.mult),
                                waits=[(P.mm, v_m)])
                P.sdma(qT_d[ts(hq, 128), :], oslab[:, 0:TQ], waits=[(P.dv, v_ev)])
                v_sq = P.scalar(lambda s: s.activation(sqs[:, 0:TQ], oslab[:, 0:TQ], AF.Square),
                                waits=[(P.dv, v_ev)])
                def f(t):
                    inst = None
                    for (o, n) in NTQ:
                        inst = t.matmul(out_p[0:1, o:o + n], ones_c[:], sqs[:, o:o + n],
                                        start=True, stop=True)
                    return inst
                v_sqm = P.tensor(f, waits=[(P.ac, v_sq)])
                prev_evq = P.vector(
                    lambda v, hq=hq: v.reduce_max(qmax2[:, hq:hq + 1], out_p[0:1, 0:TQ], axis=AXX),
                    waits=[(P.mm, v_sqm)])

            p2w.close()
            P.barrier()

            with contextlib.ExitStack() as p3:
                E3 = p3.enter_context
                rw_sb = E3(nc.sbuf_tensor("rw_sb", [128, KC * 8], BF16))
                rb_sb = E3(nc.sbuf_tensor("rb_sb", [1, 8], BF16))
                aall_sb = E3(nc.sbuf_tensor("aall_sb", [128, KC * 128], BF16))
                bmf_sb = E3(nc.sbuf_tensor("bmf_sb", [128, DM], BF16))
                h_sb = E3(nc.sbuf_tensor("h_sb", [128, 1152], BF16))
                hw_sb = E3(nc.sbuf_tensor("hw_sb", [128, 1152], BF16))
                wtT_sb = E3(nc.sbuf_tensor("wtT_sb", [8, 128], BF16))
                w_sb = E3(nc.sbuf_tensor("w_sb", [128, 8], BF16))
                e8 = E3(nc.sbuf_tensor("e8", [128, 8], F32))
                s8 = E3(nc.sbuf_tensor("s8", [128, 8], BF16))
                sc1 = E3(nc.sbuf_tensor("sc1", [128, 8], F32))
                logit_p = E3(nc.psum_tensor("logit_p", [128, 8], F32))
                wdbg_sb = E3(nc.sbuf_tensor("wdbg_sb", [128, 72], BF16))
                wt_p = E3(nc.psum_tensor("wt_p", [8, 128], BF16))
                wexp_p = E3(nc.psum_tensor("wexp_p", [128, 128], F32))
                h_p = E3(nc.psum_tensor("h_p", [128, 128], F32))
                moe_p = E3(nc.psum_tensor("moe_p", [128, 1152], F32))

                P.gdma(rw_sb[:].rearrange("p (c n) -> p c n", n=8),
                       rwT[:, :].rearrange("(c p) n -> p c n", p=128))
                P.gdma(rb_sb[:], rbrow[:, :])
                P.gdma(aall_sb[:].rearrange("p (c n) -> p c n", n=128),
                       AallT[:, :].rearrange("(c p) n -> p c n", p=128))
                P.gdma(bmf_sb[:], BmF[:, :])
                v_wmoe = P.ld.n

                m1c, nm1c, m2c, d21c, e21c, s2c, recc = [sc1[:, i:i + 1] for i in range(7)]
                prev_lg = None
                prev_wexp = None
                prev_h = None
                for tt in range(9):
                    col0 = 512 + tt * 128
                    waits = [(P.ld, v_wmoe), (P.st, v_xn0)]
                    if prev_lg is not None:
                        waits.append((P.dv, prev_lg))
                    def f(t, col0=col0):
                        for kc in range(KC):
                            t.matmul(logit_p[:], xn0_sb[:, kc * TKV + col0: kc * TKV + col0 + 128],
                                     rw_sb[:, ts(kc, 8)], start=(kc == 0), stop=False)
                        return t.matmul(logit_p[:], ones_r[:, 0:128], rb_sb[:],
                                        start=False, stop=True)
                    v_lg = P.tensor(f, waits=waits)
                    def f(v):
                        v.reduce_max(m1c, logit_p[:], axis=AXX)
                        v.reduce_max(nm1c, logit_p[:], axis=AXX, negate=True)
                        v.drain()
                        v.tensor_scalar(out=s8[:], in0=logit_p[:], scalar1=m1c, scalar2=None,
                                        op0=ALU.is_equal)
                        v.drain()
                        v.scalar_tensor_tensor(out=e8[:], in0=s8[:], scalar=NEG, in1=logit_p[:],
                                               op0=ALU.mult, op1=ALU.add)
                        v.drain()
                        v.reduce_max(m2c, e8[:], axis=AXX)
                        v.drain()
                        return v.tensor_tensor(d21c, m2c, nm1c, op=ALU.add)
                    v_tw = P.vector(f, waits=[(P.mm, v_lg)])
                    v_e21 = P.scalar(lambda s: s.activation(e21c, d21c, AF.Exp),
                                     waits=[(P.dv, v_tw)])
                    def f(v):
                        v.tensor_scalar(out=s2c, in0=e21c, scalar1=1.0, scalar2=2.0,
                                        op0=ALU.add, op1=ALU.mult)
                        v.drain()
                        v.reciprocal(recc, s2c)
                        v.drain()
                        return v.tensor_scalar(out=s8[:], in0=logit_p[:], scalar1=m2c, scalar2=None,
                                               op0=ALU.is_ge)
                    v_sel = P.vector(f, waits=[(P.ac, v_e21)])
                    v_exp8 = P.scalar(lambda s: s.activation(e8[:], logit_p[:], AF.Exp, bias=nm1c),
                                      waits=[(P.dv, v_sel)])
                    def f(v, tt=tt):
                        v.scalar_tensor_tensor(out=w_sb[:], in0=e8[:], scalar=recc,
                                               in1=s8[:], op0=ALU.mult, op1=ALU.mult)
                        v.drain()
                        return v.tensor_copy(wdbg_sb[:, ts(tt, 8)], w_sb[:])
                    prev_lg = v_wfin = P.vector(f, waits=[(P.ac, v_exp8)])
                    wtw = [(P.dv, prev_wexp)] if prev_wexp is not None else []
                    v_wt = P.tensor(lambda t: t.transpose(wt_p[:], w_sb[:], i128_sb[:]),
                                    waits=[(P.dv, v_wfin)] + wtw)
                    v_wtev = P.vector(lambda v: v.tensor_copy(wtT_sb[:], wt_p[:]),
                                      waits=[(P.mm, v_wt)])
                    v_we = P.tensor(
                        lambda t: t.matmul(wexp_p[:], i8e_sb[:], wtT_sb[:],
                                           start=True, stop=True),
                        waits=[(P.dv, v_wtev)] + wtw)
                    wh = [(P.dv, prev_h)] if prev_h is not None else []
                    def f(t, col0=col0):
                        inst = None
                        for kc in range(KC):
                            inst = t.matmul(h_p[:], aall_sb[:, ts(kc, 128)],
                                            xn0_sb[:, kc * TKV + col0: kc * TKV + col0 + 128],
                                            start=(kc == 0), stop=(kc == KC - 1))
                        return inst
                    v_h = P.tensor(f, waits=[(P.ld, v_wmoe)] + wh)
                    def f(v, tt=tt):
                        v.tensor_copy(h_sb[:, ts(tt, 128)], h_p[:])
                        v.drain()
                        return v.tensor_tensor(hw_sb[:, ts(tt, 128)], h_sb[:, ts(tt, 128)],
                                               wexp_p[:], op=ALU.mult)
                    prev_h = prev_wexp = P.vector(f, waits=[(P.mm, max(v_h, v_we))])
                v_hwall = P.dv.n

                NT9 = [(0, 512), (512, 512), (1024, 128)]
                prev_ev = None
                if debug:
                    wdbg_o = nc.dram_tensor("wdbg_o", [128, 72], BF16, kind="ExternalOutput")
                    hdbg_o = nc.dram_tensor("hdbg_o", [128, 1152], BF16, kind="ExternalOutput")
                    hwdbg_o = nc.dram_tensor("hwdbg_o", [128, 1152], BF16, kind="ExternalOutput")
                    P.sdma(wdbg_o[:, :], wdbg_sb[:], waits=[(P.dv, v_hwall)])
                    P.sdma(hdbg_o[:, :], h_sb[:], waits=[(P.dv, v_hwall)])
                    P.sdma(hwdbg_o[:, :], hw_sb[:], waits=[(P.dv, v_hwall)])
                for dc in range(KC):
                    waits = [(P.dv, v_hwall)]
                    if prev_ev is not None:
                        waits.append((P.dv, prev_ev))
                    def f(t, dc=dc):
                        inst = None
                        for (o, n) in NT9:
                            inst = t.matmul(moe_p[:, o:o + n], bmf_sb[:, ts(dc, 128)],
                                            hw_sb[:, o:o + n], start=True, stop=True)
                        return inst
                    v_m = P.tensor(f, waits=waits)
                    prev_ev = P.vector(lambda v: v.tensor_copy(oslab[:, 0:1152], moe_p[:]),
                                       waits=[(P.mm, v_m)])
                    P.sdma(moe_d[ts(dc, 128), :], oslab[:, 0:1152], waits=[(P.dv, prev_ev)])

            P.barrier()

        o_sb = E(nc.sbuf_tensor("o_sb", [128, KC * TQ], BF16))

        with contextlib.ExitStack() as p4:
            E4 = p4.enter_context
            qbuf = E4(nc.sbuf_tensor("qbuf", [128, 2 * 512], BF16))
            qhalo = E4(nc.sbuf_tensor("qhalo", [128, KC * 3], BF16))
            pT_sb = E4(nc.sbuf_tensor("pT_sb", [128, 2 * 4096], BF16))
            ph_sb = E4(nc.sbuf_tensor("ph_sb", [128, 15], BF16))
            urow = E4(nc.sbuf_tensor("urow", [1, 512], F32))
            rrow = E4(nc.sbuf_tensor("rrow", [1, 512], F32R))
            rrow3 = E4(nc.sbuf_tensor("rrow3", [1, 4], F32))
            recb_sb = E4(nc.sbuf_tensor("recb_sb", [128, 512], F32))
            s_pA = E4(nc.psum_tensor("s_pA", [128, 1024], F32))
            s_pB = E4(nc.psum_tensor("s_pB", [128, 1024], F32))
            den_p = E4(nc.psum_tensor("den_p", [1, 512], F32))
            o_p = E4(nc.psum_tensor("o_p", [128, 512], F32))
            rb_p = E4(nc.psum_tensor("rb_p", [128, 512], F32))

            v_c0 = P.vector(
                lambda v: v.tensor_tensor(qmax2[:], qmax2[:],
                                          kmax2[:].unsqueeze(2).broadcast_to([1, 8, 4]),
                                          op=ALU.mult))
            v_c1 = P.scalar(lambda s: s.activation(encrow[:], qmax2[:], AF.Sqrt),
                            waits=[(P.dv, v_c0)])
            v_c2 = P.vector(
                lambda v: v.tensor_scalar(out=negC_row[:], in0=encrow[:],
                                          scalar1=-1.0 / math.sqrt(128.0), scalar2=None,
                                          op0=ALU.mult),
                waits=[(P.ac, v_c1)])
            v_c3 = P.tensor(
                lambda t: t.matmul(rb_p[:, 0:32], ones_rf[:, :], negC_row[:],
                                   start=True, stop=True),
                waits=[(P.dv, v_c2)])
            v_c4 = P.vector(lambda v: v.tensor_copy(negC_cols[:], rb_p[:, 0:32]),
                            waits=[(P.mm, v_c3)])
            v_c5 = P.scalar(lambda s: s.activation(encrow[:], negC_row[:], AF.Exp),
                            waits=[(P.dv, v_c2)])

            P.sdma(qhalo[:].rearrange("p (c n) -> p c n", n=3),
                   qT_d[:, 0:3].rearrange("(c p) n -> p c n", p=128))
            v_qh = P.st.n

            qpv = {}
            prevA = prevB = None
            prev_den = prev_o = prev_rb = None
            for h in range(32):
                hk = h // 4
                ncol = negC_cols[:, h:h + 1]
                waits = [(P.st, v_qh), (P.dv, v_c4)]
                if prevB is not None:
                    waits.append((P.ac, prevB))
                def f(t, hk=hk, h=h):
                    inst = None
                    for kb in range(5):
                        inst = t.matmul(s_pB[:, kb * 3:(kb + 1) * 3],
                                        kT_sb[:, hk * TKV + kb * 128: hk * TKV + (kb + 1) * 128],
                                        qhalo[:, ts(h, 3)], start=True, stop=True)
                    return inst
                v_hm = P.tensor(f, waits=waits)
                v_hmm = P.vector(
                    lambda v: v.tensor_tensor(s_pB[:, 0:15], s_pB[:, 0:15], mhalo_sb[:], op=ALU.add),
                    waits=[(P.mm, v_hm)])
                prevB = v_hx = P.scalar(
                    lambda s, ncol=ncol: s.activation(ph_sb[:], s_pB[:, 0:15], AF.Exp, bias=ncol),
                    waits=[(P.dv, v_hmm), (P.ac, v_c5)])
                wd = [(P.dv, prev_den)] if prev_den is not None else []
                def f(t):
                    inst = None
                    for kb in range(5):
                        inst = t.matmul(den_p[:, 0:3], ones_c[:], ph_sb[:, kb * 3:(kb + 1) * 3],
                                        start=(kb == 0), stop=(kb == 4))
                    return inst
                v_hd = P.tensor(f, waits=[(P.ac, v_hx)] + wd)
                def f(v, h=h):
                    v.scalar_tensor_tensor(out=urow[:, 0:3], in0=cnth_sb[:], scalar=encrow[:, h:h + 1],
                                           in1=den_p[:, 0:3], op0=ALU.mult, op1=ALU.subtract)
                    v.drain()
                    v.tensor_scalar(out=urow[:, 0:3], in0=urow[:, 0:3], scalar1=-1e-30, scalar2=None,
                                    op0=ALU.min)
                    v.drain()
                    return v.reciprocal(rrow3[:, 0:3], urow[:, 0:3])
                prev_den = v_hu = P.vector(f, waits=[(P.mm, v_hd)])
                wrb = [(P.dv, prev_rb)] if prev_rb is not None else [(P.dv, v_c4)]
                v_hrb = P.tensor(
                    lambda t: t.matmul(rb_p[:, 0:3], nones_f[:, :], rrow3[:, 0:3], start=True, stop=True),
                    waits=[(P.dv, v_hu)] + wrb)
                prev_rb = v_hrbe = P.vector(lambda v: v.tensor_copy(recb_sb[:, 0:3], rb_p[:, 0:3]),
                                            waits=[(P.mm, v_hrb)])
                wo_ = [(P.dv, prev_o)] if prev_o is not None else []
                def f(t, hk=hk):
                    inst = None
                    for kb in range(5):
                        inst = t.matmul(o_p[:, 0:3],
                                        v_sb[:, kb * 1024 + hk * 128: kb * 1024 + (hk + 1) * 128],
                                        ph_sb[:, kb * 3:(kb + 1) * 3], start=(kb == 0), stop=(kb == 4))
                    return inst
                v_ho = P.tensor(f, waits=[(P.ac, v_hx)] + wo_)
                prev_o = P.vector(
                    lambda v, h=h: v.tensor_tensor(o_sb[:, h * TQ: h * TQ + 3], o_p[:, 0:3],
                                                   recb_sb[:, 0:3], op=ALU.mult),
                    waits=[(P.mm, v_ho), (P.dv, v_hrbe)])

                for qt in range(2):
                    qb_ = qbuf[:, ts(qt, 512)]
                    wq_ = [(P.mm, qpv[(h - 1, qt)])] if (h - 1, qt) in qpv else []
                    v_q = P.sdma(qb_, qT_d[ts(h, 128), 3 + qt * 512: 3 + qt * 512 + 512], waits=wq_)
                    kbs = list(range(4 * qt + 1, 4 * qt + 9))
                    spans = []
                    for kb in kbs:
                        lo = max(kb - 5, 4 * qt)
                        hi = min(kb - 1, 4 * qt + 3)
                        spans.append((kb, lo * 128 - qt * 512, (hi - lo + 1) * 128))
                    pslab = pT_sb[:, ts(qt, 4096)]
                    exps = []
                    for half in range(4):
                        sp = s_pA if half % 2 == 0 else s_pB
                        prevexp = prevA if half % 2 == 0 else prevB
                        wx = [(P.ac, prevexp)] if prevexp is not None else []
                        sub = spans[half * 2:(half + 1) * 2]
                        def f(t, sub=sub, sp=sp, qt=qt, hk=hk, qb_=qb_):
                            inst = None
                            for j, (kb, qo, wdt) in enumerate(sub):
                                mms = [("s", qo, wdt)]
                                qbself = kb - 5
                                if 4 * qt <= qbself <= 4 * qt + 3:
                                    mms.append(("self", (qbself - 4 * qt) * 128, 128))
                                qbfar = kb - 1
                                if 4 * qt <= qbfar <= 4 * qt + 3:
                                    mms.append(("far", (qbfar - 4 * qt) * 128, 128))
                                if kb <= 4:
                                    mms.append(("full", qo, wdt))
                                for idx, (kind, o_, n_) in enumerate(mms):
                                    st_ = (idx == 0)
                                    sp_ = (idx == len(mms) - 1)
                                    dst = sp[:, j * 512 + o_: j * 512 + o_ + n_]
                                    if kind == "s":
                                        inst = t.matmul(dst, kT_sb[:, hk * TKV + kb * 128: hk * TKV + (kb + 1) * 128],
                                                        qb_[:, o_:o_ + n_], start=st_, stop=sp_)
                                    elif kind == "self":
                                        inst = t.matmul(dst, i128_sb[:], mself_sb[:], start=st_, stop=sp_)
                                    elif kind == "far":
                                        inst = t.matmul(dst, i128_sb[:], mfar_sb[:], start=st_, stop=sp_)
                                    else:
                                        inst = t.matmul(dst, fmask_sb[:, ts(kb, 128)],
                                                        ones_r[:, 0:n_], start=st_, stop=sp_)
                            return inst
                        wpv = [(P.mm, qpv[(h - 1, qt)])] if (h - 1, qt) in qpv else []
                        v_sm = P.tensor(f, waits=[(P.st, v_q)] + wx + wpv)
                        v_x = P.scalar(
                            lambda s, half=half, sp=sp, ncol=ncol, pslab=pslab:
                                s.activation(pslab[:, half * 1024:(half + 1) * 1024], sp[:],
                                             AF.Exp, bias=ncol),
                            waits=[(P.mm, v_sm), (P.ac, v_c5), (P.dv, v_c4)] + wpv)
                        exps.append(v_x)
                        if half % 2 == 0:
                            prevA = v_x
                        else:
                            prevB = v_x
                    wd = [(P.dv, prev_den)] if prev_den is not None else []
                    def f(t, spans=spans, pslab=pslab):
                        inst = None
                        for j, (kb, qo, wdt) in enumerate(spans):
                            inst = t.matmul(den_p[:, qo:qo + wdt], ones_c[:],
                                            pslab[:, j * 512 + qo: j * 512 + qo + wdt],
                                            start=(j == 0), stop=(j == 7))
                        return inst
                    v_dm = P.tensor(f, waits=[(P.ac, exps[-1])] + wd)
                    def f(v, qt=qt, h=h):
                        v.scalar_tensor_tensor(out=urow[:], in0=cntq_sb[:, ts(qt, 512)],
                                               scalar=encrow[:, h:h + 1], in1=den_p[:],
                                               op0=ALU.mult, op1=ALU.subtract)
                        v.drain()
                        v.tensor_scalar(out=urow[:], in0=urow[:], scalar1=-1e-30, scalar2=None,
                                        op0=ALU.min)
                        v.drain()
                        return v.reciprocal(rrow[:], urow[:])
                    prev_den = v_u = P.vector(f, waits=[(P.mm, v_dm)])
                    wrb = [(P.dv, prev_rb)] if prev_rb is not None else []
                    v_rbm = P.tensor(
                        lambda t: t.matmul(rb_p[:], nones_r[:, :], rrow[:], start=True, stop=True),
                        waits=[(P.dv, v_u)] + wrb)
                    prev_rb = v_rbe = P.vector(lambda v: v.tensor_copy(recb_sb[:], rb_p[:]),
                                               waits=[(P.mm, v_rbm)])
                    wo_ = [(P.dv, prev_o)] if prev_o is not None else []
                    def f(t, spans=spans, pslab=pslab, hk=hk):
                        inst = None
                        for j, (kb, qo, wdt) in enumerate(spans):
                            inst = t.matmul(o_p[:, qo:qo + wdt],
                                            v_sb[:, kb * 1024 + hk * 128: kb * 1024 + (hk + 1) * 128],
                                            pslab[:, j * 512 + qo: j * 512 + qo + wdt],
                                            start=(j == 0), stop=(j == 7))
                        return inst
                    v_pv = P.tensor(f, waits=[(P.ac, exps[-1])] + wo_)
                    qpv[(h, qt)] = v_pv
                    prev_o = P.vector(
                        lambda v, h=h, qt=qt: v.tensor_tensor(
                            o_sb[:, h * TQ + 3 + qt * 512: h * TQ + 3 + qt * 512 + 512],
                            o_p[:], recb_sb[:], op=ALU.mult),
                        waits=[(P.mm, v_pv), (P.dv, v_rbe)])

        if debug:
            o_o = nc.dram_tensor("o_o", [128, KC * TQ], BF16, kind="ExternalOutput")
            P.sdma(o_o[:, :], o_sb[:])
        P.barrier()

        with contextlib.ExitStack() as p5:
            E5 = p5.enter_context
            wobuf = E5(nc.sbuf_tensor("wobuf", [128, 2 * KC * 128], BF16))
            x0re = E5(nc.sbuf_tensor("x0re", [128, 2 * TQ], BF16))
            moesl = E5(nc.sbuf_tensor("moesl", [128, 2 * TQ], BF16))
            x1sl = E5(nc.sbuf_tensor("x1sl", [128, 2 * TQ], BF16))
            wo_p = E5(nc.psum_tensor("wo_p", [128, TQ], F32))

            hist = {}
            for dc in range(KC):
                wb = wobuf[:, ts(dc % 2, KC * 128)]
                x0b = x0re[:, ts(dc % 2, TQ)]
                moeb = moesl[:, ts(dc % 2, TQ)]
                x1b = x1sl[:, ts(dc % 2, TQ)]
                wv_ = [(P.mm, hist[dc - 2])] if dc >= 2 else []
                wv2 = [(P.dv, hist[('e', dc - 2)])] if dc >= 2 else []
                wv3 = [(P.st, hist[('s', dc - 2)])] if dc >= 2 else []
                v_w = P.gdma(wb.rearrange("p (c n) -> p c n", n=128),
                             woT[:, ts(dc, 128)].rearrange("(c p) n -> p c n", p=128), waits=wv_)
                v_x0 = P.gdma(x0b, x0T[ts(dc, 128), 637:1664], waits=wv2)
                v_moe = P.sdma(moeb, moe_d[ts(dc, 128), 125:1152], waits=wv2)
                wprev = [(P.dv, hist[('e', dc - 1)])] if dc >= 1 else []
                def f(t, wb=wb):
                    inst = None
                    for hh in range(KC):
                        for (o, n) in NTQ:
                            inst = t.matmul(wo_p[:, o:o + n], wb[:, ts(hh, 128)],
                                            o_sb[:, hh * TQ + o: hh * TQ + o + n],
                                            start=(hh == 0), stop=(hh == KC - 1))
                    return inst
                hist[dc] = v_m = P.tensor(f, waits=[(P.ld, v_w)] + wprev)
                def f(v, x0b=x0b, moeb=moeb, x1b=x1b):
                    v.tensor_tensor(x1b, wo_p[:], x0b, op=ALU.add)
                    v.drain()
                    return v.tensor_tensor(x1b, x1b, moeb, op=ALU.add)
                hist[('e', dc)] = v_e = P.vector(
                    f, waits=[(P.mm, v_m), (P.ld, v_x0), (P.st, v_moe)] + wv3)
                hist[('s', dc)] = P.sdma(x1T[ts(dc, 128), :], x1b, waits=[(P.dv, v_e)])

            if debug:
                xn0_o = nc.dram_tensor("xn0_o", [DM, TKV], BF16, kind="ExternalOutput")
                qT_o = nc.dram_tensor("qT_o", [DM, TQ], BF16, kind="ExternalOutput")
                moe_o = nc.dram_tensor("moe_o", [DM, 1152], BF16, kind="ExternalOutput")
                P.sdma(xn0_o[:, :], xn0_d[:, :])
                P.sdma(qT_o[:, :], qT_d[:, :])
                P.sdma(moe_o[:, :], moe_d[:, :])
            vfin = P.st.n
            sched.on("sync", lambda s, vfin=vfin: s.wait_ge(P.st.h, vfin))

        with nc.allow_low_precision("bf16 operands / f32r broadcasts by design"):
            sched.emit(nc)
    return nc



def _l0_host_consts():
    i = np.arange(128)[:, None]
    j = np.arange(128)[None, :]
    mself = np.where(i <= j, 0.0, NEG).astype(np.float32)
    mfar = np.where(i > j, 0.0, NEG).astype(np.float32)
    mhalo = np.zeros((128, 5, 3), np.float32)
    for kb in range(5):
        for kj in range(128):
            g = -640 + kb * 128 + kj
            for ii in range(3):
                p_ = ii - 3
                ok = (g <= p_) and (p_ - g < 512)
                mhalo[kj, kb, ii] = 0.0 if ok else NEG
    return {
        "i128b": np.eye(128, dtype=BF),
        "onescol": np.ones((128, 1), BF),
        "onesrow": np.ones((1, 512), BF),
        "negones": -np.ones((1, 128), np.float32),
        "onesrow_f": np.ones((1, 128), np.float32),
        "maskSelf": mself.astype(BF),
        "maskFar": mfar.astype(BF),
        "maskHalo": mhalo.reshape(128, 15),
        "fconsts": np.tile(np.array([FCONSTS], np.float32), (128, 1)),
        "i8exp": np.repeat(np.eye(8, dtype=BF), 16, axis=1),
    }


def _l0_percore(core, x):
    b, cc = core // 4, core % 4
    Q0 = cc * 1024
    lo = Q0 - 640
    pad = max(0, -lo)
    sl = x[b, max(0, lo):Q0 + 1024]
    x0s = np.concatenate([np.zeros((pad, DM), np.float32), sl], 0)
    fm = np.zeros((13, 128), np.float32)
    cntq = np.zeros(1024, np.float32)
    cnth = np.zeros(3, np.float32)
    if cc == 0:
        fm[:5] = NEG
        for ii in range(3):
            g = np.arange(-640, 0)
            p_ = ii - 3
            valid = (g <= p_) & (p_ - g < 512)
            cnth[ii] = ((g < 0) & valid).sum()
    return {
        "x0T": np.ascontiguousarray(x0s.T),
        "fullmaskT": np.repeat(fm, 1, 0).reshape(1, 13 * 128).astype(BF),
        "cntQ": cntq.reshape(1, 1024),
        "cntH": cnth.reshape(1, 3),
    }


def host_inputs_lA(inp):
    consts = _l0_host_consts()
    shared = {
        "wqT": np.ascontiguousarray(inp["L0_attn_wq"].T),
        "wkT": np.ascontiguousarray(inp["L0_attn_wk"].T),
        "wvT": np.ascontiguousarray(inp["L0_attn_wv"].T),
        "woT": np.ascontiguousarray(inp["L0_attn_wo"].T),
        "rwT": np.ascontiguousarray(inp["L0_ffn_router_w"].T),
        "rbrow": inp["L0_ffn_router_b"].reshape(1, 8),
        "AallT": np.ascontiguousarray(inp["L0_ffn_A"].reshape(128, DM).T),
        "BmF": np.ascontiguousarray(inp["L0_ffn_Bm"].transpose(0, 2, 1).reshape(128, DM)),
    }
    maps = []
    for core in range(8):
        m = dict(consts)
        m.update(shared)
        m.update(_l0_percore(core, inp["L0_attn_x"]))
        maps.append(m)
    return maps




def build_lB():
    nc = bass.Bass("TRN2", target_bir_lowering=False)
    x1T = nc.dram_tensor("x1T", [DM, TQ], BF16, kind="ExternalInput")
    wipT = nc.dram_tensor("wipT", [DM, 18560], F32, kind="ExternalInput")
    convW = nc.dram_tensor("convW", [128, 80 * 4], F32, kind="ExternalInput")
    convB = nc.dram_tensor("convB", [128, 80], F32, kind="ExternalInput")
    dtBias = nc.dram_tensor("dtBias", [128, 1], F32, kind="ExternalInput")
    Alog = nc.dram_tensor("Alog", [128, 1], F32, kind="ExternalInput")
    onescol = nc.dram_tensor("onescol", [128, 1], BF16, kind="ExternalInput")
    onesrow_f = nc.dram_tensor("onesrow_f", [1, 128], F32, kind="ExternalInput")
    fconsts = nc.dram_tensor("fconsts", [128, 4], F32, kind="ExternalInput")

    xn1T = nc.dram_tensor("xn1T", [DM, TQ], BF16, kind="ExternalOutput")
    gateT = nc.dram_tensor("gateT", [8192, 1024], BF16, kind="ExternalOutput")
    xT = nc.dram_tensor("xT", [8192, 1024], BF16, kind="ExternalOutput")
    BT = nc.dram_tensor("BT", [1024, 1024], BF16, kind="ExternalOutput")
    CT = nc.dram_tensor("CT", [1024, 1024], BF16, kind="ExternalOutput")
    dtF = nc.dram_tensor("dtF", [128, 1024], F32, kind="ExternalOutput")
    dtB = nc.dram_tensor("dtB", [128, 1024], BF16, kind="ExternalOutput")
    AdtF = nc.dram_tensor("AdtF", [128, 1024], F32, kind="ExternalOutput")
    cumrelF = nc.dram_tensor("cumrelF", [128, 1024], F32, kind="ExternalOutput")

    sched = Sched()
    NTQ = [(0, 512), (512, 512), (1024, 3)]

    with contextlib.ExitStack() as ctx:
        E = ctx.enter_context
        P = Prog(nc, ctx, sched)

        ones_c = E(nc.sbuf_tensor("ones_c", [128, 1], BF16))
        ones_rf = E(nc.sbuf_tensor("ones_rf", [1, 128], F32R))
        ones_ff = E(nc.sbuf_tensor("ones_ff", [1, 128], F32))
        cw_sb = E(nc.sbuf_tensor("cw_sb", [128, 80 * 4], F32))
        cb_sb = E(nc.sbuf_tensor("cb_sb", [128, 80], F32))
        dtb_sb = E(nc.sbuf_tensor("dtb_sb", [128, 1], F32))
        alog_sb = E(nc.sbuf_tensor("alog_sb", [128, 1], F32))
        nega_sb = E(nc.sbuf_tensor("nega_sb", [128, 1], F32))
        fc_sb = E(nc.sbuf_tensor("fc_sb", [128, 4], F32))
        for i, val in enumerate(FCONSTS):
            nc.const_aps.register(F32, val, fc_sb[:, i:i + 1])
        for dst, srcd in [(ones_c, onescol), (ones_rf, onesrow_f), (ones_ff, onesrow_f),
                          (cw_sb, convW),
                          (cb_sb, convB), (dtb_sb, dtBias), (alog_sb, Alog),
                          (fc_sb, fconsts)]:
            P.gdma(dst[:], srcd[:, :])
        v_const = P.ld.n
        v_na0 = P.scalar(lambda s: s.activation(nega_sb[:], alog_sb[:], AF.Exp),
                         waits=[(P.ld, v_const)])
        v_na = P.vector(
            lambda v: v.tensor_scalar(out=nega_sb[:], in0=nega_sb[:], scalar1=-1.0,
                                      scalar2=None, op0=ALU.mult),
            waits=[(P.ac, v_na0)])

        xn1_sb = E(nc.sbuf_tensor("xn1_sb", [128, KC * TQ], BF16))

        with contextlib.ExitStack() as b1:
            Ep = b1.enter_context
            sqslab = Ep(nc.sbuf_tensor("sqslab", [128, 2 * TQ], BF16))
            rows = Ep(nc.sbuf_tensor("rows", [1, 4 * TQ], F32))
            rstd_r = Ep(nc.sbuf_tensor("rstd_r", [1, TQ], F32R))
            negmr_r = Ep(nc.sbuf_tensor("negmr_r", [1, TQ], F32R))
            rstd_b = Ep(nc.sbuf_tensor("rstd_b", [128, TQ], F32))
            negmr_b = Ep(nc.sbuf_tensor("negmr_b", [128, TQ], F32))

            mean_r = rows[:, 0:TQ]
            m2_r = rows[:, TQ:2 * TQ]
            var_r = rows[:, 2 * TQ:3 * TQ]
            lnv_r = rows[:, 3 * TQ:4 * TQ]

            with contextlib.ExitStack() as b1s:
                msum_p = b1s.enter_context(nc.psum_tensor("msum_p", [1, TQ], F32))
                sqsum_p = b1s.enter_context(nc.psum_tensor("sqsum_p", [1, TQ], F32))
                mm_hist = {}
                for kc in range(KC):
                    buf = xn1_sb[:, ts(kc, TQ)]
                    sq = sqslab[:, ts(kc % 2, TQ)]
                    v_ld = P.sdma(buf, x1T[ts(kc, 128), :])
                    w = [(P.mm, mm_hist[kc - 2])] if kc >= 2 else []
                    v_sq = P.scalar(lambda s, buf=buf, sq=sq: s.activation(sq, buf, AF.Square),
                                    waits=[(P.st, v_ld)] + w)
                    def f(t, kc=kc, buf=buf, sq=sq):
                        inst = None
                        for (o, n) in NTQ:
                            t.matmul(msum_p[:, o:o + n], ones_c[:], buf[:, o:o + n],
                                     start=(kc == 0), stop=(kc == KC - 1))
                        for (o, n) in NTQ:
                            inst = t.matmul(sqsum_p[:, o:o + n], ones_c[:], sq[:, o:o + n],
                                            start=(kc == 0), stop=(kc == KC - 1))
                        return inst
                    mm_hist[kc] = P.tensor(f, waits=[(P.st, v_ld), (P.ac, v_sq)])
                v_stats = P.mm.n

                def f(v):
                    v.tensor_scalar(out=mean_r, in0=msum_p[:], scalar1=1.0 / DM, scalar2=None, op0=ALU.mult)
                    v.drain()
                    v.tensor_tensor(m2_r, mean_r, mean_r, op=ALU.mult)
                    v.drain()
                    return v.scalar_tensor_tensor(out=var_r, in0=sqsum_p[:], scalar=1.0 / DM, in1=m2_r,
                                                  op0=ALU.mult, op1=ALU.subtract)
                v_rows = P.vector(f, waits=[(P.mm, v_stats)])
                def f(s):
                    s.activation(lnv_r, var_r, AF.Ln, bias=1e-5)
                    s.drain()
                    return s.activation(rstd_r[:], lnv_r, AF.Exp, scale=-0.5)
                v_rstd = P.scalar(f, waits=[(P.dv, v_rows)])
                v_negmr = P.vector(
                    lambda v: v.scalar_tensor_tensor(out=negmr_r[:], in0=mean_r, scalar=-1.0,
                                                     in1=rstd_r[:], op0=ALU.mult, op1=ALU.mult),
                    waits=[(P.ac, v_rstd)])

            with contextlib.ExitStack() as b1b:
                bc_p = b1b.enter_context(nc.psum_tensor("bc_p", [128, 512], F32))
                prev_bc = None
                for row, bcast in [(rstd_r, rstd_b), (negmr_r, negmr_b)]:
                    for (o, n) in NTQ:
                        wb_ = [(P.dv, prev_bc)] if prev_bc is not None else []
                        if n >= 256:
                            v_b = P.tensor(
                                lambda t, row=row, o=o, n=n: t.matmul(
                                    bc_p[:, 0:n], ones_rf[:, :], row[:, o:o + n].bitcast(F32R),
                                    start=True, stop=True),
                                waits=[(P.dv, v_negmr)] + wb_)
                        else:
                            v_b = P.tensor(
                                lambda t, row=row, o=o, n=n: t.matmul(
                                    bc_p[:, 0:n], ones_ff[:, :], row[:, o:o + n].bitcast(F32),
                                    start=True, stop=True),
                                waits=[(P.dv, v_negmr)] + wb_)
                        prev_bc = P.vector(
                            lambda v, bcast=bcast, o=o, n=n: v.tensor_copy(bcast[:, o:o + n], bc_p[:, 0:n]),
                            waits=[(P.mm, v_b)])
                v_bc = P.dv.n

                for kc in range(KC):
                    buf = xn1_sb[:, ts(kc, TQ)]
                    sq = sqslab[:, ts(kc % 2, TQ)]
                    def f(v, buf=buf, sq=sq):
                        v.tensor_tensor(sq, buf, rstd_b[:], op=ALU.mult)
                        v.drain()
                        return v.tensor_tensor(buf, sq, negmr_b[:], op=ALU.add)
                    v_n = P.vector(f, waits=[(P.dv, v_bc)])
                    P.sdma(xn1T[ts(kc, 128), :], buf, waits=[(P.dv, v_n)])
        v_xn1 = P.dv.n

        P.barrier()

        with contextlib.ExitStack() as b2:
            Ep = b2.enter_context
            wbuf = Ep(nc.sbuf_tensor("wbuf", [128, 2 * KC * 128], BF16))
            ev = Ep(nc.sbuf_tensor("ev", [128, 2 * TQ], BF16))
            cacc = Ep(nc.sbuf_tensor("cacc", [128, 1024], F32))
            csil = Ep(nc.sbuf_tensor("csil", [128, 2 * 1024], BF16))
            dtraw = Ep(nc.sbuf_tensor("dtraw", [128, TQ], F32))
            dts = Ep(nc.sbuf_tensor("dts", [128, 5 * 1024], F32))
            ip_p = Ep(nc.psum_tensor("ip_p", [128, TQ], F32))

            hist = {}
            for oc in range(145):
                wb = wbuf[:, ts(oc % 2, KC * 128)]
                evb = ev[:, ts(oc % 2, TQ)]
                w = [(P.mm, hist[oc - 2])] if oc >= 2 else []
                v_w = P.gdma(wb.rearrange("p (c n) -> p c n", n=128),
                             wipT[:, ts(oc, 128)].rearrange("(c p) n -> p c n", p=128), waits=w)
                wprev = [(P.dv, hist[('e', oc - 1)])] if oc >= 1 else []
                def f(t, wb=wb):
                    inst = None
                    for kc in range(KC):
                        for (o, n) in NTQ:
                            inst = t.matmul(ip_p[:, o:o + n], wb[:, ts(kc, 128)],
                                            xn1_sb[:, kc * TQ + o: kc * TQ + o + n],
                                            start=(kc == 0), stop=(kc == KC - 1))
                    return inst
                hist[oc] = v_m = P.tensor(f, waits=[(P.ld, v_w), (P.dv, v_xn1)] + wprev)

                wev = []
                if ('s', oc - 2) in hist:
                    wev.append((P.st, hist[('s', oc - 2)]))
                if ('cv', oc - 2) in hist:
                    wev.append((P.dv, hist[('cv', oc - 2)]))
                if oc < 64:
                    hist[('e', oc)] = v_e = P.vector(
                        lambda v, evb=evb: v.tensor_copy(evb[:, 0:1024], ip_p[:, 3:TQ]),
                        waits=[(P.mm, v_m)] + wev)
                    hist[('s', oc)] = P.sdma(gateT[ts(oc, 128), :], evb[:, 0:1024], waits=[(P.dv, v_e)])
                elif oc < 144:
                    cc = oc - 64
                    hist[('e', oc)] = v_e = P.vector(
                        lambda v, evb=evb: v.tensor_copy(evb[:, 0:TQ], ip_p[:, 0:TQ]),
                        waits=[(P.mm, v_m)] + wev)
                    def fpool(g, evb=evb, cc=cc):
                        g.tensor_scalar(out=cacc[:], in0=evb[:, 0:1024],
                                        scalar1=cw_sb[:, cc * 4: cc * 4 + 1], scalar2=None,
                                        op0=ALU.mult)
                        inst = None
                        for tap in range(1, 4):
                            g.drain()
                            inst = g.scalar_tensor_tensor(out=cacc[:], in0=evb[:, tap:tap + 1024],
                                                          scalar=cw_sb[:, cc * 4 + tap: cc * 4 + tap + 1],
                                                          in1=cacc[:], op0=ALU.mult, op1=ALU.add)
                        return inst
    
                    wcv = [(P.dv, v_e), (P.ld, v_const)]
                    if ('c', oc - 1) in hist:
                        wcv.append((P.ac, hist[('c', oc - 1)]))
                    hist[('cv', oc)] = v_cv = P.vector(fpool, waits=wcv)
                    csb = csil[:, ts(cc % 2, 1024)]
                    wsi = [(P.dv, v_cv)]
                    if ('s', oc - 2) in hist:
                        wsi.append((P.st, hist[('s', oc - 2)]))
                    v_si = P.scalar(
                        lambda s, csb=csb, cc=cc: s.activation(csb, cacc[:], AF.Silu,
                                                               bias=cb_sb[:, cc:cc + 1]),
                        waits=wsi)
                    hist[('c', oc)] = v_si
                    if cc < 64:
                        v_s = P.sdma(xT[ts(cc, 128), :], csb, waits=[(P.ac, v_si)])
                    elif cc < 72:
                        v_s = P.sdma(BT[ts(cc - 64, 128), :], csb, waits=[(P.ac, v_si)])
                    else:
                        v_s = P.sdma(CT[ts(cc - 72, 128), :], csb, waits=[(P.ac, v_si)])
                    hist[('s', oc)] = v_s
                else:
                    v_e = P.vector(lambda v: v.tensor_copy(dtraw[:], ip_p[:, 0:TQ]),
                                   waits=[(P.mm, v_m)])
                    hist[('e', oc)] = v_e
                    rb = dts[:, 0:1024]
                    ab = dts[:, 1024:2048]
                    eb = dts[:, 2048:3072]
                    lb = dts[:, 3072:4096]
                    dt_ = dts[:, 4096:5120]
                    v_rb = P.vector(lambda v: v.tensor_scalar(out=rb, in0=dtraw[:, 3:TQ],
                                                              scalar1=dtb_sb[:, 0:1], scalar2=None,
                                                              op0=ALU.add),
                                    waits=[(P.dv, v_e), (P.ld, v_const)])
                    v_abs = P.scalar(lambda s: s.activation(ab, rb, AF.Abs),
                                     waits=[(P.dv, v_rb)])
                    def f(s):
                        s.activation(eb, ab, AF.Exp, scale=-1.0)
                        s.drain()
                        return s.activation(lb, eb, AF.Ln, bias=1.0)
                    v_ln = P.scalar(f, waits=[(P.ac, v_abs)])
                    def fv(v):
                        v.tensor_scalar(out=dt_, in0=rb, scalar1=0.0, scalar2=None, op0=ALU.max)
                        v.drain()
                        v.tensor_tensor(dt_, dt_, lb, op=ALU.add)
                        v.drain()
                        inst = v.tensor_scalar(out=ab, in0=dt_, scalar1=nega_sb[:, 0:1],
                                               scalar2=None, op0=ALU.mult)
                        return inst
                    v_dt = P.vector(fv, waits=[(P.ac, v_ln), (P.dv, v_na)])
                    def fv(v):
                        inst = None
                        for c in range(4):
                            inst = v.tensor_tensor_scan(out=eb[:, ts(c, 256)], data0=ab[:, ts(c, 256)],
                                                        data1=ab[:, ts(c, 256)], initial=0.0,
                                                        op0=ALU.add, op1=ALU.bypass)
                        return inst
                    v_cum = P.vector(fv, waits=[(P.dv, v_dt)])
                    v_dtb = P.vector(lambda v: v.tensor_copy(csil[:, 0:1024], dt_),
                                     waits=[(P.dv, v_dt), (P.ac, hist.get(('c', 143), 0)),
                                            (P.st, hist.get(('s', 142), 0))])
                    P.sdma(dtF[:, :], dt_, waits=[(P.dv, v_dt)])
                    P.sdma(dtB[:, :], csil[:, 0:1024], waits=[(P.dv, v_dtb)])
                    P.sdma(AdtF[:, :], ab, waits=[(P.dv, v_cum)])
                    P.sdma(cumrelF[:, :], eb, waits=[(P.dv, v_cum)])

            vfin = P.st.n
            sched.on("sync", lambda s, vfin=vfin: s.wait_ge(P.st.h, vfin))

        with nc.allow_low_precision("bf16 by design"):
            sched.emit(nc)
    return nc


def host_inputs_lB(inp, x1_list):
    shared = {
        "wipT": np.ascontiguousarray(inp["L1_attn_in_proj_w"].T),
        "convW": np.ascontiguousarray(inp["L1_attn_conv_w"][:, 0, :].reshape(80, 128, 4)
                                      .transpose(1, 0, 2).reshape(128, 320)),
        "convB": np.ascontiguousarray(inp["L1_attn_conv_b"].reshape(80, 128).T),
        "dtBias": inp["L1_attn_dt_bias"].reshape(128, 1),
        "Alog": inp["L1_attn_A_log"].reshape(128, 1),
        "onescol": np.ones((128, 1), BF),
        "onesrow_f": np.ones((1, 128), np.float32),
        "fconsts": np.tile(np.array([FCONSTS], np.float32), (128, 1)),
    }
    maps = []
    for core in range(8):
        m = dict(shared)
        m["x1T"] = x1_list[core]
        maps.append(m)
    return maps



CS = 256


def build_lC():
    nc = bass.Bass("TRN2", target_bir_lowering=False)
    ins = {}
    for p in range(2):
        ins[f"xTM{p}"] = nc.dram_tensor(f"xTM{p}", [4096, 1024], BF16, kind="ExternalInput")
        ins[f"Bfm{p}"] = nc.dram_tensor(f"Bfm{p}", [128, 4096], BF16, kind="ExternalInput")
        ins[f"Cfm{p}"] = nc.dram_tensor(f"Cfm{p}", [128, 4096], BF16, kind="ExternalInput")
        ins[f"dtTM{p}"] = nc.dram_tensor(f"dtTM{p}", [4096, 16], BF16, kind="ExternalInput")
        ins[f"AdtTM{p}"] = nc.dram_tensor(f"AdtTM{p}", [4096, 16], F32, kind="ExternalInput")
        ins[f"cumF{p}"] = nc.dram_tensor(f"cumF{p}", [16, 4096], F32, kind="ExternalInput")
    sel16 = nc.dram_tensor("sel16", [16, 16 * 256], F32, kind="ExternalInput")
    i128b = nc.dram_tensor("i128b", [128, 128], BF16, kind="ExternalInput")
    trilS = nc.dram_tensor("trilS", [128, 128], F32, kind="ExternalInput")
    triuI = nc.dram_tensor("triuI", [128, 128], F32, kind="ExternalInput")
    ones2 = nc.dram_tensor("ones2", [128, 128], F32, kind="ExternalInput")
    gmask = nc.dram_tensor("gmask", [128, 512], BF16, kind="ExternalInput")
    onesrow_f = nc.dram_tensor("onesrow_f", [1, 128], F32, kind="ExternalInput")
    i16f = nc.dram_tensor("i16f", [16, 16], F32, kind="ExternalInput")
    outs = {}
    for p in range(2):
        outs[p] = nc.dram_tensor(f"yTM{p}", [4096, 1024], BF16, kind="ExternalOutput")

    sched = Sched()
    with contextlib.ExitStack() as ctx:
        E = ctx.enter_context
        P = Prog(nc, ctx, sched)

        sel_r = E(nc.sbuf_tensor("sel_r", [16, 16 * 256], F32R))
        i128_sb = E(nc.sbuf_tensor("i128_sb", [128, 128], BF16))
        tril_sb = E(nc.sbuf_tensor("tril_sb", [128, 128], F32))
        triu_sb = E(nc.sbuf_tensor("triu_sb", [128, 128], F32))
        ones2_sb = E(nc.sbuf_tensor("ones2_sb", [128, 128], F32))
        gmask_sb = E(nc.sbuf_tensor("gmask_sb", [128, 512], BF16))
        ones_rf = E(nc.sbuf_tensor("ones_rf", [1, 128], F32R))
        i16_sb = E(nc.sbuf_tensor("i16_sb", [16, 16], F32))

        xtm_sb = E(nc.sbuf_tensor("xtm_sb", [128, 32 * 1024], BF16))
        bfm_sb = E(nc.sbuf_tensor("bfm_sb", [128, 4096], BF16))
        cfm_sb = E(nc.sbuf_tensor("cfm_sb", [128, 4096], BF16))
        dttm_sb = E(nc.sbuf_tensor("dttm_sb", [128, 32 * 16], BF16))
        adt_sb = E(nc.sbuf_tensor("adt_sb", [128, 32 * 16], F32))
        cum_r = E(nc.sbuf_tensor("cum_r", [16, 4096], F32R))
        ncum_r = E(nc.sbuf_tensor("ncum_r", [16, 4096], F32R))
        R_sb = E(nc.sbuf_tensor("R_sb", [128, 1024], F32))
        Rbf_sb = E(nc.sbuf_tensor("Rbf_sb", [128, 1024], BF16))
        xs_sb = E(nc.sbuf_tensor("xs_sb", [128, 2 * 1024], BF16))
        xsd_sb = E(nc.sbuf_tensor("xsd_sb", [128, 2 * 1024], BF16))
        dec_sb = E(nc.sbuf_tensor("dec_sb", [128, 64], BF16))
        btm_sb = E(nc.sbuf_tensor("btm_sb", [128, 2 * 128], BF16))
        gms_sb = E(nc.sbuf_tensor("gms_sb", [128, 512], BF16))
        tmin_sb = E(nc.sbuf_tensor("tmin_sb", [128, 512], F32))
        lbf_sb = E(nc.sbuf_tensor("lbf_sb", [128, 512], BF16))
        sc_sb = E(nc.sbuf_tensor("sc_sb", [128, 16 * 512], BF16))
        ysl_sb = E(nc.sbuf_tensor("ysl_sb", [128, 2 * 1024], BF16))
        ea_sb = E(nc.sbuf_tensor("ea_sb", [1, 16], F32))
        eahd_sb = E(nc.sbuf_tensor("eahd_sb", [1, 1024], F32R))
        da_sb = E(nc.sbuf_tensor("da_sb", [128, 1024], BF16))

        dec_p = E(nc.psum_tensor("dec_p", [128, 64], F32))
        gm_p = E(nc.psum_tensor("gm_p", [128, 512], F32))
        tp2_p = E(nc.psum_tensor("tp2_p", [128, 512], F32))
        y_p = E(nc.psum_tensor("y_p", [128, 1024], F32))
        s_p = E(nc.psum_tensor("s_p", [128, 1024], F32))
        tpb_p = E(nc.psum_tensor("tpb_p", [128, 128], F32))

        for dst, srcd in [(sel_r, sel16), (i128_sb, i128b), (tril_sb, trilS),
                          (triu_sb, triuI), (ones2_sb, ones2), (gmask_sb, gmask),
                          (ones_rf, onesrow_f), (i16_sb, i16f)]:
            P.gdma(dst[:], srcd[:, :])

        for p in range(2):
            for tc in range(32):
                P.sdma(xtm_sb[:, ts(tc, 1024)], ins[f"xTM{p}"][ts(tc, 128), :])
            P.sdma(bfm_sb[:], ins[f"Bfm{p}"][:, :])
            P.sdma(cfm_sb[:], ins[f"Cfm{p}"][:, :])
            P.sdma(dttm_sb[:].rearrange("p (c n) -> p c n", n=16),
                   ins[f"dtTM{p}"][:, :].rearrange("(c p) n -> p c n", p=128))
            v_ld1 = P.gdma(adt_sb[:].rearrange("p (c n) -> p c n", n=16),
                           ins[f"AdtTM{p}"][:, :].rearrange("(c p) n -> p c n", p=128))
            v_ld2 = P.gdma(cum_r[:], ins[f"cumF{p}"][:, :])
            v_lds = P.st.n
            v_ldg = P.ld.n
            v_neg = P.vector(
                lambda v: v.tensor_scalar(out=ncum_r[:], in0=cum_r[:], scalar1=-1.0,
                                          scalar2=None, op0=ALU.mult),
                waits=[(P.ld, v_ldg)])
            def f(v):
                v.memset(R_sb[:], 0.0)
                v.drain()
                return v.memset(Rbf_sb[:], 0.0)
            v_r0 = P.vector(f, waits=[(P.dv, v_neg)])

            hist = {}
            prev = dict(dec=None, gms=None, sc=None, y=None, s=None, tpb=None,
                        xs=None, da=None, rup=v_r0, yst={}, tmin=None)
            for c in range(16):
                tok0 = c * CS
                t0c = tok0
                wdec = [(P.ld, v_ldg), (P.st, v_lds)]
                if prev["dec"] is not None:
                    wdec.append((P.ac, prev["dec"]))
                adt0 = adt_sb[:, (tok0 // 128) * 16: (tok0 // 128) * 16 + 16]
                adt1 = adt_sb[:, (tok0 // 128 + 1) * 16: (tok0 // 128 + 1) * 16 + 16]
                def f(t, adt0=adt0, adt1=adt1):
                    t.matmul(dec_p[:, 0:16], tril_sb[:], adt0, start=True, stop=False)
                    t.matmul(dec_p[:, 0:16], ones2_sb[:], adt1, start=False, stop=True)
                    t.matmul(dec_p[:, 16:32], tril_sb[:], adt1, start=False, stop=True)
                    t.matmul(dec_p[:, 32:48], triu_sb[:], adt0, start=False, stop=True)
                    t.matmul(dec_p[:, 48:64], ones2_sb[:], adt0, start=False, stop=False)
                    return t.matmul(dec_p[:, 48:64], triu_sb[:], adt1, start=False, stop=True)
                v_dm = P.tensor(f, waits=wdec)
                prev["dec"] = v_de = P.scalar(
                    lambda s: s.activation(dec_sb[:], dec_p[:], AF.Exp),
                    waits=[(P.mm, v_dm)])
                wxs = [(P.ac, v_de)]
                if prev["xs"] is not None:
                    wxs.append((P.mm, prev["xs"]))
                def f(v, tok0=tok0):
                    inst = None
                    for ti in range(2):
                        tc = tok0 // 128 + ti
                        dtv = dttm_sb[:, tc * 16:(tc + 1) * 16].unsqueeze(2).broadcast_to([128, 16, 64])
                        v.tensor_tensor(xs_sb[:, ts(ti, 1024)], xtm_sb[:, ts(tc, 1024)], dtv,
                                        op=ALU.mult)
                        v.drain()
                        dcv = dec_sb[:, ti * 16:(ti + 1) * 16].unsqueeze(2).broadcast_to([128, 16, 64])
                        inst = v.tensor_tensor(xsd_sb[:, ts(ti, 1024)], xs_sb[:, ts(ti, 1024)], dcv,
                                               op=ALU.mult)
                        v.drain()
                    return inst
                v_xs = P.vector(f, waits=wxs)
                wg = [(P.st, v_lds)]
                if prev["gms"] is not None:
                    wg.append((P.dv, prev["gms"]))
                def f(t, t0c=t0c):
                    t.matmul(gm_p[:, 0:256], bfm_sb[:, t0c:t0c + 128], cfm_sb[:, t0c:t0c + 256],
                             start=True, stop=True)
                    return t.matmul(gm_p[:, 256:512], bfm_sb[:, t0c + 128:t0c + 256],
                                    cfm_sb[:, t0c:t0c + 256], start=False, stop=True)
                v_gm = P.tensor(f, waits=wg)
                prev["gms"] = v_gms = P.vector(
                    lambda v: v.tensor_tensor(gms_sb[:], gm_p[:], gmask_sb[:], op=ALU.mult),
                    waits=[(P.mm, v_gm)])
                wb = [(P.dv, prev["tpb"])] if prev["tpb"] is not None else []
                v_bt0 = P.tensor(lambda t, t0c=t0c: t.transpose(
                    tpb_p[:].bitcast(BF16)[:, 0:128], bfm_sb[:, t0c:t0c + 128], i128_sb[:]),
                    waits=[(P.st, v_lds)] + wb)
                v_be0 = P.vector(lambda v: v.tensor_copy(btm_sb[:, 0:128],
                                                         tpb_p[:].bitcast(BF16)[:, 0:128]),
                                 waits=[(P.mm, v_bt0)])
                v_bt1 = P.tensor(lambda t, t0c=t0c: t.transpose(
                    tpb_p[:].bitcast(BF16)[:, 0:128], bfm_sb[:, t0c + 128:t0c + 256], i128_sb[:]),
                    waits=[(P.dv, v_be0)])
                prev["tpb"] = v_be1 = P.vector(
                    lambda v: v.tensor_copy(btm_sb[:, 128:256], tpb_p[:].bitcast(BF16)[:, 0:128]),
                    waits=[(P.mm, v_bt1)])
                for hh in range(16):
                    wt = [(P.dv, prev["tmin"])] if prev["tmin"] is not None else [(P.dv, v_neg)]
                    def f(t, hh=hh, t0c=t0c):
                        sel0 = sel_r[:, hh * 256: hh * 256 + 128]
                        sel1 = sel_r[:, hh * 256: hh * 256 + 256]
                        t.matmul(tp2_p[:, 0:256], sel0, cum_r[:, t0c:t0c + 256], start=True, stop=False)
                        t.matmul(tp2_p[:, 0:256], ncum_r[:, t0c:t0c + 128], sel1, start=False, stop=True)
                        t.matmul(tp2_p[:, 256:512], sel0, cum_r[:, t0c:t0c + 256], start=False, stop=False)
                        return t.matmul(tp2_p[:, 256:512], ncum_r[:, t0c + 128:t0c + 256], sel1,
                                        start=False, stop=True)
                    v_t2 = P.tensor(f, waits=wt)
                    prev["tmin"] = v_tm = P.vector(
                        lambda v: v.tensor_scalar(out=tmin_sb[:], in0=tp2_p[:], scalar1=0.0,
                                                  scalar2=None, op0=ALU.min),
                        waits=[(P.mm, v_t2)])
                    v_lx = P.scalar(lambda s: s.activation(lbf_sb[:], tmin_sb[:], AF.Exp),
                                    waits=[(P.dv, v_tm)])
                    hist[('sc', hh)] = P.vector(
                        lambda v, hh=hh: v.tensor_tensor(sc_sb[:, ts(hh, 512)], lbf_sb[:],
                                                         gms_sb[:], op=ALU.mult),
                        waits=[(P.ac, v_lx), (P.dv, v_gms),
                               (P.mm, prev["yd"]) if False else (P.ac, v_lx)])
                v_scall = P.dv.n
                for ih in range(2):
                    wy = [(P.dv, prev["rup"]), (P.dv, v_scall), (P.dv, v_xs)]
                    if prev["y"] is not None:
                        wy.append((P.dv, prev["y"]))
                    def f(t, ih=ih, t0c=t0c):
                        inst = None
                        for hh in range(16):
                            inst = t.matmul(y_p[:, hh * 64:(hh + 1) * 64],
                                            cfm_sb[:, t0c + ih * 128: t0c + ih * 128 + 128],
                                            Rbf_sb[:, hh * 64:(hh + 1) * 64],
                                            start=(hh % 8 == 0), stop=False)
                        return inst
                    v_yo = P.tensor(f, waits=wy)
                    def f(v, ih=ih):
                        sdv = dec_sb[:, 32 + ih * 16: 48 + ih * 16].unsqueeze(2).broadcast_to([128, 16, 64])
                        return v.tensor_tensor(y_p[:], y_p[:], sdv, op=ALU.mult)
                    v_sd = P.vector(f, waits=[(P.mm, v_yo), (P.ac, v_de)])
                    def f(t, ih=ih):
                        inst = None
                        for hh in range(16):
                            base = hh * 512
                            if ih == 0:
                                inst = t.matmul(y_p[:, hh * 64:(hh + 1) * 64],
                                                sc_sb[:, base: base + 128],
                                                xs_sb[:, hh * 64: hh * 64 + 64],
                                                start=False, stop=False)
                            else:
                                t.matmul(y_p[:, hh * 64:(hh + 1) * 64],
                                         sc_sb[:, base + 128: base + 256],
                                         xs_sb[:, hh * 64: hh * 64 + 64],
                                         start=False, stop=False)
                                inst = t.matmul(y_p[:, hh * 64:(hh + 1) * 64],
                                                sc_sb[:, base + 384: base + 512],
                                                xs_sb[:, 1024 + hh * 64: 1024 + hh * 64 + 64],
                                                start=False, stop=True)
                        return inst
                    v_yd = P.tensor(f, waits=[(P.dv, v_sd)])
                    ysl = ysl_sb[:, ts(ih, 1024)]
                    wev = [(P.st, prev["yst"].get(ih, 0))] if prev["yst"].get(ih) else []
                    prev["y"] = v_ye = P.vector(
                        lambda v, ysl=ysl: v.tensor_copy(ysl, y_p[:]),
                        waits=[(P.mm, v_yd)] + wev)
                    prev["yst"][ih] = P.sdma(outs[p][tok0 + ih * 128: tok0 + ih * 128 + 128, :],
                                             ysl, waits=[(P.dv, v_ye)])
                ws = [(P.dv, prev["rup"]), (P.dv, v_xs), (P.dv, v_be1)]
                if prev["s"] is not None:
                    ws.append((P.dv, prev["s"]))
                def f(t):
                    inst = None
                    for hh in range(16):
                        for kh in range(2):
                            inst = t.matmul(s_p[:, hh * 64:(hh + 1) * 64],
                                            btm_sb[:, ts(kh, 128)],
                                            xsd_sb[:, kh * 1024 + hh * 64: kh * 1024 + hh * 64 + 64],
                                            start=(hh % 8 == 0 and kh == 0), stop=(kh == 1))
                    return inst
                v_sm = P.tensor(f, waits=ws)
                wda = [(P.dv, prev["y"])]
                v_at = P.tensor(
                    lambda t, t0c=t0c: t.matmul(tpb_p[0:1, 0:16],
                                                cum_r[:, t0c + 255: t0c + 256].bitcast(F32),
                                                i16_sb[:], start=True, stop=True),
                    waits=[(P.dv, prev["tpb"])])
                v_ae = P.vector(lambda v: v.tensor_copy(ea_sb[:], tpb_p[0:1, 0:16]),
                                waits=[(P.mm, v_at)])
                v_ax = P.scalar(lambda s: s.activation(ea_sb[:], ea_sb[:], AF.Exp),
                                waits=[(P.dv, v_ae)])
                v_ah = P.vector(
                    lambda v: v.tensor_copy(eahd_sb[:],
                                            ea_sb[:].unsqueeze(2).broadcast_to([1, 16, 64])),
                    waits=[(P.ac, v_ax)])
                def f(t):
                    t.matmul(y_p[:, 0:512], ones_rf[:, :], eahd_sb[:, 0:512], start=True, stop=True)
                    return t.matmul(y_p[:, 512:1024], ones_rf[:, :], eahd_sb[:, 512:1024],
                                    start=True, stop=True)
                v_dam = P.tensor(f, waits=[(P.dv, v_ah)] + wda)
                v_dae = P.vector(lambda v: v.tensor_copy(da_sb[:], y_p[:]),
                                 waits=[(P.mm, v_dam)])
                prev["da"] = v_dae
                def f(v):
                    v.tensor_tensor(R_sb[:], R_sb[:], da_sb[:], op=ALU.mult)
                    v.drain()
                    v.tensor_tensor(R_sb[:], R_sb[:], s_p[:], op=ALU.add)
                    v.drain()
                    return v.tensor_copy(Rbf_sb[:], R_sb[:])
                prev["rup"] = P.vector(f, waits=[(P.dv, v_dae), (P.mm, v_sm)])
                prev["y"] = prev["rup"]
            P.barrier()

        vfin = P.st.n
        sched.on("sync", lambda s, vfin=vfin: s.wait_ge(P.st.h, vfin))
        with nc.allow_low_precision("bf16/f32r by design"):
            sched.emit(nc)
    return nc


def host_inputs_lC(lb_outs):
    xs = {}
    for b in range(2):
        cores = [b * 4 + i for i in range(4)]
        xs[('x', b)] = np.concatenate([lb_outs[c]["xT"] for c in cores], 1)
        xs[('B', b)] = np.concatenate([lb_outs[c]["BT"] for c in cores], 1)
        xs[('C', b)] = np.concatenate([lb_outs[c]["CT"] for c in cores], 1)
        xs[('dt', b)] = np.concatenate([lb_outs[c]["dtB"] for c in cores], 1)
        xs[('Adt', b)] = np.concatenate([lb_outs[c]["AdtF"] for c in cores], 1)
        xs[('cum', b)] = np.concatenate([lb_outs[c]["cumrelF"] for c in cores], 1)
    pair_table = []
    sel = np.zeros((16, 16 * 256), np.float32)
    for hh in range(16):
        sel[hh, hh * 256:(hh + 1) * 256] = 1.0
    i = np.arange(128)[:, None]; j = np.arange(128)[None, :]
    tril_s = (i > j).astype(np.float32)
    triu_i = (i <= j).astype(np.float32)
    jj = np.arange(128)[:, None]; ii = np.arange(256)[None, :]
    g0 = (jj <= ii).astype(np.float32)
    g1 = ((jj + 128) <= ii).astype(np.float32)
    consts = {
        "sel16": sel,
        "i128b": np.eye(128, dtype=BF),
        "trilS": tril_s, "triuI": triu_i,
        "ones2": np.ones((128, 128), np.float32),
        "gmask": np.concatenate([g0, g1], 1).astype(BF),
        "onesrow_f": np.ones((1, 128), np.float32),
        "i16f": np.eye(16, dtype=np.float32),
    }
    maps = []
    for core in range(8):
        pairs = [(core // 4, (core % 4) * 2), (core // 4, (core % 4) * 2 + 1)]
        pair_table.append(pairs)
        m = dict(consts)
        for p, (b, g) in enumerate(pairs):
            m[f"xTM{p}"] = np.ascontiguousarray(xs[('x', b)][g * 1024:(g + 1) * 1024, :].T)
            m[f"Bfm{p}"] = np.ascontiguousarray(xs[('B', b)][g * 128:(g + 1) * 128, :])
            m[f"Cfm{p}"] = np.ascontiguousarray(xs[('C', b)][g * 128:(g + 1) * 128, :])
            m[f"dtTM{p}"] = np.ascontiguousarray(xs[('dt', b)][g * 16:(g + 1) * 16, :].T)
            m[f"AdtTM{p}"] = np.ascontiguousarray(xs[('Adt', b)][g * 16:(g + 1) * 16, :].T)
            m[f"cumF{p}"] = np.ascontiguousarray(xs[('cum', b)][g * 16:(g + 1) * 16, :])
        maps.append(m)
    return maps, pair_table



def build_lD():
    nc = bass.Bass("TRN2", target_bir_lowering=False)
    yT = nc.dram_tensor("yT", [8192, 1024], BF16, kind="ExternalInput")
    xT = nc.dram_tensor("xT", [8192, 1024], BF16, kind="ExternalInput")
    gateT = nc.dram_tensor("gateT", [8192, 1024], BF16, kind="ExternalInput")
    xn1T = nc.dram_tensor("xn1T", [DM, TQ], BF16, kind="ExternalInput")
    x1T = nc.dram_tensor("x1T", [DM, TQ], BF16, kind="ExternalInput")
    wopT = nc.dram_tensor("wopT", [8192, DM], F32, kind="ExternalInput")
    U1w = nc.dram_tensor("U1w", [DM, 1024], F32, kind="ExternalInput")
    V1w = nc.dram_tensor("V1w", [1024, 22016], F32, kind="ExternalInput")
    U2w = nc.dram_tensor("U2w", [11008, 1024], F32, kind="ExternalInput")
    V2w = nc.dram_tensor("V2w", [1024, DM], F32, kind="ExternalInput")
    b1c = nc.dram_tensor("b1c", [128, 172], F32, kind="ExternalInput")
    b2c = nc.dram_tensor("b2c", [128, 32], F32, kind="ExternalInput")
    nwc = nc.dram_tensor("nwc", [128, 64], F32, kind="ExternalInput")
    Dc = nc.dram_tensor("Dc", [128, 64], F32, kind="ExternalInput")
    onescol = nc.dram_tensor("onescol", [128, 1], BF16, kind="ExternalInput")
    onesrow_f = nc.dram_tensor("onesrow_f", [1, 128], F32, kind="ExternalInput")
    fconsts = nc.dram_tensor("fconsts", [128, 4], F32, kind="ExternalInput")

    x2T = nc.dram_tensor("x2T", [DM, 1024], F32, kind="ExternalOutput")

    sched = Sched()
    with contextlib.ExitStack() as ctx:
        E = ctx.enter_context
        P = Prog(nc, ctx, sched)
        ones_c = E(nc.sbuf_tensor("ones_c", [128, 1], BF16))
        ones_rf = E(nc.sbuf_tensor("ones_rf", [1, 128], F32R))
        b1_sb = E(nc.sbuf_tensor("b1_sb", [128, 172], F32))
        b2_sb = E(nc.sbuf_tensor("b2_sb", [128, 32], F32))
        nw_sb = E(nc.sbuf_tensor("nw_sb", [128, 64], F32))
        d_sb = E(nc.sbuf_tensor("d_sb", [128, 64], F32))
        fc_sb = E(nc.sbuf_tensor("fc_sb", [128, 4], F32))
        for i, val in enumerate(FCONSTS):
            nc.const_aps.register(F32, val, fc_sb[:, i:i + 1])
        for dst, srcd in [(ones_c, onescol), (ones_rf, onesrow_f), (b1_sb, b1c),
                          (b2_sb, b2c), (nw_sb, nwc), (d_sb, Dc), (fc_sb, fconsts)]:
            P.gdma(dst[:], srcd[:, :])
        v_const = P.ld.n

        with contextlib.ExitStack() as d1:
            Ep = d1.enter_context
            yn_sb = Ep(nc.sbuf_tensor("yn_sb", [128, 64 * 1024], BF16))
            ysl = Ep(nc.sbuf_tensor("ysl", [128, 2 * 1024], BF16))
            xsl = Ep(nc.sbuf_tensor("xsl", [128, 2 * 1024], BF16))
            gsl = Ep(nc.sbuf_tensor("gsl", [128, 2 * 1024], BF16))
            sgs = Ep(nc.sbuf_tensor("sgs", [128, 2 * 1024], BF16))
            sqs = Ep(nc.sbuf_tensor("sqs", [128, 2 * 1024], BF16))
            rows = Ep(nc.sbuf_tensor("rows", [1, 2 * 1024], F32))
            rstd_r = Ep(nc.sbuf_tensor("rstd_r", [1, 1024], F32R))
            ss_p = Ep(nc.psum_tensor("ss_p", [1, 1024], F32))
            rb_p = Ep(nc.psum_tensor("rb_p", [128, 1024], F32))

            hist = {}
            for g in range(8):
                for k in range(8):
                    cc = g * 8 + k
                    yb = ysl[:, ts(cc % 2, 1024)]
                    xb = xsl[:, ts(cc % 2, 1024)]
                    gb = gsl[:, ts(cc % 2, 1024)]
                    sg = sgs[:, ts(cc % 2, 1024)]
                    sq = sqs[:, ts(cc % 2, 1024)]
                    w = [(P.dv, hist[cc - 2])] if cc >= 2 else []
                    v_y = P.sdma(yb, yT[ts(cc, 128), :], waits=w)
                    v_x = P.sdma(xb, xT[ts(cc, 128), :], waits=w)
                    v_g = P.sdma(gb, gateT[ts(cc, 128), :], waits=w)
                    wsi = [(P.mm, hist[('q', cc - 2)])] if cc >= 2 else []
                    v_si = P.scalar(lambda s, gb=gb, sg=sg: s.activation(sg, gb, AF.Silu),
                                    waits=[(P.st, v_g)] + wsi)
                    def f(v, yb=yb, xb=xb, sg=sg, cc=cc):
                        v.scalar_tensor_tensor(out=yb, in0=xb, scalar=d_sb[:, cc:cc + 1],
                                               in1=yb, op0=ALU.mult, op1=ALU.add)
                        v.drain()
                        return v.tensor_tensor(yn_sb[:, ts(cc, 1024)], yb, sg, op=ALU.mult)
                    hist[cc] = v_yg = P.vector(
                        f, waits=[(P.st, max(v_y, v_x)), (P.ac, v_si), (P.ld, v_const)])
                    v_sq = P.scalar(lambda s, sq=sq, cc=cc: s.activation(sq, yn_sb[:, ts(cc, 1024)],
                                                                         AF.Square),
                                    waits=[(P.dv, v_yg)])
                    def f(t, k=k, sq=sq):
                        t.matmul(ss_p[:, 0:512], ones_c[:], sq[:, 0:512],
                                 start=(k == 0), stop=(k == 7))
                        return t.matmul(ss_p[:, 512:1024], ones_c[:], sq[:, 512:1024],
                                        start=(k == 0), stop=(k == 7))
                    hist[('q', cc)] = P.tensor(f, waits=[(P.ac, v_sq)])
                v_ssall = P.mm.n
                def f(v):
                    return v.tensor_scalar(out=rows[:, 0:1024], in0=ss_p[:], scalar1=1.0 / 1024.0,
                                           scalar2=None, op0=ALU.mult)
                v_var = P.vector(f, waits=[(P.mm, v_ssall)])
                def f(s):
                    s.activation(rows[:, 1024:2048], rows[:, 0:1024], AF.Ln, bias=1e-5)
                    s.drain()
                    return s.activation(rstd_r[:], rows[:, 1024:2048], AF.Exp, scale=-0.5)
                v_rst = P.scalar(f, waits=[(P.dv, v_var)])
                wrb = [(P.dv, hist[('rb', g - 1)])] if g >= 1 else []
                def f(t):
                    t.matmul(rb_p[:, 0:512], ones_rf[:, :], rstd_r[:, 0:512], start=True, stop=True)
                    return t.matmul(rb_p[:, 512:1024], ones_rf[:, :], rstd_r[:, 512:1024],
                                    start=True, stop=True)
                v_rbm = P.tensor(f, waits=[(P.ac, v_rst)] + wrb)
                last = None
                for k in range(8):
                    cc = g * 8 + k
                    def f(v, cc=cc):
                        return v.scalar_tensor_tensor(out=yn_sb[:, ts(cc, 1024)],
                                                      in0=yn_sb[:, ts(cc, 1024)],
                                                      scalar=nw_sb[:, cc:cc + 1],
                                                      in1=rb_p[:], op0=ALU.mult, op1=ALU.mult)
                    last = P.vector(f, waits=[(P.mm, v_rbm)])
                hist[('rb', g)] = last
            v_ynall = P.dv.n

            P.barrier()

            with contextlib.ExitStack() as d2:
                E2 = d2.enter_context
                wbuf = E2(nc.sbuf_tensor("wbuf", [128, 2 * 64 * 128], BF16))
                x1sl = E2(nc.sbuf_tensor("x1sl", [128, 2 * 1024], BF16))
                x2sl = E2(nc.sbuf_tensor("x2sl", [128, 2 * 1024], F32))
                op_p = E2(nc.psum_tensor("op_p", [128, 1024], F32))
                h2 = {}
                for dc in range(KC):
                    wb = wbuf[:, ts(dc % 2, 64 * 128)]
                    x1b = x1sl[:, ts(dc % 2, 1024)]
                    x2b = x2sl[:, ts(dc % 2, 1024)]
                    wv_ = [(P.mm, h2[dc - 2])] if dc >= 2 else []
                    wv2 = [(P.dv, h2[('e', dc - 2)])] if dc >= 2 else []
                    wv3 = [(P.st, h2[('s', dc - 2)])] if dc >= 2 else []
                    v_w = P.gdma(wb.rearrange("p (c n) -> p c n", n=128),
                                 wopT[:, ts(dc, 128)].rearrange("(c p) n -> p c n", p=128),
                                 waits=wv_)
                    v_x1 = P.sdma(x1b, x1T[ts(dc, 128), 3:TQ], waits=wv2)
                    wprev = [(P.dv, h2[('e', dc - 1)])] if dc >= 1 else []
                    def f(t, wb=wb):
                        inst = None
                        for kk in range(64):
                            for (o, n) in [(0, 512), (512, 512)]:
                                inst = t.matmul(op_p[:, o:o + n], wb[:, ts(kk, 128)],
                                                yn_sb[:, kk * 1024 + o: kk * 1024 + o + n],
                                                start=(kk == 0), stop=(kk == 63))
                        return inst
                    h2[dc] = v_m = P.tensor(f, waits=[(P.ld, v_w), (P.dv, v_ynall)] + wprev)
                    h2[('e', dc)] = v_e = P.vector(
                        lambda v, x1b=x1b, x2b=x2b: v.tensor_tensor(x2b, op_p[:], x1b, op=ALU.add),
                        waits=[(P.mm, v_m), (P.st, v_x1)] + wv3)
                    h2[('s', dc)] = P.sdma(x2T[ts(dc, 128), :], x2b, waits=[(P.dv, v_e)])

        P.barrier()

        with contextlib.ExitStack() as d3:
            E3 = d3.enter_context
            p1_sb = E3(nc.sbuf_tensor("p1_sb", [128, 8 * 1024], BF16))

            with contextlib.ExitStack() as dp1:
                xn1_sb = dp1.enter_context(nc.sbuf_tensor("xn1_sb", [128, KC * 1024], BF16))
                wu1 = dp1.enter_context(nc.sbuf_tensor("wu1", [128, 2 * KC * 128], BF16))
                p1_p = dp1.enter_context(nc.psum_tensor("p1_p", [128, 1024], F32))
                for kc in range(KC):
                    P.sdma(xn1_sb[:, ts(kc, 1024)], xn1T[ts(kc, 128), 3:TQ])
                v_xn1 = P.st.n
                h3 = {}
                for rc in range(8):
                    wb = wu1[:, ts(rc % 2, KC * 128)]
                    wv_ = [(P.mm, h3[rc - 2])] if rc >= 2 else []
                    v_w = P.gdma(wb.rearrange("p (c n) -> p c n", n=128),
                                 U1w[:, ts(rc, 128)].rearrange("(c p) n -> p c n", p=128),
                                 waits=wv_)
                    wprev = [(P.dv, h3[('e', rc - 1)])] if rc >= 1 else []
                    def f(t, wb=wb):
                        inst = None
                        for kc in range(KC):
                            for (o, n) in [(0, 512), (512, 512)]:
                                inst = t.matmul(p1_p[:, o:o + n], wb[:, ts(kc, 128)],
                                                xn1_sb[:, kc * 1024 + o: kc * 1024 + o + n],
                                                start=(kc == 0), stop=(kc == KC - 1))
                        return inst
                    h3[rc] = v_m = P.tensor(f, waits=[(P.ld, v_w), (P.st, v_xn1)] + wprev)
                    h3[('e', rc)] = P.vector(
                        lambda v, rc=rc: v.tensor_copy(p1_sb[:, ts(rc, 1024)], p1_p[:]),
                        waits=[(P.mm, v_m)])
                v_p1 = P.dv.n

            P.barrier()

            act_sb = E3(nc.sbuf_tensor("act_sb", [128, 86 * 512], BF16))
            z_sb = E3(nc.sbuf_tensor("z_sb", [128, 8 * 512], BF16))
            wv1 = E3(nc.sbuf_tensor("wv1", [128, 4 * 8 * 128], BF16))
            wu2 = E3(nc.sbuf_tensor("wu2", [128, 2 * 8 * 128], BF16))
            wv2_ = E3(nc.sbuf_tensor("wv2_", [128, 2 * 8 * 128], BF16))
            sgf = E3(nc.sbuf_tensor("sgf", [128, 512], BF16))
            x2sl2 = E3(nc.sbuf_tensor("x2sl2", [128, 2 * 512], F32))
            x2rl = E3(nc.sbuf_tensor("x2rl", [128, 2 * 512], F32))

            for tt in range(2):
                to = tt * 512
                with contextlib.ExitStack() as dh:
                    hg_p = dh.enter_context(nc.psum_tensor(f"hg_p{tt}", [128, 512], F32))
                    hu_p = dh.enter_context(nc.psum_tensor(f"hu_p{tt}", [128, 512], F32))
                    h4 = {}
                    for fc in range(86):
                        wbg = wv1[:, ts((2 * fc) % 4, 8 * 128)]
                        wbu = wv1[:, ts((2 * fc + 1) % 4, 8 * 128)]
                        wv_ = [(P.mm, h4[fc - 2])] if fc >= 2 else []
                        v_wg = P.gdma(wbg.rearrange("p (c n) -> p c n", n=128),
                                      V1w[:, ts(fc, 128)].rearrange("(c p) n -> p c n", p=128),
                                      waits=wv_)
                        v_wu = P.gdma(wbu.rearrange("p (c n) -> p c n", n=128),
                                      V1w[:, ts(fc + 86, 128)].rearrange("(c p) n -> p c n", p=128),
                                      waits=wv_)
                        wprev = [(P.ac, h4[('a', fc - 1)])] if fc >= 1 else []
                        wprev2 = [(P.dv, h4[('v', fc - 1)])] if fc >= 1 else []
                        def f(t, wbg=wbg, wbu=wbu, to=to):
                            inst = None
                            for rc in range(8):
                                t.matmul(hg_p[:], wbg[:, ts(rc, 128)],
                                         p1_sb[:, rc * 1024 + to: rc * 1024 + to + 512],
                                         start=(rc == 0), stop=(rc == 7))
                            for rc in range(8):
                                inst = t.matmul(hu_p[:], wbu[:, ts(rc, 128)],
                                                p1_sb[:, rc * 1024 + to: rc * 1024 + to + 512],
                                                start=(rc == 0), stop=(rc == 7))
                            return inst
                        h4[fc] = v_m = P.tensor(f, waits=[(P.ld, max(v_wg, v_wu)), (P.dv, v_p1)]
                                                + wprev + wprev2)
                        h4[('a', fc)] = v_sg = P.scalar(
                            lambda s, fc=fc: s.activation(sgf[:], hg_p[:], AF.Silu,
                                                          bias=b1_sb[:, fc:fc + 1]),
                            waits=[(P.mm, v_m)])
                        h4[('v', fc)] = P.vector(
                            lambda v, fc=fc: v.scalar_tensor_tensor(
                                out=act_sb[:, ts(fc, 512)], in0=hu_p[:],
                                scalar=b1_sb[:, 86 + fc: 87 + fc], in1=sgf[:],
                                op0=ALU.add, op1=ALU.mult),
                            waits=[(P.ac, v_sg)])
                    v_act = P.dv.n

                P.barrier()

                with contextlib.ExitStack() as dz:
                    z_p = dz.enter_context(nc.psum_tensor(f"z_p{tt}", [128, 8 * 512], F32))
                    h5 = {}
                    for fc in range(86):
                        wb = wu2[:, ts(fc % 2, 8 * 128)]
                        wv_ = [(P.mm, h5[fc - 2])] if fc >= 2 else []
                        v_w = P.gdma(wb.rearrange("p (c n) -> p c n", n=128),
                                     U2w[ts(fc, 128), :].rearrange("p (c n) -> p c n", n=128),
                                     waits=wv_)
                        def f(t, wb=wb, fc=fc):
                            inst = None
                            for rc in range(8):
                                inst = t.matmul(z_p[:, ts(rc, 512)], wb[:, ts(rc, 128)],
                                                act_sb[:, ts(fc, 512)],
                                                start=(fc == 0), stop=(fc == 85))
                            return inst
                        h5[fc] = P.tensor(f, waits=[(P.ld, v_w), (P.dv, v_act)])
                    v_zmm = P.mm.n
                    v_ze = P.vector(lambda v: v.tensor_copy(z_sb[:], z_p[:]),
                                    waits=[(P.mm, v_zmm)])

                with contextlib.ExitStack() as dz2:
                    op2_p = dz2.enter_context(nc.psum_tensor(f"op2_{tt}", [128, 512], F32))
                    h6 = {}
                    for dc in range(KC):
                        wb = wv2_[:, ts(dc % 2, 8 * 128)]
                        x2b = x2sl2[:, ts(dc % 2, 512)]
                        x2r = x2rl[:, ts(dc % 2, 512)]
                        wv_ = [(P.mm, h6[dc - 2])] if dc >= 2 else []
                        wv2x = [(P.dv, h6[('e', dc - 2)])] if dc >= 2 else []
                        wv3x = [(P.st, h6[('s', dc - 2)])] if dc >= 2 else []
                        v_w = P.gdma(wb.rearrange("p (c n) -> p c n", n=128),
                                     V2w[:, ts(dc, 128)].rearrange("(c p) n -> p c n", p=128),
                                     waits=wv_)
                        v_xr = P.sdma(x2r, x2T[ts(dc, 128), to:to + 512], waits=wv2x)
                        wprev = [(P.dv, h6[('e', dc - 1)])] if dc >= 1 else []
                        def f(t, wb=wb):
                            inst = None
                            for rc in range(8):
                                inst = t.matmul(op2_p[:], wb[:, ts(rc, 128)],
                                                z_sb[:, ts(rc, 512)],
                                                start=(rc == 0), stop=(rc == 7))
                            return inst
                        h6[dc] = v_m = P.tensor(f, waits=[(P.ld, v_w), (P.dv, v_ze)] + wprev)
                        def f(v, x2b=x2b, x2r=x2r, dc=dc):
                            return v.scalar_tensor_tensor(out=x2b, in0=op2_p[:],
                                                          scalar=b2_sb[:, dc:dc + 1], in1=x2r,
                                                          op0=ALU.add, op1=ALU.add)
                        h6[('e', dc)] = v_e = P.vector(f, waits=[(P.mm, v_m), (P.st, v_xr),
                                                                 (P.ld, v_const)] + wv3x)
                        h6[('s', dc)] = P.sdma(x2T[ts(dc, 128), to:to + 512], x2b,
                                               waits=[(P.dv, v_e)])
                P.barrier()

        vfin = P.st.n
        sched.on("sync", lambda s, vfin=vfin: s.wait_ge(P.st.h, vfin))
        with nc.allow_low_precision("bf16 by design"):
            sched.emit(nc)
    return nc


def host_inputs_lD(inp, lb_outs, y_fm_list):
    shared = {
        "wopT": np.ascontiguousarray(inp["L1_attn_out_proj_w"].T),
        "U1w": inp["L1_ffn_U1"],
        "V1w": inp["L1_ffn_V1"],
        "U2w": inp["L1_ffn_U2"],
        "V2w": inp["L1_ffn_V2"],
        "b1c": np.ascontiguousarray(inp["L1_ffn_b1"].reshape(172, 128).T),
        "b2c": np.ascontiguousarray(inp["L1_ffn_b2"].reshape(32, 128).T),
        "nwc": np.ascontiguousarray(inp["L1_attn_norm_w"].reshape(64, 128).T),
        "Dc": np.ascontiguousarray(np.repeat(inp["L1_attn_D"], 64).reshape(64, 128).T),
        "onescol": np.ones((128, 1), BF),
        "onesrow_f": np.ones((1, 128), np.float32),
        "fconsts": np.tile(np.array([FCONSTS], np.float32), (128, 1)),
    }
    maps = []
    for core in range(8):
        m = dict(shared)
        m["yT"] = y_fm_list[core]
        m["xT"] = lb_outs[core]["xT"]
        m["gateT"] = lb_outs[core]["gateT"]
        m["xn1T"] = lb_outs[core]["xn1T"]
        m["x1T"] = lb_outs[core]["x1T"]
        maps.append(m)
    return maps



def kernel(**inputs):
    inp = {k: np.asarray(v) for k, v in inputs.items()}

    ncA = build_lA()
    resA = run_bass_kernel_spmd(ncA, host_inputs_lA(inp), core_ids=list(range(8)))
    x1_list = [np.asarray(resA.results[c]["x1T"]) for c in range(8)]

    ncB = build_lB()
    resB = run_bass_kernel_spmd(ncB, host_inputs_lB(inp, x1_list), core_ids=list(range(8)))
    keysB = ["xn1T", "gateT", "xT", "BT", "CT", "dtF", "dtB", "AdtF", "cumrelF"]
    lb = [{k: np.asarray(resB.results[c][k]) for k in keysB} for c in range(8)]
    for c in range(8):
        lb[c]["x1T"] = x1_list[c]

    ncC = build_lC()
    mapsC, pair_table = host_inputs_lC(lb)
    resC = run_bass_kernel_spmd(ncC, mapsC, core_ids=list(range(8)))

    y_fm_list = []
    for core in range(8):
        b, cc = core // 4, core % 4
        y_fm = np.empty((8192, 1024), BF)
        for g in range(8):
            owner = b * 4 + g // 2
            p = g % 2
            ytm = np.asarray(resC.results[owner][f"yTM{p}"])
            y_fm[g * 1024:(g + 1) * 1024, :] = ytm[cc * 1024:(cc + 1) * 1024, :].T
        y_fm_list.append(np.ascontiguousarray(y_fm))

    ncD = build_lD()
    resD = run_bass_kernel_spmd(ncD, host_inputs_lD(inp, lb, y_fm_list), core_ids=list(range(8)))

    out = np.empty((2, 4096, DM), np.float32)
    for core in range(8):
        b, cc = core // 4, core % 4
        out[b, cc * 1024:(cc + 1) * 1024, :] = np.asarray(resD.results[core]["x2T"]).T
    return out



# version 02
# speedup vs baseline: 1.1213x; 1.1213x over previous
import contextlib
import math
import numpy as np
import ml_dtypes

import concourse.bass as bass
import concourse.mybir as mybir
from concourse.bass_utils import run_bass_kernel_spmd

F32 = mybir.dt.float32
F32R = mybir.dt.float32r
BF16 = mybir.dt.bfloat16
AF = mybir.ActivationFunctionType
ALU = mybir.AluOpType
AXX = mybir.AxisListType.X
BF = ml_dtypes.bfloat16

DM = 4096
TOK = 1024
TQ = 1027
TKV = 1664
KC = 32
NEG = -1e30
FCONSTS = (1e-5, -0.5, 0.5, 2.0)

ENGINES = ("sync", "scalar", "vector", "tensor", "gpsimd")


class Sched:
    def __init__(self):
        self.steps = {e: [] for e in ENGINES}

    def on(self, eng, fn):
        self.steps[eng].append(fn)

    def emit(self, nc):
        with nc.Block() as block:
            for name in ENGINES:
                def mk(nm):
                    def body(eng):
                        for fn in self.steps[nm]:
                            fn(eng)
                    return body
                getattr(block, name)(mk(name))


class Sem:
    def __init__(self, ctx, nc, name):
        self.h = ctx.enter_context(nc.semaphore(name))
        self.n = 0

    def next(self, k=1):
        self.n += k
        return self.n


class Prog:

    def __init__(self, nc, ctx, sched):
        self.nc = nc
        self.sched = sched
        self.ld = Sem(ctx, nc, "s_ld")
        self.st = Sem(ctx, nc, "s_st")
        self.mm = Sem(ctx, nc, "s_mm")
        self.dv = Sem(ctx, nc, "s_dv")
        self.ac = Sem(ctx, nc, "s_ac")
        self.po = Sem(ctx, nc, "s_po")
        self.ph = Sem(ctx, nc, "s_ph")
        self.phk = 0

    def _wrap(self, fn, waits, incs, drain=False):
        def g(eng):
            for (s, v) in waits:
                eng.wait_ge(s.h, v)
            if drain:
                eng.drain()
            inst = fn(eng)
            if drain:
                eng.drain()
                for (s, k) in incs:
                    eng.sem_inc(s.h, k)
            else:
                for (s, k) in incs:
                    inst.then_inc(s.h, k)
        return g

    def tensor(self, fn, waits=()):
        v = self.mm.next()
        self.sched.on("tensor", self._wrap(fn, waits, [(self.mm, 1)]))
        return v

    def vector(self, fn, waits=(), ninc=1):
        v = self.dv.next(ninc)
        self.sched.on("vector", self._wrap(fn, waits, [(self.dv, ninc)], drain=True))
        return v

    def scalar(self, fn, waits=(), ninc=1):
        v = self.ac.next(ninc)
        self.sched.on("scalar", self._wrap(fn, waits, [(self.ac, ninc)], drain=True))
        return v

    def pool(self, fn, waits=(), ninc=1):
        v = self.po.next(ninc)
        self.sched.on("gpsimd", self._wrap(fn, waits, [(self.po, ninc)], drain=True))
        return v

    def mksem(self, ctx, name):
        return Sem(ctx, self.nc, name)

    def gdma(self, dst, src, waits=(), sem=None):
        sem = sem or self.ld
        sem._dma = True
        v = sem.next(16)
        def fn(g):
            return g.dma_start(dst, src)
        self.sched.on("gpsimd", self._wrap(fn, waits, [(sem, 16)]))
        return v

    def sdma(self, dst, src, waits=(), sem=None):
        sem = sem or self.st
        sem._dma = True
        v = sem.next(16)
        def fn(s):
            return s.dma_start(dst, src)
        self.sched.on("sync", self._wrap(fn, waits, [(sem, 16)]))
        return v

    def barrier(self):
        self.phk += 1
        tgt = 5 * self.phk
        self.ph.n = tgt
        for e in ENGINES:
            def fn(eng, tgt=tgt):
                eng.sem_inc(self.ph.h, 1)
                eng.wait_ge(self.ph.h, tgt)
            self.sched.on(e, fn)


def ts(i, n):
    return slice(i * n, (i + 1) * n)



def build_lA(debug=False):
    nc = bass.Bass("TRN2", target_bir_lowering=False)
    x0T = nc.dram_tensor("x0T", [DM, TKV], F32, kind="ExternalInput")
    wqT = nc.dram_tensor("wqT", [DM, DM], F32, kind="ExternalInput")
    wkT = nc.dram_tensor("wkT", [DM, 1024], F32, kind="ExternalInput")
    wvT = nc.dram_tensor("wvT", [DM, 1024], F32, kind="ExternalInput")
    woT = nc.dram_tensor("woT", [DM, DM], F32, kind="ExternalInput")
    rwT = nc.dram_tensor("rwT", [DM, 8], F32, kind="ExternalInput")
    rbrow = nc.dram_tensor("rbrow", [1, 8], F32, kind="ExternalInput")
    AallT = nc.dram_tensor("AallT", [DM, 128], F32, kind="ExternalInput")
    BmF = nc.dram_tensor("BmF", [128, DM], F32, kind="ExternalInput")
    i128b = nc.dram_tensor("i128b", [128, 128], BF16, kind="ExternalInput")
    onescol = nc.dram_tensor("onescol", [128, 1], BF16, kind="ExternalInput")
    onesrow = nc.dram_tensor("onesrow", [1, 512], BF16, kind="ExternalInput")
    negones = nc.dram_tensor("negones", [1, 128], F32, kind="ExternalInput")
    onesrow_f = nc.dram_tensor("onesrow_f", [1, 128], F32, kind="ExternalInput")
    maskSelf = nc.dram_tensor("maskSelf", [128, 128], BF16, kind="ExternalInput")
    maskFar = nc.dram_tensor("maskFar", [128, 128], BF16, kind="ExternalInput")
    fullmaskT = nc.dram_tensor("fullmaskT", [1, 13 * 128], BF16, kind="ExternalInput")
    maskHalo = nc.dram_tensor("maskHalo", [128, 15], F32, kind="ExternalInput")
    cntQ = nc.dram_tensor("cntQ", [1, 1024], F32, kind="ExternalInput")
    cntH = nc.dram_tensor("cntH", [1, 3], F32, kind="ExternalInput")
    fconsts = nc.dram_tensor("fconsts", [128, 4], F32, kind="ExternalInput")
    i8exp = nc.dram_tensor("i8exp", [8, 128], BF16, kind="ExternalInput")

    x1T = nc.dram_tensor("x1T", [DM, TQ], BF16, kind="ExternalOutput")

    xn0_d = nc.dram_tensor("xn0_d", [DM, TKV], BF16)
    qT_d = nc.dram_tensor("qT_d", [DM, TQ], BF16)
    moe_d = nc.dram_tensor("moe_d", [DM, 1152], BF16)

    sched = Sched()
    NT4 = [(0, 512), (512, 512), (1024, 512), (1536, 128)]
    NTQ = [(0, 512), (512, 512), (1024, 3)]

    with contextlib.ExitStack() as ctx:
        E = ctx.enter_context
        P = Prog(nc, ctx, sched)
        sm_x0 = P.ld
        sm_wb = P.ld
        sm_sp = P.st
        sm_qb = P.st
        sm_wo = P.ld
        sm_xr = P.ld
        sm_ms = P.st
        sm_xs = P.st

        i128_sb = E(nc.sbuf_tensor("i128_sb", [128, 128], BF16))
        ones_c = E(nc.sbuf_tensor("ones_c", [128, 1], BF16))
        ones_r = E(nc.sbuf_tensor("ones_r", [1, 512], BF16))
        nones_r = E(nc.sbuf_tensor("nones_r", [1, 128], F32R))
        nones_f = E(nc.sbuf_tensor("nones_f", [1, 128], F32))
        ones_rf = E(nc.sbuf_tensor("ones_rf", [1, 128], F32R))
        mself_sb = E(nc.sbuf_tensor("mself_sb", [128, 128], BF16))
        mfar_sb = E(nc.sbuf_tensor("mfar_sb", [128, 128], BF16))
        fmask_sb = E(nc.sbuf_tensor("fmask_sb", [1, 13 * 128], BF16))
        mhalo_sb = E(nc.sbuf_tensor("mhalo_sb", [128, 15], F32))
        cntq_sb = E(nc.sbuf_tensor("cntq_sb", [1, 1024], F32))
        cnth_sb = E(nc.sbuf_tensor("cnth_sb", [1, 3], F32))
        qmax2 = E(nc.sbuf_tensor("qmax2", [1, 32], F32))
        kmax2 = E(nc.sbuf_tensor("kmax2", [1, 8], F32))
        negC_cols = E(nc.sbuf_tensor("negC_cols", [128, 32], F32))
        negC_row = E(nc.sbuf_tensor("negC_row", [1, 32], F32R))
        encrow = E(nc.sbuf_tensor("encrow", [1, 32], F32))
        fc_sb = E(nc.sbuf_tensor("fc_sb", [128, 4], F32))
        i8e_sb = E(nc.sbuf_tensor("i8e_sb", [8, 128], BF16))
        for i, val in enumerate(FCONSTS):
            nc.const_aps.register(F32, val, fc_sb[:, i:i + 1])

        for dst, src in [(i128_sb, i128b), (ones_c, onescol), (ones_r, onesrow),
                         (nones_r, negones), (ones_rf, onesrow_f),
                         (mself_sb, maskSelf), (mfar_sb, maskFar),
                         (fmask_sb, fullmaskT), (mhalo_sb, maskHalo),
                         (cntq_sb, cntQ), (cnth_sb, cntH), (fc_sb, fconsts),
                         (i8e_sb, i8exp), (nones_f, negones)]:
            P.gdma(dst[:], src[:, :])

        with contextlib.ExitStack() as p1:
            Ep = p1.enter_context
            x0slab = Ep(nc.sbuf_tensor("x0slab", [128, 2 * TKV], BF16))
            sqslab = Ep(nc.sbuf_tensor("sqslab", [128, 2 * TKV], BF16))
            rows = Ep(nc.sbuf_tensor("rows", [1, 4 * TKV], F32))
            rstd_r = Ep(nc.sbuf_tensor("rstd_r", [1, TKV], F32R))
            negmr_r = Ep(nc.sbuf_tensor("negmr_r", [1, TKV], F32R))
            rstd_b = Ep(nc.sbuf_tensor("rstd_b", [128, TKV], F32))
            negmr_b = Ep(nc.sbuf_tensor("negmr_b", [128, TKV], F32))

            mean_r = rows[:, 0:TKV]
            m2_r = rows[:, TKV:2 * TKV]
            var_r = rows[:, 2 * TKV:3 * TKV]
            lnv_r = rows[:, 3 * TKV:4 * TKV]

            with contextlib.ExitStack() as p1s:
                msum_p = p1s.enter_context(nc.psum_tensor("msum_p", [1, TKV], F32))
                sqsum_p = p1s.enter_context(nc.psum_tensor("sqsum_p", [1, TKV], F32))

                mm_hist = {}
                for kc in range(KC):
                    buf = x0slab[:, ts(kc % 2, TKV)]
                    sq = sqslab[:, ts(kc % 2, TKV)]
                    w = [(P.mm, mm_hist[kc - 2])] if kc >= 2 else []
                    v_ld = P.gdma(buf, x0T[ts(kc, 128), :], waits=w, sem=sm_x0)
                    v_sq = P.scalar(lambda s, buf=buf, sq=sq: s.activation(sq, buf, AF.Square),
                                    waits=[(sm_x0, v_ld)] + w)
                    def f(t, kc=kc, buf=buf, sq=sq):
                        for (o, n) in NT4:
                            t.matmul(msum_p[:, o:o + n], ones_c[:], buf[:, o:o + n],
                                     start=(kc == 0), stop=(kc == KC - 1))
                        inst = None
                        for (o, n) in NT4:
                            inst = t.matmul(sqsum_p[:, o:o + n], ones_c[:], sq[:, o:o + n],
                                            start=(kc == 0), stop=(kc == KC - 1))
                        return inst
                    mm_hist[kc] = P.tensor(f, waits=[(sm_x0, v_ld), (P.ac, v_sq)])
                v_stats = P.mm.n

                def f(v):
                    v.tensor_scalar(out=mean_r, in0=msum_p[:], scalar1=1.0 / DM, scalar2=None, op0=ALU.mult)
                    v.drain()
                    v.tensor_tensor(m2_r, mean_r, mean_r, op=ALU.mult)
                    v.drain()
                    return v.scalar_tensor_tensor(out=var_r, in0=sqsum_p[:], scalar=1.0 / DM, in1=m2_r,
                                                  op0=ALU.mult, op1=ALU.subtract)
                v_rows = P.vector(f, waits=[(P.mm, v_stats)])
                def f(s):
                    s.activation(lnv_r, var_r, AF.Ln, bias=1e-5)
                    s.drain()
                    return s.activation(rstd_r[:], lnv_r, AF.Exp, scale=-0.5)
                v_rstd = P.scalar(f, waits=[(P.dv, v_rows)])
                v_negmr = P.vector(
                    lambda v: v.scalar_tensor_tensor(out=negmr_r[:], in0=mean_r, scalar=-1.0,
                                                     in1=rstd_r[:], op0=ALU.mult, op1=ALU.mult),
                    waits=[(P.ac, v_rstd)])

            with contextlib.ExitStack() as p1b:
                bc_p = p1b.enter_context(nc.psum_tensor("bc_p", [128, 512], F32))
                prev_bc = None
                for row, bcast in [(rstd_r, rstd_b), (negmr_r, negmr_b)]:
                    for (o, n) in NT4:
                        wb_ = [(P.dv, prev_bc)] if prev_bc is not None else []
                        v_b = P.tensor(
                            lambda t, row=row, o=o, n=n: t.matmul(
                                bc_p[:, 0:n], ones_rf[:, :], row[:, o:o + n].bitcast(F32R),
                                start=True, stop=True),
                            waits=[(P.dv, v_negmr)] + wb_)
                        prev_bc = P.vector(
                            lambda v, bcast=bcast, o=o, n=n: v.tensor_copy(bcast[:, o:o + n], bc_p[:, 0:n]),
                            waits=[(P.mm, v_b)])
                v_bc = P.dv.n

                dv_hist = {}
                st_hist = {}
                for kc in range(KC):
                    buf = x0slab[:, ts(kc % 2, TKV)]
                    sq = sqslab[:, ts(kc % 2, TKV)]
                    w = [(P.dv, dv_hist[kc - 2]), (sm_sp, st_hist[kc - 2])] if kc >= 2 else [(P.mm, v_stats)]
                    v_ld = P.gdma(buf, x0T[ts(kc, 128), :], waits=w, sem=sm_x0)
                    def f(v, buf=buf, sq=sq):
                        v.tensor_tensor(sq, buf, rstd_b[:], op=ALU.mult)
                        v.drain()
                        return v.tensor_tensor(buf, sq, negmr_b[:], op=ALU.add)
                    dv_hist[kc] = P.vector(f, waits=[(sm_x0, v_ld), (P.dv, v_bc)])
                    st_hist[kc] = P.sdma(xn0_d[ts(kc, 128), :], buf, waits=[(P.dv, dv_hist[kc])],
                                         sem=sm_sp)

        P.barrier()

        kT_sb = E(nc.sbuf_tensor("kT_sb", [128, 8 * TKV], BF16))
        v_sb = E(nc.sbuf_tensor("v_sb", [128, 13 * 1024], BF16))

        with contextlib.ExitStack() as p2:
            Ep = p2.enter_context
            xn0_sb = Ep(nc.sbuf_tensor("xn0_sb", [128, KC * TKV], BF16))
            oslab = Ep(nc.sbuf_tensor("oslab", [128, TKV], BF16))
            sqs = Ep(nc.sbuf_tensor("sqs", [128, TKV], BF16))

            v_sp1 = sm_sp.n
            for kc in range(KC):
                P.sdma(xn0_sb[:, ts(kc, TKV)], xn0_d[ts(kc, 128), :], waits=[(sm_sp, v_sp1)])
            v_xn0 = P.st.n

            p2w = contextlib.ExitStack()
            wbuf = p2w.enter_context(nc.sbuf_tensor("wbuf", [128, 2 * KC * 128], BF16))
            out_p = p2w.enter_context(nc.psum_tensor("out_p", [128, TKV], F32))
            tp_p = p2w.enter_context(nc.psum_tensor("tp_p", [128, 128], BF16))
            kweight_hist = {}
            prev_red = None
            for hk in range(8):
                wb = wbuf[:, ts(hk % 2, KC * 128)]
                w = [(P.mm, kweight_hist[hk - 2])] if hk >= 2 else []
                v_w = P.gdma(wb.rearrange("p (c n) -> p c n", n=128),
                             wkT[:, ts(hk, 128)].rearrange("(c p) n -> p c n", p=128), waits=w,
                             sem=sm_wb)
                waits = [(sm_wb, v_w), (P.st, v_xn0)]
                if prev_red is not None:
                    waits.append((P.dv, prev_red))
                def f(t, wb=wb):
                    inst = None
                    for kc in range(KC):
                        for (o, n) in NT4:
                            inst = t.matmul(out_p[:, o:o + n], wb[:, ts(kc, 128)],
                                            xn0_sb[:, kc * TKV + o: kc * TKV + o + n],
                                            start=(kc == 0), stop=(kc == KC - 1))
                    return inst
                kweight_hist[hk] = v_k = P.tensor(f, waits=waits)
                v_ev = P.vector(lambda v, hk=hk: v.tensor_copy(kT_sb[:, ts(hk, TKV)], out_p[:]),
                                waits=[(P.mm, v_k)])
                v_sq = P.scalar(lambda s, hk=hk: s.activation(sqs[:], kT_sb[:, ts(hk, TKV)], AF.Square),
                                waits=[(P.dv, v_ev)])
                def f(t):
                    inst = None
                    for (o, n) in NT4:
                        inst = t.matmul(out_p[0:1, o:o + n], ones_c[:], sqs[:, o:o + n],
                                        start=True, stop=True)
                    return inst
                v_sqm = P.tensor(f, waits=[(P.ac, v_sq), (P.dv, v_ev)])
                prev_red = P.vector(
                    lambda v, hk=hk: v.reduce_max(kmax2[:, hk:hk + 1], out_p[0:1, :], axis=AXX),
                    waits=[(P.mm, v_sqm)])

            vweight_hist = {}
            prev_tp = None
            for hv in range(8):
                wb = wbuf[:, ts(hv % 2, KC * 128)]
                w = [(P.mm, vweight_hist[hv - 2])] if hv >= 2 else [(P.mm, kweight_hist[7])]
                v_w = P.gdma(wb.rearrange("p (c n) -> p c n", n=128),
                             wvT[:, ts(hv, 128)].rearrange("(c p) n -> p c n", p=128), waits=w,
                             sem=sm_wb)
                waits = [(sm_wb, v_w)]
                if prev_red is not None:
                    waits.append((P.dv, prev_red))
                    prev_red = None
                def f(t, wb=wb):
                    inst = None
                    for kc in range(KC):
                        for (o, n) in NT4:
                            inst = t.matmul(out_p[:, o:o + n], wb[:, ts(kc, 128)],
                                            xn0_sb[:, kc * TKV + o: kc * TKV + o + n],
                                            start=(kc == 0), stop=(kc == KC - 1))
                    return inst
                vweight_hist[hv] = v_m = P.tensor(f, waits=waits)
                v_ev = P.vector(lambda v: v.tensor_copy(oslab[:], out_p[:]), waits=[(P.mm, v_m)])
                for tc in range(13):
                    wtp = [(P.dv, prev_tp)] if prev_tp is not None else []
                    v_t = P.tensor(
                        lambda t, tc=tc: t.transpose(tp_p[:], oslab[:, ts(tc, 128)], i128_sb[:]),
                        waits=[(P.dv, v_ev)] + wtp)
                    prev_tp = P.vector(
                        lambda v, hv=hv, tc=tc: v.tensor_copy(
                            v_sb[:, tc * 1024 + hv * 128: tc * 1024 + (hv + 1) * 128], tp_p[:]),
                        waits=[(P.mm, v_t)])

            qweight_hist = {}
            prev_evq = None
            for hq in range(KC):
                wb = wbuf[:, ts(hq % 2, KC * 128)]
                w = [(P.mm, qweight_hist[hq - 2])] if hq >= 2 else [(P.mm, vweight_hist[7])]
                v_w = P.gdma(wb.rearrange("p (c n) -> p c n", n=128),
                             wqT[:, ts(hq, 128)].rearrange("(c p) n -> p c n", p=128), waits=w,
                             sem=sm_wb)
                waits = [(sm_wb, v_w)]
                if prev_tp is not None:
                    waits.append((P.dv, prev_tp))
                    prev_tp = None
                if prev_evq is not None:
                    waits.append((P.dv, prev_evq))
                def f(t, wb=wb):
                    inst = None
                    for kc in range(KC):
                        for (o, n) in NTQ:
                            inst = t.matmul(out_p[:, o:o + n], wb[:, ts(kc, 128)],
                                            xn0_sb[:, kc * TKV + 637 + o: kc * TKV + 637 + o + n],
                                            start=(kc == 0), stop=(kc == KC - 1))
                    return inst
                qweight_hist[hq] = v_m = P.tensor(f, waits=waits)
                v_ev = P.vector(lambda v: v.tensor_scalar(out=oslab[:, 0:TQ], in0=out_p[:, 0:TQ],
                                                          scalar1=1.0 / math.sqrt(128.0),
                                                          scalar2=None, op0=ALU.mult),
                                waits=[(P.mm, v_m)])
                P.sdma(qT_d[ts(hq, 128), :], oslab[:, 0:TQ], waits=[(P.dv, v_ev)])
                v_sq = P.scalar(lambda s: s.activation(sqs[:, 0:TQ], oslab[:, 0:TQ], AF.Square),
                                waits=[(P.dv, v_ev)])
                def f(t):
                    inst = None
                    for (o, n) in NTQ:
                        inst = t.matmul(out_p[0:1, o:o + n], ones_c[:], sqs[:, o:o + n],
                                        start=True, stop=True)
                    return inst
                v_sqm = P.tensor(f, waits=[(P.ac, v_sq)])
                prev_evq = P.vector(
                    lambda v, hq=hq: v.reduce_max(qmax2[:, hq:hq + 1], out_p[0:1, 0:TQ], axis=AXX),
                    waits=[(P.mm, v_sqm)])

            p2w.close()
            P.barrier()

            with contextlib.ExitStack() as p3:
                E3 = p3.enter_context
                rw_sb = E3(nc.sbuf_tensor("rw_sb", [128, KC * 8], BF16))
                rb_sb = E3(nc.sbuf_tensor("rb_sb", [1, 8], BF16))
                aall_sb = E3(nc.sbuf_tensor("aall_sb", [128, KC * 128], BF16))
                bmf_sb = E3(nc.sbuf_tensor("bmf_sb", [128, DM], BF16))
                h_sb = E3(nc.sbuf_tensor("h_sb", [128, 1152], BF16))
                hw_sb = E3(nc.sbuf_tensor("hw_sb", [128, 1152], BF16))
                wtT_sb = E3(nc.sbuf_tensor("wtT_sb", [8, 128], BF16))
                w_sb = E3(nc.sbuf_tensor("w_sb", [128, 8], BF16))
                e8 = E3(nc.sbuf_tensor("e8", [128, 8], F32))
                eqf = E3(nc.sbuf_tensor("eqf", [128, 8], F32))
                s8f = E3(nc.sbuf_tensor("s8f", [128, 8], F32))
                se8 = E3(nc.sbuf_tensor("se8", [128, 8], F32))
                s8 = E3(nc.sbuf_tensor("s8", [128, 8], BF16))
                sc1 = E3(nc.sbuf_tensor("sc1", [128, 8], F32))
                logit_p = E3(nc.psum_tensor("logit_p", [128, 8], F32))
                wdbg_sb = E3(nc.sbuf_tensor("wdbg_sb", [128, 72], BF16))
                wt_p = E3(nc.psum_tensor("wt_p", [8, 128], BF16))
                wexp_p = E3(nc.psum_tensor("wexp_p", [128, 128], F32))
                h_p = E3(nc.psum_tensor("h_p", [128, 128], F32))
                moe_p = E3(nc.psum_tensor("moe_p", [128, 1152], F32))

                P.gdma(rw_sb[:].rearrange("p (c n) -> p c n", n=8),
                       rwT[:, :].rearrange("(c p) n -> p c n", p=128))
                P.gdma(rb_sb[:], rbrow[:, :])
                P.gdma(aall_sb[:].rearrange("p (c n) -> p c n", n=128),
                       AallT[:, :].rearrange("(c p) n -> p c n", p=128))
                P.gdma(bmf_sb[:], BmF[:, :])
                v_wmoe = P.ld.n

                m1c, nm1c, m2c, ntc, tltc, denc, recc = [sc1[:, i:i + 1] for i in range(7)]
                prev_lg = None
                prev_wexp = None
                prev_h = None
                for tt in range(9):
                    col0 = 512 + tt * 128
                    waits = [(P.ld, v_wmoe), (P.st, v_xn0)]
                    if prev_lg is not None:
                        waits.append((P.dv, prev_lg))
                    def f(t, col0=col0):
                        for kc in range(KC):
                            t.matmul(logit_p[:], xn0_sb[:, kc * TKV + col0: kc * TKV + col0 + 128],
                                     rw_sb[:, ts(kc, 8)], start=(kc == 0), stop=False)
                        return t.matmul(logit_p[:], ones_r[:, 0:128], rb_sb[:],
                                        start=False, stop=True)
                    v_lg = P.tensor(f, waits=waits)
                    def f(v):
                        v.reduce_max(m1c, logit_p[:], axis=AXX)
                        v.reduce_max(nm1c, logit_p[:], axis=AXX, negate=True)
                        v.drain()
                        v.tensor_scalar(out=eqf[:], in0=logit_p[:], scalar1=m1c, scalar2=None,
                                        op0=ALU.is_equal)
                        v.drain()
                        v.reduce_sum(ntc, eqf[:], axis=AXX)
                        v.scalar_tensor_tensor(out=e8[:], in0=eqf[:], scalar=NEG, in1=logit_p[:],
                                               op0=ALU.mult, op1=ALU.add)
                        v.drain()
                        v.reduce_max(m2c, e8[:], axis=AXX)
                        v.tensor_scalar(out=tltc, in0=ntc, scalar1=2.0, scalar2=None,
                                        op0=ALU.is_lt)
                        v.drain()
                        v.tensor_scalar(out=s8f[:], in0=logit_p[:], scalar1=m2c, scalar2=None,
                                        op0=ALU.is_ge)
                        v.drain()
                        return v.scalar_tensor_tensor(out=s8f[:], in0=s8f[:], scalar=tltc,
                                                      in1=eqf[:], op0=ALU.mult, op1=ALU.max)
                    v_tw = P.vector(f, waits=[(P.mm, v_lg)])
                    v_exp8 = P.scalar(lambda s: s.activation(e8[:], logit_p[:], AF.Exp, bias=nm1c),
                                      waits=[(P.dv, v_tw)])
                    def f(v, tt=tt):
                        v.tensor_tensor(se8[:], s8f[:], e8[:], op=ALU.mult)
                        v.drain()
                        v.reduce_sum(denc, se8[:], axis=AXX)
                        v.drain()
                        v.tensor_scalar(out=denc, in0=denc, scalar1=2.0, scalar2=None, op0=ALU.mult)
                        v.drain()
                        v.reciprocal(recc, denc)
                        v.drain()
                        v.scalar_tensor_tensor(out=w_sb[:], in0=e8[:], scalar=recc,
                                               in1=s8f[:], op0=ALU.mult, op1=ALU.mult)
                        v.drain()
                        return v.tensor_copy(wdbg_sb[:, ts(tt, 8)], w_sb[:])
                    prev_lg = v_wfin = P.vector(f, waits=[(P.ac, v_exp8)])
                    wtw = [(P.dv, prev_wexp)] if prev_wexp is not None else []
                    v_wt = P.tensor(lambda t: t.transpose(wt_p[:], w_sb[:], i128_sb[:]),
                                    waits=[(P.dv, v_wfin)] + wtw)
                    v_wtev = P.vector(lambda v: v.tensor_copy(wtT_sb[:], wt_p[:]),
                                      waits=[(P.mm, v_wt)])
                    v_we = P.tensor(
                        lambda t: t.matmul(wexp_p[:], i8e_sb[:], wtT_sb[:],
                                           start=True, stop=True),
                        waits=[(P.dv, v_wtev)] + wtw)
                    wh = [(P.dv, prev_h)] if prev_h is not None else []
                    def f(t, col0=col0):
                        inst = None
                        for kc in range(KC):
                            inst = t.matmul(h_p[:], aall_sb[:, ts(kc, 128)],
                                            xn0_sb[:, kc * TKV + col0: kc * TKV + col0 + 128],
                                            start=(kc == 0), stop=(kc == KC - 1))
                        return inst
                    v_h = P.tensor(f, waits=[(P.ld, v_wmoe)] + wh)
                    def f(v, tt=tt):
                        v.tensor_copy(h_sb[:, ts(tt, 128)], h_p[:])
                        v.drain()
                        return v.tensor_tensor(hw_sb[:, ts(tt, 128)], h_sb[:, ts(tt, 128)],
                                               wexp_p[:], op=ALU.mult)
                    prev_h = prev_wexp = P.vector(f, waits=[(P.mm, max(v_h, v_we))])
                v_hwall = P.dv.n

                NT9 = [(0, 512), (512, 512), (1024, 128)]
                prev_ev = None
                moe_st = {}
                if debug:
                    wdbg_o = nc.dram_tensor("wdbg_o", [128, 72], BF16, kind="ExternalOutput")
                    hdbg_o = nc.dram_tensor("hdbg_o", [128, 1152], BF16, kind="ExternalOutput")
                    hwdbg_o = nc.dram_tensor("hwdbg_o", [128, 1152], BF16, kind="ExternalOutput")
                    P.sdma(wdbg_o[:, :], wdbg_sb[:], waits=[(P.dv, v_hwall)])
                    P.sdma(hdbg_o[:, :], h_sb[:], waits=[(P.dv, v_hwall)])
                    P.sdma(hwdbg_o[:, :], hw_sb[:], waits=[(P.dv, v_hwall)])
                for dc in range(KC):
                    waits = [(P.dv, v_hwall)]
                    if prev_ev is not None:
                        waits.append((P.dv, prev_ev))
                    def f(t, dc=dc):
                        inst = None
                        for (o, n) in NT9:
                            inst = t.matmul(moe_p[:, o:o + n], bmf_sb[:, ts(dc, 128)],
                                            hw_sb[:, o:o + n], start=True, stop=True)
                        return inst
                    v_m = P.tensor(f, waits=waits)
                    wst = [(sm_sp, moe_st[dc - 1])] if dc >= 1 else []
                    prev_ev = P.vector(lambda v: v.tensor_copy(oslab[:, 0:1152], moe_p[:]),
                                       waits=[(P.mm, v_m)] + wst)
                    moe_st[dc] = P.sdma(moe_d[ts(dc, 128), :], oslab[:, 0:1152],
                                        waits=[(P.dv, prev_ev)], sem=sm_sp)

            P.barrier()

        o_sb = E(nc.sbuf_tensor("o_sb", [128, KC * TQ], BF16))

        with contextlib.ExitStack() as p4:
            E4 = p4.enter_context
            qbuf = E4(nc.sbuf_tensor("qbuf", [128, 2 * 512], BF16))
            qhalo = E4(nc.sbuf_tensor("qhalo", [128, KC * 3], BF16))
            pT_sb = E4(nc.sbuf_tensor("pT_sb", [128, 2 * 4096], BF16))
            ph_sb = E4(nc.sbuf_tensor("ph_sb", [128, 15], BF16))
            urow = E4(nc.sbuf_tensor("urow", [1, 512], F32))
            rrow = E4(nc.sbuf_tensor("rrow", [1, 512], F32R))
            rrow3 = E4(nc.sbuf_tensor("rrow3", [1, 4], F32))
            recb_sb = E4(nc.sbuf_tensor("recb_sb", [128, 512], F32))
            s_pA = E4(nc.psum_tensor("s_pA", [128, 1024], F32))
            s_pB = E4(nc.psum_tensor("s_pB", [128, 1024], F32))
            den_p = E4(nc.psum_tensor("den_p", [1, 512], F32))
            o_p = E4(nc.psum_tensor("o_p", [128, 512], F32))
            rb_p = E4(nc.psum_tensor("rb_p", [128, 512], F32))

            v_c0 = P.vector(
                lambda v: v.tensor_tensor(qmax2[:], qmax2[:],
                                          kmax2[:].unsqueeze(2).broadcast_to([1, 8, 4]),
                                          op=ALU.mult))
            v_c1 = P.scalar(lambda s: s.activation(encrow[:], qmax2[:], AF.Sqrt),
                            waits=[(P.dv, v_c0)])
            v_c2 = P.vector(
                lambda v: v.tensor_scalar(out=negC_row[:], in0=encrow[:],
                                          scalar1=-1.0 / math.sqrt(128.0), scalar2=None,
                                          op0=ALU.mult),
                waits=[(P.ac, v_c1)])
            v_c3 = P.tensor(
                lambda t: t.matmul(rb_p[:, 0:32], ones_rf[:, :], negC_row[:],
                                   start=True, stop=True),
                waits=[(P.dv, v_c2)])
            v_c4 = P.vector(lambda v: v.tensor_copy(negC_cols[:], rb_p[:, 0:32]),
                            waits=[(P.mm, v_c3)])
            v_c5 = P.scalar(lambda s: s.activation(encrow[:], negC_row[:], AF.Exp),
                            waits=[(P.dv, v_c2)])

            v_sp2 = sm_sp.n
            P.sdma(qhalo[:].rearrange("p (c n) -> p c n", n=3),
                   qT_d[:, 0:3].rearrange("(c p) n -> p c n", p=128), waits=[(sm_sp, v_sp2)],
                   sem=sm_qb)
            v_qh = sm_qb.n

            qpv = {}
            prevA = prevB = None
            prev_den = prev_o = prev_rb = None
            for h in range(32):
                hk = h // 4
                ncol = negC_cols[:, h:h + 1]
                waits = [(P.st, v_qh), (P.dv, v_c4)]
                if prevB is not None:
                    waits.append((P.ac, prevB))
                def f(t, hk=hk, h=h):
                    inst = None
                    for kb in range(5):
                        inst = t.matmul(s_pB[:, kb * 3:(kb + 1) * 3],
                                        kT_sb[:, hk * TKV + kb * 128: hk * TKV + (kb + 1) * 128],
                                        qhalo[:, ts(h, 3)], start=True, stop=True)
                    return inst
                v_hm = P.tensor(f, waits=waits)
                v_hmm = P.vector(
                    lambda v: v.tensor_tensor(s_pB[:, 0:15], s_pB[:, 0:15], mhalo_sb[:], op=ALU.add),
                    waits=[(P.mm, v_hm)])
                prevB = v_hx = P.scalar(
                    lambda s, ncol=ncol: s.activation(ph_sb[:], s_pB[:, 0:15], AF.Exp, bias=ncol),
                    waits=[(P.dv, v_hmm), (P.ac, v_c5)])
                wd = [(P.dv, prev_den)] if prev_den is not None else []
                def f(t):
                    inst = None
                    for kb in range(5):
                        inst = t.matmul(den_p[:, 0:3], ones_c[:], ph_sb[:, kb * 3:(kb + 1) * 3],
                                        start=(kb == 0), stop=(kb == 4))
                    return inst
                v_hd = P.tensor(f, waits=[(P.ac, v_hx)] + wd)
                def f(v, h=h):
                    v.scalar_tensor_tensor(out=urow[:, 0:3], in0=cnth_sb[:], scalar=encrow[:, h:h + 1],
                                           in1=den_p[:, 0:3], op0=ALU.mult, op1=ALU.subtract)
                    v.drain()
                    v.tensor_scalar(out=urow[:, 0:3], in0=urow[:, 0:3], scalar1=-1e-30, scalar2=None,
                                    op0=ALU.min)
                    v.drain()
                    return v.reciprocal(rrow3[:, 0:3], urow[:, 0:3])
                prev_den = v_hu = P.vector(f, waits=[(P.mm, v_hd)])
                wrb = [(P.dv, prev_rb)] if prev_rb is not None else [(P.dv, v_c4)]
                v_hrb = P.tensor(
                    lambda t: t.matmul(rb_p[:, 0:3], nones_f[:, :], rrow3[:, 0:3], start=True, stop=True),
                    waits=[(P.dv, v_hu)] + wrb)
                prev_rb = v_hrbe = P.vector(lambda v: v.tensor_copy(recb_sb[:, 0:3], rb_p[:, 0:3]),
                                            waits=[(P.mm, v_hrb)])
                wo_ = [(P.dv, prev_o)] if prev_o is not None else []
                def f(t, hk=hk):
                    inst = None
                    for kb in range(5):
                        inst = t.matmul(o_p[:, 0:3],
                                        v_sb[:, kb * 1024 + hk * 128: kb * 1024 + (hk + 1) * 128],
                                        ph_sb[:, kb * 3:(kb + 1) * 3], start=(kb == 0), stop=(kb == 4))
                    return inst
                v_ho = P.tensor(f, waits=[(P.ac, v_hx)] + wo_)
                prev_o = P.vector(
                    lambda v, h=h: v.tensor_tensor(o_sb[:, h * TQ: h * TQ + 3], o_p[:, 0:3],
                                                   recb_sb[:, 0:3], op=ALU.mult),
                    waits=[(P.mm, v_ho), (P.dv, v_hrbe)])

                for qt in range(2):
                    qb_ = qbuf[:, ts(qt, 512)]
                    wq_ = [(P.mm, qpv[(h - 1, qt)])] if (h - 1, qt) in qpv else []
                    v_q = P.sdma(qb_, qT_d[ts(h, 128), 3 + qt * 512: 3 + qt * 512 + 512],
                                 waits=wq_ + [(sm_sp, v_sp2)], sem=sm_qb)
                    kbs = list(range(4 * qt + 1, 4 * qt + 9))
                    spans = []
                    for kb in kbs:
                        lo = max(kb - 5, 4 * qt)
                        hi = min(kb - 1, 4 * qt + 3)
                        spans.append((kb, lo * 128 - qt * 512, (hi - lo + 1) * 128))
                    pslab = pT_sb[:, ts(qt, 4096)]
                    exps = []
                    for half in range(4):
                        sp = s_pA if half % 2 == 0 else s_pB
                        prevexp = prevA if half % 2 == 0 else prevB
                        wx = [(P.ac, prevexp)] if prevexp is not None else []
                        sub = spans[half * 2:(half + 1) * 2]
                        def f(t, sub=sub, sp=sp, qt=qt, hk=hk, qb_=qb_):
                            inst = None
                            for j, (kb, qo, wdt) in enumerate(sub):
                                mms = [("s", qo, wdt)]
                                qbself = kb - 5
                                if 4 * qt <= qbself <= 4 * qt + 3:
                                    mms.append(("self", (qbself - 4 * qt) * 128, 128))
                                qbfar = kb - 1
                                if 4 * qt <= qbfar <= 4 * qt + 3:
                                    mms.append(("far", (qbfar - 4 * qt) * 128, 128))
                                if kb <= 4:
                                    mms.append(("full", qo, wdt))
                                for idx, (kind, o_, n_) in enumerate(mms):
                                    st_ = (idx == 0)
                                    sp_ = (idx == len(mms) - 1)
                                    dst = sp[:, j * 512 + o_: j * 512 + o_ + n_]
                                    if kind == "s":
                                        inst = t.matmul(dst, kT_sb[:, hk * TKV + kb * 128: hk * TKV + (kb + 1) * 128],
                                                        qb_[:, o_:o_ + n_], start=st_, stop=sp_)
                                    elif kind == "self":
                                        inst = t.matmul(dst, i128_sb[:], mself_sb[:], start=st_, stop=sp_)
                                    elif kind == "far":
                                        inst = t.matmul(dst, i128_sb[:], mfar_sb[:], start=st_, stop=sp_)
                                    else:
                                        inst = t.matmul(dst, fmask_sb[:, ts(kb, 128)],
                                                        ones_r[:, 0:n_], start=st_, stop=sp_)
                            return inst
                        wpv = [(P.mm, qpv[(h - 1, qt)])] if (h - 1, qt) in qpv else []
                        v_sm = P.tensor(f, waits=[(P.st, v_q)] + wx + wpv)
                        v_x = P.scalar(
                            lambda s, half=half, sp=sp, ncol=ncol, pslab=pslab:
                                s.activation(pslab[:, half * 1024:(half + 1) * 1024], sp[:],
                                             AF.Exp, bias=ncol),
                            waits=[(P.mm, v_sm), (P.ac, v_c5), (P.dv, v_c4)] + wpv)
                        exps.append(v_x)
                        if half % 2 == 0:
                            prevA = v_x
                        else:
                            prevB = v_x
                    wd = [(P.dv, prev_den)] if prev_den is not None else []
                    def f(t, spans=spans, pslab=pslab):
                        inst = None
                        for j, (kb, qo, wdt) in enumerate(spans):
                            inst = t.matmul(den_p[:, qo:qo + wdt], ones_c[:],
                                            pslab[:, j * 512 + qo: j * 512 + qo + wdt],
                                            start=(j == 0), stop=(j == 7))
                        return inst
                    v_dm = P.tensor(f, waits=[(P.ac, exps[-1])] + wd)
                    def f(v, qt=qt, h=h):
                        v.scalar_tensor_tensor(out=urow[:], in0=cntq_sb[:, ts(qt, 512)],
                                               scalar=encrow[:, h:h + 1], in1=den_p[:],
                                               op0=ALU.mult, op1=ALU.subtract)
                        v.drain()
                        v.tensor_scalar(out=urow[:], in0=urow[:], scalar1=-1e-30, scalar2=None,
                                        op0=ALU.min)
                        v.drain()
                        return v.reciprocal(rrow[:], urow[:])
                    prev_den = v_u = P.vector(f, waits=[(P.mm, v_dm)])
                    wrb = [(P.dv, prev_rb)] if prev_rb is not None else []
                    v_rbm = P.tensor(
                        lambda t: t.matmul(rb_p[:], nones_r[:, :], rrow[:], start=True, stop=True),
                        waits=[(P.dv, v_u)] + wrb)
                    prev_rb = v_rbe = P.vector(lambda v: v.tensor_copy(recb_sb[:], rb_p[:]),
                                               waits=[(P.mm, v_rbm)])
                    wo_ = [(P.dv, prev_o)] if prev_o is not None else []
                    def f(t, spans=spans, pslab=pslab, hk=hk):
                        inst = None
                        for j, (kb, qo, wdt) in enumerate(spans):
                            inst = t.matmul(o_p[:, qo:qo + wdt],
                                            v_sb[:, kb * 1024 + hk * 128: kb * 1024 + (hk + 1) * 128],
                                            pslab[:, j * 512 + qo: j * 512 + qo + wdt],
                                            start=(j == 0), stop=(j == 7))
                        return inst
                    v_pv = P.tensor(f, waits=[(P.ac, exps[-1])] + wo_)
                    qpv[(h, qt)] = v_pv
                    prev_o = P.vector(
                        lambda v, h=h, qt=qt: v.tensor_tensor(
                            o_sb[:, h * TQ + 3 + qt * 512: h * TQ + 3 + qt * 512 + 512],
                            o_p[:], recb_sb[:], op=ALU.mult),
                        waits=[(P.mm, v_pv), (P.dv, v_rbe)])

        if debug:
            o_o = nc.dram_tensor("o_o", [128, KC * TQ], BF16, kind="ExternalOutput")
            P.sdma(o_o[:, :], o_sb[:])
        P.barrier()

        v_sp3 = P.st.n
        with contextlib.ExitStack() as p5:
            E5 = p5.enter_context
            wobuf = E5(nc.sbuf_tensor("wobuf", [128, 2 * KC * 128], BF16))
            x0re = E5(nc.sbuf_tensor("x0re", [128, 2 * TQ], BF16))
            moesl = E5(nc.sbuf_tensor("moesl", [128, 2 * TQ], BF16))
            x1sl = E5(nc.sbuf_tensor("x1sl", [128, 2 * TQ], BF16))
            wo_p = E5(nc.psum_tensor("wo_p", [128, TQ], F32))

            hist = {}
            for dc in range(KC):
                wb = wobuf[:, ts(dc % 2, KC * 128)]
                x0b = x0re[:, ts(dc % 2, TQ)]
                moeb = moesl[:, ts(dc % 2, TQ)]
                x1b = x1sl[:, ts(dc % 2, TQ)]
                wv_ = [(P.mm, hist[dc - 2])] if dc >= 2 else []
                wv2 = [(P.dv, hist[('e', dc - 2)])] if dc >= 2 else []
                wv3 = [(sm_xs, hist[('s', dc - 2)])] if dc >= 2 else []
                v_w = P.gdma(wb.rearrange("p (c n) -> p c n", n=128),
                             woT[:, ts(dc, 128)].rearrange("(c p) n -> p c n", p=128), waits=wv_,
                             sem=sm_wo)
                v_x0 = P.gdma(x0b, x0T[ts(dc, 128), 637:1664], waits=wv2, sem=sm_xr)
                v_moe = P.sdma(moeb, moe_d[ts(dc, 128), 125:1152], waits=wv2 + [(sm_sp, v_sp3)],
                               sem=sm_ms)
                wprev = [(P.dv, hist[('e', dc - 1)])] if dc >= 1 else []
                def f(t, wb=wb):
                    inst = None
                    for hh in range(KC):
                        for (o, n) in NTQ:
                            inst = t.matmul(wo_p[:, o:o + n], wb[:, ts(hh, 128)],
                                            o_sb[:, hh * TQ + o: hh * TQ + o + n],
                                            start=(hh == 0), stop=(hh == KC - 1))
                    return inst
                hist[dc] = v_m = P.tensor(f, waits=[(P.ld, v_w)] + wprev)
                def f(v, x0b=x0b, moeb=moeb, x1b=x1b):
                    v.tensor_tensor(x1b, wo_p[:], x0b, op=ALU.add)
                    v.drain()
                    return v.tensor_tensor(x1b, x1b, moeb, op=ALU.add)
                hist[('e', dc)] = v_e = P.vector(
                    f, waits=[(P.mm, v_m), (P.ld, v_x0), (P.st, v_moe)] + wv3)
                hist[('s', dc)] = P.sdma(x1T[ts(dc, 128), :], x1b, waits=[(P.dv, v_e)])

            if debug:
                xn0_o = nc.dram_tensor("xn0_o", [DM, TKV], BF16, kind="ExternalOutput")
                qT_o = nc.dram_tensor("qT_o", [DM, TQ], BF16, kind="ExternalOutput")
                moe_o = nc.dram_tensor("moe_o", [DM, 1152], BF16, kind="ExternalOutput")
                P.sdma(xn0_o[:, :], xn0_d[:, :])
                P.sdma(qT_o[:, :], qT_d[:, :])
                P.sdma(moe_o[:, :], moe_d[:, :])
            vfin = sm_xs.n
            vfin2 = P.st.n
            def fend(s, vfin=vfin, vfin2=vfin2):
                s.wait_ge(sm_xs.h, vfin)
                s.wait_ge(P.st.h, vfin2)
            sched.on("sync", fend)

        with nc.allow_low_precision("bf16 operands / f32r broadcasts by design"):
            sched.emit(nc)
    return nc



def _l0_host_consts():
    i = np.arange(128)[:, None]
    j = np.arange(128)[None, :]
    mself = np.where(i <= j, 0.0, NEG).astype(np.float32)
    mfar = np.where(i > j, 0.0, NEG).astype(np.float32)
    mhalo = np.zeros((128, 5, 3), np.float32)
    for kb in range(5):
        for kj in range(128):
            g = -640 + kb * 128 + kj
            for ii in range(3):
                p_ = ii - 3
                ok = (g <= p_) and (p_ - g < 512)
                mhalo[kj, kb, ii] = 0.0 if ok else NEG
    return {
        "i128b": np.eye(128, dtype=BF),
        "onescol": np.ones((128, 1), BF),
        "onesrow": np.ones((1, 512), BF),
        "negones": -np.ones((1, 128), np.float32),
        "onesrow_f": np.ones((1, 128), np.float32),
        "maskSelf": mself.astype(BF),
        "maskFar": mfar.astype(BF),
        "maskHalo": mhalo.reshape(128, 15),
        "fconsts": np.tile(np.array([FCONSTS], np.float32), (128, 1)),
        "i8exp": np.repeat(np.eye(8, dtype=BF), 16, axis=1),
    }


def _l0_percore(core, x):
    b, cc = core // 4, core % 4
    Q0 = cc * 1024
    lo = Q0 - 640
    pad = max(0, -lo)
    sl = x[b, max(0, lo):Q0 + 1024]
    x0s = np.concatenate([np.zeros((pad, DM), np.float32), sl], 0)
    fm = np.zeros((13, 128), np.float32)
    cntq = np.zeros(1024, np.float32)
    cnth = np.zeros(3, np.float32)
    if cc == 0:
        fm[:5] = NEG
        for ii in range(3):
            g = np.arange(-640, 0)
            p_ = ii - 3
            valid = (g <= p_) & (p_ - g < 512)
            cnth[ii] = ((g < 0) & valid).sum()
    return {
        "x0T": np.ascontiguousarray(x0s.T),
        "fullmaskT": np.repeat(fm, 1, 0).reshape(1, 13 * 128).astype(BF),
        "cntQ": cntq.reshape(1, 1024),
        "cntH": cnth.reshape(1, 3),
    }


def host_inputs_lA(inp):
    consts = _l0_host_consts()
    shared = {
        "wqT": np.ascontiguousarray(inp["L0_attn_wq"].T),
        "wkT": np.ascontiguousarray(inp["L0_attn_wk"].T),
        "wvT": np.ascontiguousarray(inp["L0_attn_wv"].T),
        "woT": np.ascontiguousarray(inp["L0_attn_wo"].T),
        "rwT": np.ascontiguousarray(inp["L0_ffn_router_w"].T),
        "rbrow": inp["L0_ffn_router_b"].reshape(1, 8),
        "AallT": np.ascontiguousarray(inp["L0_ffn_A"].reshape(128, DM).T),
        "BmF": np.ascontiguousarray(inp["L0_ffn_Bm"].transpose(0, 2, 1).reshape(128, DM)),
    }
    maps = []
    for core in range(8):
        m = dict(consts)
        m.update(shared)
        m.update(_l0_percore(core, inp["L0_attn_x"]))
        maps.append(m)
    return maps




def build_lB():
    nc = bass.Bass("TRN2", target_bir_lowering=False)
    x1T = nc.dram_tensor("x1T", [DM, TQ], BF16, kind="ExternalInput")
    wipT = nc.dram_tensor("wipT", [DM, 18560], F32, kind="ExternalInput")
    convW = nc.dram_tensor("convW", [128, 80 * 4], F32, kind="ExternalInput")
    convB = nc.dram_tensor("convB", [128, 80], F32, kind="ExternalInput")
    dtBias = nc.dram_tensor("dtBias", [128, 1], F32, kind="ExternalInput")
    Alog = nc.dram_tensor("Alog", [128, 1], F32, kind="ExternalInput")
    onescol = nc.dram_tensor("onescol", [128, 1], BF16, kind="ExternalInput")
    onesrow_f = nc.dram_tensor("onesrow_f", [1, 128], F32, kind="ExternalInput")
    fconsts = nc.dram_tensor("fconsts", [128, 4], F32, kind="ExternalInput")

    xn1T = nc.dram_tensor("xn1T", [DM, TQ], BF16, kind="ExternalOutput")
    gateT = nc.dram_tensor("gateT", [8192, 1024], BF16, kind="ExternalOutput")
    xT = nc.dram_tensor("xT", [8192, 1024], BF16, kind="ExternalOutput")
    BT = nc.dram_tensor("BT", [1024, 1024], BF16, kind="ExternalOutput")
    CT = nc.dram_tensor("CT", [1024, 1024], BF16, kind="ExternalOutput")
    dtF = nc.dram_tensor("dtF", [128, 1024], F32, kind="ExternalOutput")
    dtB = nc.dram_tensor("dtB", [128, 1024], BF16, kind="ExternalOutput")
    AdtF = nc.dram_tensor("AdtF", [128, 1024], F32, kind="ExternalOutput")
    cumrelF = nc.dram_tensor("cumrelF", [128, 1024], F32, kind="ExternalOutput")

    sched = Sched()
    NTQ = [(0, 512), (512, 512), (1024, 3)]

    with contextlib.ExitStack() as ctx:
        E = ctx.enter_context
        P = Prog(nc, ctx, sched)

        ones_c = E(nc.sbuf_tensor("ones_c", [128, 1], BF16))
        ones_rf = E(nc.sbuf_tensor("ones_rf", [1, 128], F32R))
        ones_ff = E(nc.sbuf_tensor("ones_ff", [1, 128], F32))
        cw_sb = E(nc.sbuf_tensor("cw_sb", [128, 80 * 4], F32))
        cb_sb = E(nc.sbuf_tensor("cb_sb", [128, 80], F32))
        dtb_sb = E(nc.sbuf_tensor("dtb_sb", [128, 1], F32))
        alog_sb = E(nc.sbuf_tensor("alog_sb", [128, 1], F32))
        nega_sb = E(nc.sbuf_tensor("nega_sb", [128, 1], F32))
        fc_sb = E(nc.sbuf_tensor("fc_sb", [128, 4], F32))
        for i, val in enumerate(FCONSTS):
            nc.const_aps.register(F32, val, fc_sb[:, i:i + 1])
        for dst, srcd in [(ones_c, onescol), (ones_rf, onesrow_f), (ones_ff, onesrow_f),
                          (cw_sb, convW),
                          (cb_sb, convB), (dtb_sb, dtBias), (alog_sb, Alog),
                          (fc_sb, fconsts)]:
            P.gdma(dst[:], srcd[:, :])
        v_const = P.ld.n
        v_na0 = P.scalar(lambda s: s.activation(nega_sb[:], alog_sb[:], AF.Exp),
                         waits=[(P.ld, v_const)])
        v_na = P.vector(
            lambda v: v.tensor_scalar(out=nega_sb[:], in0=nega_sb[:], scalar1=-1.0,
                                      scalar2=None, op0=ALU.mult),
            waits=[(P.ac, v_na0)])

        xn1_sb = E(nc.sbuf_tensor("xn1_sb", [128, KC * TQ], BF16))

        with contextlib.ExitStack() as b1:
            Ep = b1.enter_context
            sqslab = Ep(nc.sbuf_tensor("sqslab", [128, 2 * TQ], BF16))
            rows = Ep(nc.sbuf_tensor("rows", [1, 4 * TQ], F32))
            rstd_r = Ep(nc.sbuf_tensor("rstd_r", [1, TQ], F32R))
            negmr_r = Ep(nc.sbuf_tensor("negmr_r", [1, TQ], F32R))
            rstd_b = Ep(nc.sbuf_tensor("rstd_b", [128, TQ], F32))
            negmr_b = Ep(nc.sbuf_tensor("negmr_b", [128, TQ], F32))

            mean_r = rows[:, 0:TQ]
            m2_r = rows[:, TQ:2 * TQ]
            var_r = rows[:, 2 * TQ:3 * TQ]
            lnv_r = rows[:, 3 * TQ:4 * TQ]

            with contextlib.ExitStack() as b1s:
                msum_p = b1s.enter_context(nc.psum_tensor("msum_p", [1, TQ], F32))
                sqsum_p = b1s.enter_context(nc.psum_tensor("sqsum_p", [1, TQ], F32))
                mm_hist = {}
                for kc in range(KC):
                    buf = xn1_sb[:, ts(kc, TQ)]
                    sq = sqslab[:, ts(kc % 2, TQ)]
                    v_ld = P.sdma(buf, x1T[ts(kc, 128), :])
                    w = [(P.mm, mm_hist[kc - 2])] if kc >= 2 else []
                    v_sq = P.scalar(lambda s, buf=buf, sq=sq: s.activation(sq, buf, AF.Square),
                                    waits=[(P.st, v_ld)] + w)
                    def f(t, kc=kc, buf=buf, sq=sq):
                        inst = None
                        for (o, n) in NTQ:
                            t.matmul(msum_p[:, o:o + n], ones_c[:], buf[:, o:o + n],
                                     start=(kc == 0), stop=(kc == KC - 1))
                        for (o, n) in NTQ:
                            inst = t.matmul(sqsum_p[:, o:o + n], ones_c[:], sq[:, o:o + n],
                                            start=(kc == 0), stop=(kc == KC - 1))
                        return inst
                    mm_hist[kc] = P.tensor(f, waits=[(P.st, v_ld), (P.ac, v_sq)])
                v_stats = P.mm.n

                def f(v):
                    v.tensor_scalar(out=mean_r, in0=msum_p[:], scalar1=1.0 / DM, scalar2=None, op0=ALU.mult)
                    v.drain()
                    v.tensor_tensor(m2_r, mean_r, mean_r, op=ALU.mult)
                    v.drain()
                    return v.scalar_tensor_tensor(out=var_r, in0=sqsum_p[:], scalar=1.0 / DM, in1=m2_r,
                                                  op0=ALU.mult, op1=ALU.subtract)
                v_rows = P.vector(f, waits=[(P.mm, v_stats)])
                def f(s):
                    s.activation(lnv_r, var_r, AF.Ln, bias=1e-5)
                    s.drain()
                    return s.activation(rstd_r[:], lnv_r, AF.Exp, scale=-0.5)
                v_rstd = P.scalar(f, waits=[(P.dv, v_rows)])
                v_negmr = P.vector(
                    lambda v: v.scalar_tensor_tensor(out=negmr_r[:], in0=mean_r, scalar=-1.0,
                                                     in1=rstd_r[:], op0=ALU.mult, op1=ALU.mult),
                    waits=[(P.ac, v_rstd)])

            with contextlib.ExitStack() as b1b:
                bc_p = b1b.enter_context(nc.psum_tensor("bc_p", [128, 512], F32))
                prev_bc = None
                for row, bcast in [(rstd_r, rstd_b), (negmr_r, negmr_b)]:
                    for (o, n) in NTQ:
                        wb_ = [(P.dv, prev_bc)] if prev_bc is not None else []
                        if n >= 256:
                            v_b = P.tensor(
                                lambda t, row=row, o=o, n=n: t.matmul(
                                    bc_p[:, 0:n], ones_rf[:, :], row[:, o:o + n].bitcast(F32R),
                                    start=True, stop=True),
                                waits=[(P.dv, v_negmr)] + wb_)
                        else:
                            v_b = P.tensor(
                                lambda t, row=row, o=o, n=n: t.matmul(
                                    bc_p[:, 0:n], ones_ff[:, :], row[:, o:o + n].bitcast(F32),
                                    start=True, stop=True),
                                waits=[(P.dv, v_negmr)] + wb_)
                        prev_bc = P.vector(
                            lambda v, bcast=bcast, o=o, n=n: v.tensor_copy(bcast[:, o:o + n], bc_p[:, 0:n]),
                            waits=[(P.mm, v_b)])
                v_bc = P.dv.n

                for kc in range(KC):
                    buf = xn1_sb[:, ts(kc, TQ)]
                    sq = sqslab[:, ts(kc % 2, TQ)]
                    def f(v, buf=buf, sq=sq):
                        v.tensor_tensor(sq, buf, rstd_b[:], op=ALU.mult)
                        v.drain()
                        return v.tensor_tensor(buf, sq, negmr_b[:], op=ALU.add)
                    v_n = P.vector(f, waits=[(P.dv, v_bc)])
                    P.sdma(xn1T[ts(kc, 128), :], buf, waits=[(P.dv, v_n)])
        v_xn1 = P.dv.n

        P.barrier()

        with contextlib.ExitStack() as b2:
            Ep = b2.enter_context
            wbuf = Ep(nc.sbuf_tensor("wbuf", [128, 2 * KC * 128], BF16))
            ev = Ep(nc.sbuf_tensor("ev", [128, 2 * TQ], BF16))
            cacc = Ep(nc.sbuf_tensor("cacc", [128, 1024], F32))
            csil = Ep(nc.sbuf_tensor("csil", [128, 2 * 1024], BF16))
            dtraw = Ep(nc.sbuf_tensor("dtraw", [128, TQ], F32))
            dts = Ep(nc.sbuf_tensor("dts", [128, 5 * 1024], F32))
            ip_p = Ep(nc.psum_tensor("ip_p", [128, TQ], F32))

            hist = {}
            for oc in range(145):
                wb = wbuf[:, ts(oc % 2, KC * 128)]
                evb = ev[:, ts(oc % 2, TQ)]
                w = [(P.mm, hist[oc - 2])] if oc >= 2 else []
                v_w = P.gdma(wb.rearrange("p (c n) -> p c n", n=128),
                             wipT[:, ts(oc, 128)].rearrange("(c p) n -> p c n", p=128), waits=w)
                wprev = [(P.dv, hist[('e', oc - 1)])] if oc >= 1 else []
                def f(t, wb=wb):
                    inst = None
                    for kc in range(KC):
                        for (o, n) in NTQ:
                            inst = t.matmul(ip_p[:, o:o + n], wb[:, ts(kc, 128)],
                                            xn1_sb[:, kc * TQ + o: kc * TQ + o + n],
                                            start=(kc == 0), stop=(kc == KC - 1))
                    return inst
                hist[oc] = v_m = P.tensor(f, waits=[(P.ld, v_w), (P.dv, v_xn1)] + wprev)

                wev = []
                if ('s', oc - 2) in hist:
                    wev.append((P.st, hist[('s', oc - 2)]))
                if ('cv', oc - 2) in hist:
                    wev.append((P.dv, hist[('cv', oc - 2)]))
                if oc < 64:
                    hist[('e', oc)] = v_e = P.vector(
                        lambda v, evb=evb: v.tensor_copy(evb[:, 0:1024], ip_p[:, 3:TQ]),
                        waits=[(P.mm, v_m)] + wev)
                    hist[('s', oc)] = P.sdma(gateT[ts(oc, 128), :], evb[:, 0:1024], waits=[(P.dv, v_e)])
                elif oc < 144:
                    cc = oc - 64
                    hist[('e', oc)] = v_e = P.vector(
                        lambda v, evb=evb: v.tensor_copy(evb[:, 0:TQ], ip_p[:, 0:TQ]),
                        waits=[(P.mm, v_m)] + wev)
                    def fpool(g, evb=evb, cc=cc):
                        g.tensor_scalar(out=cacc[:], in0=evb[:, 0:1024],
                                        scalar1=cw_sb[:, cc * 4: cc * 4 + 1], scalar2=None,
                                        op0=ALU.mult)
                        inst = None
                        for tap in range(1, 4):
                            g.drain()
                            inst = g.scalar_tensor_tensor(out=cacc[:], in0=evb[:, tap:tap + 1024],
                                                          scalar=cw_sb[:, cc * 4 + tap: cc * 4 + tap + 1],
                                                          in1=cacc[:], op0=ALU.mult, op1=ALU.add)
                        return inst
    
                    wcv = [(P.dv, v_e), (P.ld, v_const)]
                    if ('c', oc - 1) in hist:
                        wcv.append((P.ac, hist[('c', oc - 1)]))
                    hist[('cv', oc)] = v_cv = P.vector(fpool, waits=wcv)
                    csb = csil[:, ts(cc % 2, 1024)]
                    wsi = [(P.dv, v_cv)]
                    if ('s', oc - 2) in hist:
                        wsi.append((P.st, hist[('s', oc - 2)]))
                    v_si = P.scalar(
                        lambda s, csb=csb, cc=cc: s.activation(csb, cacc[:], AF.Silu,
                                                               bias=cb_sb[:, cc:cc + 1]),
                        waits=wsi)
                    hist[('c', oc)] = v_si
                    if cc < 64:
                        v_s = P.sdma(xT[ts(cc, 128), :], csb, waits=[(P.ac, v_si)])
                    elif cc < 72:
                        v_s = P.sdma(BT[ts(cc - 64, 128), :], csb, waits=[(P.ac, v_si)])
                    else:
                        v_s = P.sdma(CT[ts(cc - 72, 128), :], csb, waits=[(P.ac, v_si)])
                    hist[('s', oc)] = v_s
                else:
                    v_e = P.vector(lambda v: v.tensor_copy(dtraw[:], ip_p[:, 0:TQ]),
                                   waits=[(P.mm, v_m)])
                    hist[('e', oc)] = v_e
                    rb = dts[:, 0:1024]
                    ab = dts[:, 1024:2048]
                    eb = dts[:, 2048:3072]
                    lb = dts[:, 3072:4096]
                    dt_ = dts[:, 4096:5120]
                    v_rb = P.vector(lambda v: v.tensor_scalar(out=rb, in0=dtraw[:, 3:TQ],
                                                              scalar1=dtb_sb[:, 0:1], scalar2=None,
                                                              op0=ALU.add),
                                    waits=[(P.dv, v_e), (P.ld, v_const)])
                    v_abs = P.scalar(lambda s: s.activation(ab, rb, AF.Abs),
                                     waits=[(P.dv, v_rb)])
                    def f(s):
                        s.activation(eb, ab, AF.Exp, scale=-1.0)
                        s.drain()
                        return s.activation(lb, eb, AF.Ln, bias=1.0)
                    v_ln = P.scalar(f, waits=[(P.ac, v_abs)])
                    def fv(v):
                        v.tensor_scalar(out=dt_, in0=rb, scalar1=0.0, scalar2=None, op0=ALU.max)
                        v.drain()
                        v.tensor_tensor(dt_, dt_, lb, op=ALU.add)
                        v.drain()
                        inst = v.tensor_scalar(out=ab, in0=dt_, scalar1=nega_sb[:, 0:1],
                                               scalar2=None, op0=ALU.mult)
                        return inst
                    v_dt = P.vector(fv, waits=[(P.ac, v_ln), (P.dv, v_na)])
                    def fv(v):
                        inst = None
                        for c in range(4):
                            inst = v.tensor_tensor_scan(out=eb[:, ts(c, 256)], data0=ab[:, ts(c, 256)],
                                                        data1=ab[:, ts(c, 256)], initial=0.0,
                                                        op0=ALU.add, op1=ALU.bypass)
                        return inst
                    v_cum = P.vector(fv, waits=[(P.dv, v_dt)])
                    v_dtb = P.vector(lambda v: v.tensor_copy(csil[:, 0:1024], dt_),
                                     waits=[(P.dv, v_dt), (P.ac, hist.get(('c', 143), 0)),
                                            (P.st, hist.get(('s', 142), 0))])
                    P.sdma(dtF[:, :], dt_, waits=[(P.dv, v_dt)])
                    P.sdma(dtB[:, :], csil[:, 0:1024], waits=[(P.dv, v_dtb)])
                    P.sdma(AdtF[:, :], ab, waits=[(P.dv, v_cum)])
                    P.sdma(cumrelF[:, :], eb, waits=[(P.dv, v_cum)])

            vfin = P.st.n
            sched.on("sync", lambda s, vfin=vfin: s.wait_ge(P.st.h, vfin))

        with nc.allow_low_precision("bf16 by design"):
            sched.emit(nc)
    return nc


def host_inputs_lB(inp, x1_list):
    shared = {
        "wipT": np.ascontiguousarray(inp["L1_attn_in_proj_w"].T),
        "convW": np.ascontiguousarray(inp["L1_attn_conv_w"][:, 0, :].reshape(80, 128, 4)
                                      .transpose(1, 0, 2).reshape(128, 320)),
        "convB": np.ascontiguousarray(inp["L1_attn_conv_b"].reshape(80, 128).T),
        "dtBias": inp["L1_attn_dt_bias"].reshape(128, 1),
        "Alog": inp["L1_attn_A_log"].reshape(128, 1),
        "onescol": np.ones((128, 1), BF),
        "onesrow_f": np.ones((1, 128), np.float32),
        "fconsts": np.tile(np.array([FCONSTS], np.float32), (128, 1)),
    }
    maps = []
    for core in range(8):
        m = dict(shared)
        m["x1T"] = x1_list[core]
        maps.append(m)
    return maps



CS = 256


def build_lC():
    nc = bass.Bass("TRN2", target_bir_lowering=False)
    ins = {}
    for p in range(2):
        ins[f"xTM{p}"] = nc.dram_tensor(f"xTM{p}", [4096, 1024], BF16, kind="ExternalInput")
        ins[f"Bfm{p}"] = nc.dram_tensor(f"Bfm{p}", [128, 4096], BF16, kind="ExternalInput")
        ins[f"Cfm{p}"] = nc.dram_tensor(f"Cfm{p}", [128, 4096], BF16, kind="ExternalInput")
        ins[f"dtTM{p}"] = nc.dram_tensor(f"dtTM{p}", [4096, 16], BF16, kind="ExternalInput")
        ins[f"AdtTM{p}"] = nc.dram_tensor(f"AdtTM{p}", [4096, 16], F32, kind="ExternalInput")
        ins[f"cumF{p}"] = nc.dram_tensor(f"cumF{p}", [16, 4096], F32, kind="ExternalInput")
    sel16 = nc.dram_tensor("sel16", [16, 16 * 256], F32, kind="ExternalInput")
    i128b = nc.dram_tensor("i128b", [128, 128], BF16, kind="ExternalInput")
    trilS = nc.dram_tensor("trilS", [128, 128], F32, kind="ExternalInput")
    triuI = nc.dram_tensor("triuI", [128, 128], F32, kind="ExternalInput")
    ones2 = nc.dram_tensor("ones2", [128, 128], F32, kind="ExternalInput")
    gmask = nc.dram_tensor("gmask", [128, 512], BF16, kind="ExternalInput")
    onesrow_f = nc.dram_tensor("onesrow_f", [1, 128], F32, kind="ExternalInput")
    i16f = nc.dram_tensor("i16f", [16, 16], F32, kind="ExternalInput")
    outs = {}
    for p in range(2):
        outs[p] = nc.dram_tensor(f"yTM{p}", [4096, 1024], BF16, kind="ExternalOutput")

    sched = Sched()
    with contextlib.ExitStack() as ctx:
        E = ctx.enter_context
        P = Prog(nc, ctx, sched)

        sel_r = E(nc.sbuf_tensor("sel_r", [16, 16 * 256], F32R))
        i128_sb = E(nc.sbuf_tensor("i128_sb", [128, 128], BF16))
        tril_sb = E(nc.sbuf_tensor("tril_sb", [128, 128], F32))
        triu_sb = E(nc.sbuf_tensor("triu_sb", [128, 128], F32))
        ones2_sb = E(nc.sbuf_tensor("ones2_sb", [128, 128], F32))
        gmask_sb = E(nc.sbuf_tensor("gmask_sb", [128, 512], BF16))
        ones_rf = E(nc.sbuf_tensor("ones_rf", [1, 128], F32R))
        i16_sb = E(nc.sbuf_tensor("i16_sb", [16, 16], F32))

        xtm_sb = E(nc.sbuf_tensor("xtm_sb", [128, 32 * 1024], BF16))
        bfm_sb = E(nc.sbuf_tensor("bfm_sb", [128, 4096], BF16))
        cfm_sb = E(nc.sbuf_tensor("cfm_sb", [128, 4096], BF16))
        dttm_sb = E(nc.sbuf_tensor("dttm_sb", [128, 32 * 16], BF16))
        adt_sb = E(nc.sbuf_tensor("adt_sb", [128, 32 * 16], F32))
        cum_r = E(nc.sbuf_tensor("cum_r", [16, 4096], F32R))
        ncum_r = E(nc.sbuf_tensor("ncum_r", [16, 4096], F32R))
        R_sb = E(nc.sbuf_tensor("R_sb", [128, 1024], F32))
        Rbf_sb = E(nc.sbuf_tensor("Rbf_sb", [128, 1024], BF16))
        xs_sb = E(nc.sbuf_tensor("xs_sb", [128, 2 * 1024], BF16))
        xsd_sb = E(nc.sbuf_tensor("xsd_sb", [128, 2 * 1024], BF16))
        dec_sb = E(nc.sbuf_tensor("dec_sb", [128, 64], BF16))
        btm_sb = E(nc.sbuf_tensor("btm_sb", [128, 2 * 128], BF16))
        gms_sb = E(nc.sbuf_tensor("gms_sb", [128, 512], BF16))
        tmin_sb = E(nc.sbuf_tensor("tmin_sb", [128, 512], F32))
        lbf_sb = E(nc.sbuf_tensor("lbf_sb", [128, 512], BF16))
        sc_sb = E(nc.sbuf_tensor("sc_sb", [128, 16 * 512], BF16))
        ysl_sb = E(nc.sbuf_tensor("ysl_sb", [128, 2 * 1024], BF16))
        ea_sb = E(nc.sbuf_tensor("ea_sb", [1, 16], F32))
        eahd_sb = E(nc.sbuf_tensor("eahd_sb", [1, 1024], F32R))
        da_sb = E(nc.sbuf_tensor("da_sb", [128, 1024], BF16))

        dec_p = E(nc.psum_tensor("dec_p", [128, 64], F32))
        gm_p = E(nc.psum_tensor("gm_p", [128, 512], F32))
        tp2_p = E(nc.psum_tensor("tp2_p", [128, 512], F32))
        y_p = E(nc.psum_tensor("y_p", [128, 1024], F32))
        s_p = E(nc.psum_tensor("s_p", [128, 1024], F32))
        tpb_p = E(nc.psum_tensor("tpb_p", [128, 128], F32))

        for dst, srcd in [(sel_r, sel16), (i128_sb, i128b), (tril_sb, trilS),
                          (triu_sb, triuI), (ones2_sb, ones2), (gmask_sb, gmask),
                          (ones_rf, onesrow_f), (i16_sb, i16f)]:
            P.gdma(dst[:], srcd[:, :])

        carry_yst = {}
        for p in range(2):
            for tc in range(32):
                P.sdma(xtm_sb[:, ts(tc, 1024)], ins[f"xTM{p}"][ts(tc, 128), :])
            P.sdma(bfm_sb[:], ins[f"Bfm{p}"][:, :])
            P.sdma(cfm_sb[:], ins[f"Cfm{p}"][:, :])
            P.sdma(dttm_sb[:].rearrange("p (c n) -> p c n", n=16),
                   ins[f"dtTM{p}"][:, :].rearrange("(c p) n -> p c n", p=128))
            v_ld1 = P.gdma(adt_sb[:].rearrange("p (c n) -> p c n", n=16),
                           ins[f"AdtTM{p}"][:, :].rearrange("(c p) n -> p c n", p=128))
            v_ld2 = P.gdma(cum_r[:], ins[f"cumF{p}"][:, :])
            v_lds = P.st.n
            v_ldg = P.ld.n
            v_neg = P.vector(
                lambda v: v.tensor_scalar(out=ncum_r[:], in0=cum_r[:], scalar1=-1.0,
                                          scalar2=None, op0=ALU.mult),
                waits=[(P.ld, v_ldg)])
            def f(v):
                v.memset(R_sb[:], 0.0)
                v.drain()
                return v.memset(Rbf_sb[:], 0.0)
            v_r0 = P.vector(f, waits=[(P.dv, v_neg)])

            hist = {}
            prev = dict(dec=None, gms=None, sc=None, y=None, s=None, tpb=None,
                        xs=None, da=None, rup=v_r0, yst=dict(carry_yst), tmin=None)
            for c in range(16):
                tok0 = c * CS
                t0c = tok0
                wdec = [(P.ld, v_ldg), (P.st, v_lds)]
                if prev["dec"] is not None:
                    wdec.append((P.ac, prev["dec"]))
                adt0 = adt_sb[:, (tok0 // 128) * 16: (tok0 // 128) * 16 + 16]
                adt1 = adt_sb[:, (tok0 // 128 + 1) * 16: (tok0 // 128 + 1) * 16 + 16]
                def f(t, adt0=adt0, adt1=adt1):
                    t.matmul(dec_p[:, 0:16], tril_sb[:], adt0, start=True, stop=False)
                    t.matmul(dec_p[:, 0:16], ones2_sb[:], adt1, start=False, stop=True)
                    t.matmul(dec_p[:, 16:32], tril_sb[:], adt1, start=False, stop=True)
                    t.matmul(dec_p[:, 32:48], triu_sb[:], adt0, start=False, stop=True)
                    t.matmul(dec_p[:, 48:64], ones2_sb[:], adt0, start=False, stop=False)
                    return t.matmul(dec_p[:, 48:64], triu_sb[:], adt1, start=False, stop=True)
                v_dm = P.tensor(f, waits=wdec)
                prev["dec"] = v_de = P.scalar(
                    lambda s: s.activation(dec_sb[:], dec_p[:], AF.Exp),
                    waits=[(P.mm, v_dm)])
                wxs = [(P.ac, v_de)]
                if prev["xs"] is not None:
                    wxs.append((P.mm, prev["xs"]))
                def f(v, tok0=tok0):
                    inst = None
                    for ti in range(2):
                        tc = tok0 // 128 + ti
                        dtv = dttm_sb[:, tc * 16:(tc + 1) * 16].unsqueeze(2).broadcast_to([128, 16, 64])
                        v.tensor_tensor(xs_sb[:, ts(ti, 1024)], xtm_sb[:, ts(tc, 1024)], dtv,
                                        op=ALU.mult)
                        v.drain()
                        dcv = dec_sb[:, ti * 16:(ti + 1) * 16].unsqueeze(2).broadcast_to([128, 16, 64])
                        inst = v.tensor_tensor(xsd_sb[:, ts(ti, 1024)], xs_sb[:, ts(ti, 1024)], dcv,
                                               op=ALU.mult)
                        v.drain()
                    return inst
                v_xs = P.vector(f, waits=wxs)
                wg = [(P.st, v_lds)]
                if prev["gms"] is not None:
                    wg.append((P.dv, prev["gms"]))
                def f(t, t0c=t0c):
                    t.matmul(gm_p[:, 0:256], bfm_sb[:, t0c:t0c + 128], cfm_sb[:, t0c:t0c + 256],
                             start=True, stop=True)
                    return t.matmul(gm_p[:, 256:512], bfm_sb[:, t0c + 128:t0c + 256],
                                    cfm_sb[:, t0c:t0c + 256], start=False, stop=True)
                v_gm = P.tensor(f, waits=wg)
                prev["gms"] = v_gms = P.vector(
                    lambda v: v.tensor_tensor(gms_sb[:], gm_p[:], gmask_sb[:], op=ALU.mult),
                    waits=[(P.mm, v_gm)])
                wb = [(P.dv, prev["tpb"])] if prev["tpb"] is not None else []
                v_bt0 = P.tensor(lambda t, t0c=t0c: t.transpose(
                    tpb_p[:].bitcast(BF16)[:, 0:128], bfm_sb[:, t0c:t0c + 128], i128_sb[:]),
                    waits=[(P.st, v_lds)] + wb)
                v_be0 = P.vector(lambda v: v.tensor_copy(btm_sb[:, 0:128],
                                                         tpb_p[:].bitcast(BF16)[:, 0:128]),
                                 waits=[(P.mm, v_bt0)])
                v_bt1 = P.tensor(lambda t, t0c=t0c: t.transpose(
                    tpb_p[:].bitcast(BF16)[:, 0:128], bfm_sb[:, t0c + 128:t0c + 256], i128_sb[:]),
                    waits=[(P.dv, v_be0)])
                prev["tpb"] = v_be1 = P.vector(
                    lambda v: v.tensor_copy(btm_sb[:, 128:256], tpb_p[:].bitcast(BF16)[:, 0:128]),
                    waits=[(P.mm, v_bt1)])
                for hh in range(16):
                    wt = [(P.dv, prev["tmin"])] if prev["tmin"] is not None else [(P.dv, v_neg)]
                    def f(t, hh=hh, t0c=t0c):
                        sel0 = sel_r[:, hh * 256: hh * 256 + 128]
                        sel1 = sel_r[:, hh * 256: hh * 256 + 256]
                        t.matmul(tp2_p[:, 0:256], sel0, cum_r[:, t0c:t0c + 256], start=True, stop=False)
                        t.matmul(tp2_p[:, 0:256], ncum_r[:, t0c:t0c + 128], sel1, start=False, stop=True)
                        t.matmul(tp2_p[:, 256:512], sel0, cum_r[:, t0c:t0c + 256], start=False, stop=False)
                        return t.matmul(tp2_p[:, 256:512], ncum_r[:, t0c + 128:t0c + 256], sel1,
                                        start=False, stop=True)
                    v_t2 = P.tensor(f, waits=wt)
                    prev["tmin"] = v_tm = P.vector(
                        lambda v: v.tensor_scalar(out=tmin_sb[:], in0=tp2_p[:], scalar1=0.0,
                                                  scalar2=None, op0=ALU.min),
                        waits=[(P.mm, v_t2)])
                    v_lx = P.scalar(lambda s: s.activation(lbf_sb[:], tmin_sb[:], AF.Exp),
                                    waits=[(P.dv, v_tm)])
                    hist[('sc', hh)] = P.vector(
                        lambda v, hh=hh: v.tensor_tensor(sc_sb[:, ts(hh, 512)], lbf_sb[:],
                                                         gms_sb[:], op=ALU.mult),
                        waits=[(P.ac, v_lx), (P.dv, v_gms),
                               (P.mm, prev["yd"]) if False else (P.ac, v_lx)])
                v_scall = P.dv.n
                for ih in range(2):
                    wy = [(P.dv, prev["rup"]), (P.dv, v_scall), (P.dv, v_xs)]
                    if prev["y"] is not None:
                        wy.append((P.dv, prev["y"]))
                    def f(t, ih=ih, t0c=t0c):
                        inst = None
                        for hh in range(16):
                            inst = t.matmul(y_p[:, hh * 64:(hh + 1) * 64],
                                            cfm_sb[:, t0c + ih * 128: t0c + ih * 128 + 128],
                                            Rbf_sb[:, hh * 64:(hh + 1) * 64],
                                            start=(hh % 8 == 0), stop=False)
                        return inst
                    v_yo = P.tensor(f, waits=wy)
                    def f(v, ih=ih):
                        sdv = dec_sb[:, 32 + ih * 16: 48 + ih * 16].unsqueeze(2).broadcast_to([128, 16, 64])
                        return v.tensor_tensor(y_p[:], y_p[:], sdv, op=ALU.mult)
                    v_sd = P.vector(f, waits=[(P.mm, v_yo), (P.ac, v_de)])
                    def f(t, ih=ih):
                        inst = None
                        for hh in range(16):
                            base = hh * 512
                            if ih == 0:
                                inst = t.matmul(y_p[:, hh * 64:(hh + 1) * 64],
                                                sc_sb[:, base: base + 128],
                                                xs_sb[:, hh * 64: hh * 64 + 64],
                                                start=False, stop=False)
                            else:
                                t.matmul(y_p[:, hh * 64:(hh + 1) * 64],
                                         sc_sb[:, base + 128: base + 256],
                                         xs_sb[:, hh * 64: hh * 64 + 64],
                                         start=False, stop=False)
                                inst = t.matmul(y_p[:, hh * 64:(hh + 1) * 64],
                                                sc_sb[:, base + 384: base + 512],
                                                xs_sb[:, 1024 + hh * 64: 1024 + hh * 64 + 64],
                                                start=False, stop=True)
                        return inst
                    v_yd = P.tensor(f, waits=[(P.dv, v_sd)])
                    ysl = ysl_sb[:, ts(ih, 1024)]
                    wev = [(P.st, prev["yst"].get(ih, 0))] if prev["yst"].get(ih) else []
                    prev["y"] = v_ye = P.vector(
                        lambda v, ysl=ysl: v.tensor_copy(ysl, y_p[:]),
                        waits=[(P.mm, v_yd)] + wev)
                    prev["yst"][ih] = P.sdma(outs[p][tok0 + ih * 128: tok0 + ih * 128 + 128, :],
                                             ysl, waits=[(P.dv, v_ye)])
                ws = [(P.dv, prev["rup"]), (P.dv, v_xs), (P.dv, v_be1)]
                if prev["s"] is not None:
                    ws.append((P.dv, prev["s"]))
                def f(t):
                    inst = None
                    for hh in range(16):
                        for kh in range(2):
                            inst = t.matmul(s_p[:, hh * 64:(hh + 1) * 64],
                                            btm_sb[:, ts(kh, 128)],
                                            xsd_sb[:, kh * 1024 + hh * 64: kh * 1024 + hh * 64 + 64],
                                            start=(hh % 8 == 0 and kh == 0), stop=(kh == 1))
                    return inst
                v_sm = P.tensor(f, waits=ws)
                wda = [(P.dv, prev["y"])]
                v_at = P.tensor(
                    lambda t, t0c=t0c: t.matmul(tpb_p[0:1, 0:16],
                                                cum_r[:, t0c + 255: t0c + 256].bitcast(F32),
                                                i16_sb[:], start=True, stop=True),
                    waits=[(P.dv, prev["tpb"])])
                v_ae = P.vector(lambda v: v.tensor_copy(ea_sb[:], tpb_p[0:1, 0:16]),
                                waits=[(P.mm, v_at)])
                v_ax = P.scalar(lambda s: s.activation(ea_sb[:], ea_sb[:], AF.Exp),
                                waits=[(P.dv, v_ae)])
                v_ah = P.vector(
                    lambda v: v.tensor_copy(eahd_sb[:],
                                            ea_sb[:].unsqueeze(2).broadcast_to([1, 16, 64])),
                    waits=[(P.ac, v_ax)])
                def f(t):
                    t.matmul(y_p[:, 0:512], ones_rf[:, :], eahd_sb[:, 0:512], start=True, stop=True)
                    return t.matmul(y_p[:, 512:1024], ones_rf[:, :], eahd_sb[:, 512:1024],
                                    start=True, stop=True)
                v_dam = P.tensor(f, waits=[(P.dv, v_ah)] + wda)
                v_dae = P.vector(lambda v: v.tensor_copy(da_sb[:], y_p[:]),
                                 waits=[(P.mm, v_dam)])
                prev["da"] = v_dae
                def f(v):
                    v.tensor_tensor(R_sb[:], R_sb[:], da_sb[:], op=ALU.mult)
                    v.drain()
                    v.tensor_tensor(R_sb[:], R_sb[:], s_p[:], op=ALU.add)
                    v.drain()
                    return v.tensor_copy(Rbf_sb[:], R_sb[:])
                prev["rup"] = P.vector(f, waits=[(P.dv, v_dae), (P.mm, v_sm)])
                prev["y"] = prev["rup"]
            carry_yst = dict(prev["yst"])
            P.barrier()

        vfin = P.st.n
        sched.on("sync", lambda s, vfin=vfin: s.wait_ge(P.st.h, vfin))
        with nc.allow_low_precision("bf16/f32r by design"):
            sched.emit(nc)
    return nc


def host_inputs_lC(lb_outs):
    xs = {}
    for b in range(2):
        cores = [b * 4 + i for i in range(4)]
        xs[('x', b)] = np.concatenate([lb_outs[c]["xT"] for c in cores], 1)
        xs[('B', b)] = np.concatenate([lb_outs[c]["BT"] for c in cores], 1)
        xs[('C', b)] = np.concatenate([lb_outs[c]["CT"] for c in cores], 1)
        xs[('dt', b)] = np.concatenate([lb_outs[c]["dtB"] for c in cores], 1)
        xs[('Adt', b)] = np.concatenate([lb_outs[c]["AdtF"] for c in cores], 1)
        xs[('cum', b)] = np.concatenate([lb_outs[c]["cumrelF"] for c in cores], 1)
    pair_table = []
    sel = np.zeros((16, 16 * 256), np.float32)
    for hh in range(16):
        sel[hh, hh * 256:(hh + 1) * 256] = 1.0
    i = np.arange(128)[:, None]; j = np.arange(128)[None, :]
    tril_s = (i > j).astype(np.float32)
    triu_i = (i <= j).astype(np.float32)
    jj = np.arange(128)[:, None]; ii = np.arange(256)[None, :]
    g0 = (jj <= ii).astype(np.float32)
    g1 = ((jj + 128) <= ii).astype(np.float32)
    consts = {
        "sel16": sel,
        "i128b": np.eye(128, dtype=BF),
        "trilS": tril_s, "triuI": triu_i,
        "ones2": np.ones((128, 128), np.float32),
        "gmask": np.concatenate([g0, g1], 1).astype(BF),
        "onesrow_f": np.ones((1, 128), np.float32),
        "i16f": np.eye(16, dtype=np.float32),
    }
    maps = []
    for core in range(8):
        pairs = [(core // 4, (core % 4) * 2), (core // 4, (core % 4) * 2 + 1)]
        pair_table.append(pairs)
        m = dict(consts)
        for p, (b, g) in enumerate(pairs):
            m[f"xTM{p}"] = np.ascontiguousarray(xs[('x', b)][g * 1024:(g + 1) * 1024, :].T)
            m[f"Bfm{p}"] = np.ascontiguousarray(xs[('B', b)][g * 128:(g + 1) * 128, :])
            m[f"Cfm{p}"] = np.ascontiguousarray(xs[('C', b)][g * 128:(g + 1) * 128, :])
            m[f"dtTM{p}"] = np.ascontiguousarray(xs[('dt', b)][g * 16:(g + 1) * 16, :].T)
            m[f"AdtTM{p}"] = np.ascontiguousarray(xs[('Adt', b)][g * 16:(g + 1) * 16, :].T)
            m[f"cumF{p}"] = np.ascontiguousarray(xs[('cum', b)][g * 16:(g + 1) * 16, :])
        maps.append(m)
    return maps, pair_table



def build_lD():
    nc = bass.Bass("TRN2", target_bir_lowering=False)
    yT = nc.dram_tensor("yT", [8192, 1024], BF16, kind="ExternalInput")
    xT = nc.dram_tensor("xT", [8192, 1024], BF16, kind="ExternalInput")
    gateT = nc.dram_tensor("gateT", [8192, 1024], BF16, kind="ExternalInput")
    xn1T = nc.dram_tensor("xn1T", [DM, TQ], BF16, kind="ExternalInput")
    x1T = nc.dram_tensor("x1T", [DM, TQ], BF16, kind="ExternalInput")
    wopT = nc.dram_tensor("wopT", [8192, DM], F32, kind="ExternalInput")
    U1w = nc.dram_tensor("U1w", [DM, 1024], F32, kind="ExternalInput")
    V1w = nc.dram_tensor("V1w", [1024, 22016], F32, kind="ExternalInput")
    U2w = nc.dram_tensor("U2w", [11008, 1024], F32, kind="ExternalInput")
    V2w = nc.dram_tensor("V2w", [1024, DM], F32, kind="ExternalInput")
    b1c = nc.dram_tensor("b1c", [128, 172], F32, kind="ExternalInput")
    b2c = nc.dram_tensor("b2c", [128, 32], F32, kind="ExternalInput")
    nwc = nc.dram_tensor("nwc", [128, 64], F32, kind="ExternalInput")
    Dc = nc.dram_tensor("Dc", [128, 64], F32, kind="ExternalInput")
    onescol = nc.dram_tensor("onescol", [128, 1], BF16, kind="ExternalInput")
    onesrow_f = nc.dram_tensor("onesrow_f", [1, 128], F32, kind="ExternalInput")
    fconsts = nc.dram_tensor("fconsts", [128, 4], F32, kind="ExternalInput")

    x2T = nc.dram_tensor("x2T", [DM, 1024], F32, kind="ExternalOutput")

    sched = Sched()
    with contextlib.ExitStack() as ctx:
        E = ctx.enter_context
        P = Prog(nc, ctx, sched)
        ones_c = E(nc.sbuf_tensor("ones_c", [128, 1], BF16))
        ones_rf = E(nc.sbuf_tensor("ones_rf", [1, 128], F32R))
        b1_sb = E(nc.sbuf_tensor("b1_sb", [128, 172], F32))
        b2_sb = E(nc.sbuf_tensor("b2_sb", [128, 32], F32))
        nw_sb = E(nc.sbuf_tensor("nw_sb", [128, 64], F32))
        d_sb = E(nc.sbuf_tensor("d_sb", [128, 64], F32))
        fc_sb = E(nc.sbuf_tensor("fc_sb", [128, 4], F32))
        for i, val in enumerate(FCONSTS):
            nc.const_aps.register(F32, val, fc_sb[:, i:i + 1])
        for dst, srcd in [(ones_c, onescol), (ones_rf, onesrow_f), (b1_sb, b1c),
                          (b2_sb, b2c), (nw_sb, nwc), (d_sb, Dc), (fc_sb, fconsts)]:
            P.gdma(dst[:], srcd[:, :])
        v_const = P.ld.n

        with contextlib.ExitStack() as d1:
            Ep = d1.enter_context
            yn_sb = Ep(nc.sbuf_tensor("yn_sb", [128, 64 * 1024], BF16))
            ysl = Ep(nc.sbuf_tensor("ysl", [128, 2 * 1024], BF16))
            xsl = Ep(nc.sbuf_tensor("xsl", [128, 2 * 1024], BF16))
            gsl = Ep(nc.sbuf_tensor("gsl", [128, 2 * 1024], BF16))
            sgs = Ep(nc.sbuf_tensor("sgs", [128, 2 * 1024], BF16))
            sqs = Ep(nc.sbuf_tensor("sqs", [128, 2 * 1024], BF16))
            rows = Ep(nc.sbuf_tensor("rows", [1, 2 * 1024], F32))
            rstd_r = Ep(nc.sbuf_tensor("rstd_r", [1, 1024], F32R))
            ss_p = Ep(nc.psum_tensor("ss_p", [1, 1024], F32))
            rb_p = Ep(nc.psum_tensor("rb_p", [128, 1024], F32))

            hist = {}
            for g in range(8):
                for k in range(8):
                    cc = g * 8 + k
                    yb = ysl[:, ts(cc % 2, 1024)]
                    xb = xsl[:, ts(cc % 2, 1024)]
                    gb = gsl[:, ts(cc % 2, 1024)]
                    sg = sgs[:, ts(cc % 2, 1024)]
                    sq = sqs[:, ts(cc % 2, 1024)]
                    w = [(P.dv, hist[cc - 2])] if cc >= 2 else []
                    v_y = P.sdma(yb, yT[ts(cc, 128), :], waits=w)
                    v_x = P.sdma(xb, xT[ts(cc, 128), :], waits=w)
                    v_g = P.sdma(gb, gateT[ts(cc, 128), :], waits=w)
                    wsi = [(P.mm, hist[('q', cc - 2)])] if cc >= 2 else []
                    v_si = P.scalar(lambda s, gb=gb, sg=sg: s.activation(sg, gb, AF.Silu),
                                    waits=[(P.st, v_g)] + wsi)
                    def f(v, yb=yb, xb=xb, sg=sg, cc=cc):
                        v.scalar_tensor_tensor(out=yb, in0=xb, scalar=d_sb[:, cc:cc + 1],
                                               in1=yb, op0=ALU.mult, op1=ALU.add)
                        v.drain()
                        return v.tensor_tensor(yn_sb[:, ts(cc, 1024)], yb, sg, op=ALU.mult)
                    hist[cc] = v_yg = P.vector(
                        f, waits=[(P.st, max(v_y, v_x)), (P.ac, v_si), (P.ld, v_const)])
                    v_sq = P.scalar(lambda s, sq=sq, cc=cc: s.activation(sq, yn_sb[:, ts(cc, 1024)],
                                                                         AF.Square),
                                    waits=[(P.dv, v_yg)])
                    def f(t, k=k, sq=sq):
                        t.matmul(ss_p[:, 0:512], ones_c[:], sq[:, 0:512],
                                 start=(k == 0), stop=(k == 7))
                        return t.matmul(ss_p[:, 512:1024], ones_c[:], sq[:, 512:1024],
                                        start=(k == 0), stop=(k == 7))
                    hist[('q', cc)] = P.tensor(f, waits=[(P.ac, v_sq)])
                v_ssall = P.mm.n
                def f(v):
                    return v.tensor_scalar(out=rows[:, 0:1024], in0=ss_p[:], scalar1=1.0 / 1024.0,
                                           scalar2=None, op0=ALU.mult)
                v_var = P.vector(f, waits=[(P.mm, v_ssall)])
                def f(s):
                    s.activation(rows[:, 1024:2048], rows[:, 0:1024], AF.Ln, bias=1e-5)
                    s.drain()
                    return s.activation(rstd_r[:], rows[:, 1024:2048], AF.Exp, scale=-0.5)
                v_rst = P.scalar(f, waits=[(P.dv, v_var)])
                wrb = [(P.dv, hist[('rb', g - 1)])] if g >= 1 else []
                def f(t):
                    t.matmul(rb_p[:, 0:512], ones_rf[:, :], rstd_r[:, 0:512], start=True, stop=True)
                    return t.matmul(rb_p[:, 512:1024], ones_rf[:, :], rstd_r[:, 512:1024],
                                    start=True, stop=True)
                v_rbm = P.tensor(f, waits=[(P.ac, v_rst)] + wrb)
                last = None
                for k in range(8):
                    cc = g * 8 + k
                    def f(v, cc=cc):
                        return v.scalar_tensor_tensor(out=yn_sb[:, ts(cc, 1024)],
                                                      in0=yn_sb[:, ts(cc, 1024)],
                                                      scalar=nw_sb[:, cc:cc + 1],
                                                      in1=rb_p[:], op0=ALU.mult, op1=ALU.mult)
                    last = P.vector(f, waits=[(P.mm, v_rbm)])
                hist[('rb', g)] = last
            v_ynall = P.dv.n

            P.barrier()

            with contextlib.ExitStack() as d2:
                E2 = d2.enter_context
                wbuf = E2(nc.sbuf_tensor("wbuf", [128, 2 * 64 * 128], BF16))
                x1sl = E2(nc.sbuf_tensor("x1sl", [128, 2 * 1024], BF16))
                x2sl = E2(nc.sbuf_tensor("x2sl", [128, 2 * 1024], F32))
                op_p = E2(nc.psum_tensor("op_p", [128, 1024], F32))
                h2 = {}
                for dc in range(KC):
                    wb = wbuf[:, ts(dc % 2, 64 * 128)]
                    x1b = x1sl[:, ts(dc % 2, 1024)]
                    x2b = x2sl[:, ts(dc % 2, 1024)]
                    wv_ = [(P.mm, h2[dc - 2])] if dc >= 2 else []
                    wv2 = [(P.dv, h2[('e', dc - 2)])] if dc >= 2 else []
                    wv3 = [(P.st, h2[('s', dc - 2)])] if dc >= 2 else []
                    v_w = P.gdma(wb.rearrange("p (c n) -> p c n", n=128),
                                 wopT[:, ts(dc, 128)].rearrange("(c p) n -> p c n", p=128),
                                 waits=wv_)
                    v_x1 = P.sdma(x1b, x1T[ts(dc, 128), 3:TQ], waits=wv2)
                    wprev = [(P.dv, h2[('e', dc - 1)])] if dc >= 1 else []
                    def f(t, wb=wb):
                        inst = None
                        for kk in range(64):
                            for (o, n) in [(0, 512), (512, 512)]:
                                inst = t.matmul(op_p[:, o:o + n], wb[:, ts(kk, 128)],
                                                yn_sb[:, kk * 1024 + o: kk * 1024 + o + n],
                                                start=(kk == 0), stop=(kk == 63))
                        return inst
                    h2[dc] = v_m = P.tensor(f, waits=[(P.ld, v_w), (P.dv, v_ynall)] + wprev)
                    h2[('e', dc)] = v_e = P.vector(
                        lambda v, x1b=x1b, x2b=x2b: v.tensor_tensor(x2b, op_p[:], x1b, op=ALU.add),
                        waits=[(P.mm, v_m), (P.st, v_x1)] + wv3)
                    h2[('s', dc)] = P.sdma(x2T[ts(dc, 128), :], x2b, waits=[(P.dv, v_e)])

        P.barrier()

        with contextlib.ExitStack() as d3:
            E3 = d3.enter_context
            p1_sb = E3(nc.sbuf_tensor("p1_sb", [128, 8 * 1024], BF16))

            with contextlib.ExitStack() as dp1:
                xn1_sb = dp1.enter_context(nc.sbuf_tensor("xn1_sb", [128, KC * 1024], BF16))
                wu1 = dp1.enter_context(nc.sbuf_tensor("wu1", [128, 2 * KC * 128], BF16))
                p1_p = dp1.enter_context(nc.psum_tensor("p1_p", [128, 1024], F32))
                for kc in range(KC):
                    P.sdma(xn1_sb[:, ts(kc, 1024)], xn1T[ts(kc, 128), 3:TQ])
                v_xn1 = P.st.n
                h3 = {}
                for rc in range(8):
                    wb = wu1[:, ts(rc % 2, KC * 128)]
                    wv_ = [(P.mm, h3[rc - 2])] if rc >= 2 else []
                    v_w = P.gdma(wb.rearrange("p (c n) -> p c n", n=128),
                                 U1w[:, ts(rc, 128)].rearrange("(c p) n -> p c n", p=128),
                                 waits=wv_)
                    wprev = [(P.dv, h3[('e', rc - 1)])] if rc >= 1 else []
                    def f(t, wb=wb):
                        inst = None
                        for kc in range(KC):
                            for (o, n) in [(0, 512), (512, 512)]:
                                inst = t.matmul(p1_p[:, o:o + n], wb[:, ts(kc, 128)],
                                                xn1_sb[:, kc * 1024 + o: kc * 1024 + o + n],
                                                start=(kc == 0), stop=(kc == KC - 1))
                        return inst
                    h3[rc] = v_m = P.tensor(f, waits=[(P.ld, v_w), (P.st, v_xn1)] + wprev)
                    h3[('e', rc)] = P.vector(
                        lambda v, rc=rc: v.tensor_copy(p1_sb[:, ts(rc, 1024)], p1_p[:]),
                        waits=[(P.mm, v_m)])
                v_p1 = P.dv.n

            P.barrier()

            act_sb = E3(nc.sbuf_tensor("act_sb", [128, 86 * 512], BF16))
            z_sb = E3(nc.sbuf_tensor("z_sb", [128, 8 * 512], BF16))
            wv1 = E3(nc.sbuf_tensor("wv1", [128, 4 * 8 * 128], BF16))
            wu2 = E3(nc.sbuf_tensor("wu2", [128, 2 * 8 * 128], BF16))
            wv2_ = E3(nc.sbuf_tensor("wv2_", [128, 2 * 8 * 128], BF16))
            sgf = E3(nc.sbuf_tensor("sgf", [128, 512], BF16))
            x2sl2 = E3(nc.sbuf_tensor("x2sl2", [128, 2 * 512], F32))
            x2rl = E3(nc.sbuf_tensor("x2rl", [128, 2 * 512], F32))

            for tt in range(2):
                to = tt * 512
                with contextlib.ExitStack() as dh:
                    hg_p = dh.enter_context(nc.psum_tensor(f"hg_p{tt}", [128, 512], F32))
                    hu_p = dh.enter_context(nc.psum_tensor(f"hu_p{tt}", [128, 512], F32))
                    h4 = {}
                    for fc in range(86):
                        wbg = wv1[:, ts((2 * fc) % 4, 8 * 128)]
                        wbu = wv1[:, ts((2 * fc + 1) % 4, 8 * 128)]
                        wv_ = [(P.mm, h4[fc - 2])] if fc >= 2 else []
                        v_wg = P.gdma(wbg.rearrange("p (c n) -> p c n", n=128),
                                      V1w[:, ts(fc, 128)].rearrange("(c p) n -> p c n", p=128),
                                      waits=wv_)
                        v_wu = P.gdma(wbu.rearrange("p (c n) -> p c n", n=128),
                                      V1w[:, ts(fc + 86, 128)].rearrange("(c p) n -> p c n", p=128),
                                      waits=wv_)
                        wprev = [(P.ac, h4[('a', fc - 1)])] if fc >= 1 else []
                        wprev2 = [(P.dv, h4[('v', fc - 1)])] if fc >= 1 else []
                        def f(t, wbg=wbg, wbu=wbu, to=to):
                            inst = None
                            for rc in range(8):
                                t.matmul(hg_p[:], wbg[:, ts(rc, 128)],
                                         p1_sb[:, rc * 1024 + to: rc * 1024 + to + 512],
                                         start=(rc == 0), stop=(rc == 7))
                            for rc in range(8):
                                inst = t.matmul(hu_p[:], wbu[:, ts(rc, 128)],
                                                p1_sb[:, rc * 1024 + to: rc * 1024 + to + 512],
                                                start=(rc == 0), stop=(rc == 7))
                            return inst
                        h4[fc] = v_m = P.tensor(f, waits=[(P.ld, max(v_wg, v_wu)), (P.dv, v_p1)]
                                                + wprev + wprev2)
                        h4[('a', fc)] = v_sg = P.scalar(
                            lambda s, fc=fc: s.activation(sgf[:], hg_p[:], AF.Silu,
                                                          bias=b1_sb[:, fc:fc + 1]),
                            waits=[(P.mm, v_m)])
                        h4[('v', fc)] = P.vector(
                            lambda v, fc=fc: v.scalar_tensor_tensor(
                                out=act_sb[:, ts(fc, 512)], in0=hu_p[:],
                                scalar=b1_sb[:, 86 + fc: 87 + fc], in1=sgf[:],
                                op0=ALU.add, op1=ALU.mult),
                            waits=[(P.ac, v_sg)])
                    v_act = P.dv.n

                P.barrier()

                with contextlib.ExitStack() as dz:
                    z_p = dz.enter_context(nc.psum_tensor(f"z_p{tt}", [128, 8 * 512], F32))
                    h5 = {}
                    for fc in range(86):
                        wb = wu2[:, ts(fc % 2, 8 * 128)]
                        wv_ = [(P.mm, h5[fc - 2])] if fc >= 2 else []
                        v_w = P.gdma(wb.rearrange("p (c n) -> p c n", n=128),
                                     U2w[ts(fc, 128), :].rearrange("p (c n) -> p c n", n=128),
                                     waits=wv_)
                        def f(t, wb=wb, fc=fc):
                            inst = None
                            for rc in range(8):
                                inst = t.matmul(z_p[:, ts(rc, 512)], wb[:, ts(rc, 128)],
                                                act_sb[:, ts(fc, 512)],
                                                start=(fc == 0), stop=(fc == 85))
                            return inst
                        h5[fc] = P.tensor(f, waits=[(P.ld, v_w), (P.dv, v_act)])
                    v_zmm = P.mm.n
                    v_ze = P.vector(lambda v: v.tensor_copy(z_sb[:], z_p[:]),
                                    waits=[(P.mm, v_zmm)])

                with contextlib.ExitStack() as dz2:
                    op2_p = dz2.enter_context(nc.psum_tensor(f"op2_{tt}", [128, 512], F32))
                    v_allst = P.st.n
                    h6 = {}
                    for dc in range(KC):
                        wb = wv2_[:, ts(dc % 2, 8 * 128)]
                        x2b = x2sl2[:, ts(dc % 2, 512)]
                        x2r = x2rl[:, ts(dc % 2, 512)]
                        wv_ = [(P.mm, h6[dc - 2])] if dc >= 2 else []
                        wv2x = [(P.dv, h6[('e', dc - 2)])] if dc >= 2 else []
                        wv3x = [(P.st, h6[('s', dc - 2)])] if dc >= 2 else []
                        v_w = P.gdma(wb.rearrange("p (c n) -> p c n", n=128),
                                     V2w[:, ts(dc, 128)].rearrange("(c p) n -> p c n", p=128),
                                     waits=wv_)
                        v_xr = P.sdma(x2r, x2T[ts(dc, 128), to:to + 512],
                                      waits=wv2x + [(P.st, v_allst)])
                        wprev = [(P.dv, h6[('e', dc - 1)])] if dc >= 1 else []
                        def f(t, wb=wb):
                            inst = None
                            for rc in range(8):
                                inst = t.matmul(op2_p[:], wb[:, ts(rc, 128)],
                                                z_sb[:, ts(rc, 512)],
                                                start=(rc == 0), stop=(rc == 7))
                            return inst
                        h6[dc] = v_m = P.tensor(f, waits=[(P.ld, v_w), (P.dv, v_ze)] + wprev)
                        def f(v, x2b=x2b, x2r=x2r, dc=dc):
                            return v.scalar_tensor_tensor(out=x2b, in0=op2_p[:],
                                                          scalar=b2_sb[:, dc:dc + 1], in1=x2r,
                                                          op0=ALU.add, op1=ALU.add)
                        h6[('e', dc)] = v_e = P.vector(f, waits=[(P.mm, v_m), (P.st, v_xr),
                                                                 (P.ld, v_const)] + wv3x)
                        h6[('s', dc)] = P.sdma(x2T[ts(dc, 128), to:to + 512], x2b,
                                               waits=[(P.dv, v_e)])
                P.barrier()

        vfin = P.st.n
        sched.on("sync", lambda s, vfin=vfin: s.wait_ge(P.st.h, vfin))
        with nc.allow_low_precision("bf16 by design"):
            sched.emit(nc)
    return nc


def host_inputs_lD(inp, lb_outs, y_fm_list):
    shared = {
        "wopT": np.ascontiguousarray(inp["L1_attn_out_proj_w"].T),
        "U1w": inp["L1_ffn_U1"],
        "V1w": inp["L1_ffn_V1"],
        "U2w": inp["L1_ffn_U2"],
        "V2w": inp["L1_ffn_V2"],
        "b1c": np.ascontiguousarray(inp["L1_ffn_b1"].reshape(172, 128).T),
        "b2c": np.ascontiguousarray(inp["L1_ffn_b2"].reshape(32, 128).T),
        "nwc": np.ascontiguousarray(inp["L1_attn_norm_w"].reshape(64, 128).T),
        "Dc": np.ascontiguousarray(np.repeat(inp["L1_attn_D"], 64).reshape(64, 128).T),
        "onescol": np.ones((128, 1), BF),
        "onesrow_f": np.ones((1, 128), np.float32),
        "fconsts": np.tile(np.array([FCONSTS], np.float32), (128, 1)),
    }
    maps = []
    for core in range(8):
        m = dict(shared)
        m["yT"] = y_fm_list[core]
        m["xT"] = lb_outs[core]["xT"]
        m["gateT"] = lb_outs[core]["gateT"]
        m["xn1T"] = lb_outs[core]["xn1T"]
        m["x1T"] = lb_outs[core]["x1T"]
        maps.append(m)
    return maps



def kernel(**inputs):
    inp = {k: np.asarray(v) for k, v in inputs.items()}

    ncA = build_lA()
    resA = run_bass_kernel_spmd(ncA, host_inputs_lA(inp), core_ids=list(range(8)))
    x1_list = [np.asarray(resA.results[c]["x1T"]) for c in range(8)]

    ncB = build_lB()
    resB = run_bass_kernel_spmd(ncB, host_inputs_lB(inp, x1_list), core_ids=list(range(8)))
    keysB = ["xn1T", "gateT", "xT", "BT", "CT", "dtF", "dtB", "AdtF", "cumrelF"]
    lb = [{k: np.asarray(resB.results[c][k]) for k in keysB} for c in range(8)]
    for c in range(8):
        lb[c]["x1T"] = x1_list[c]

    ncC = build_lC()
    mapsC, pair_table = host_inputs_lC(lb)
    resC = run_bass_kernel_spmd(ncC, mapsC, core_ids=list(range(8)))

    y_fm_list = []
    for core in range(8):
        b, cc = core // 4, core % 4
        y_fm = np.empty((8192, 1024), BF)
        for g in range(8):
            owner = b * 4 + g // 2
            p = g % 2
            ytm = np.asarray(resC.results[owner][f"yTM{p}"])
            y_fm[g * 1024:(g + 1) * 1024, :] = ytm[cc * 1024:(cc + 1) * 1024, :].T
        y_fm_list.append(np.ascontiguousarray(y_fm))

    ncD = build_lD()
    resD = run_bass_kernel_spmd(ncD, host_inputs_lD(inp, lb, y_fm_list), core_ids=list(range(8)))

    out = np.empty((2, 4096, DM), np.float32)
    for core in range(8):
        b, cc = core // 4, core % 4
        out[b, cc * 1024:(cc + 1) * 1024, :] = np.asarray(resD.results[core]["x2T"]).T
    return out



# version 05
# speedup vs baseline: 1.1263x; 1.0044x over previous
import contextlib
import math
import numpy as np
import ml_dtypes

import concourse.bass as bass
import concourse.mybir as mybir
from concourse.bass_utils import run_bass_kernel_spmd

F32 = mybir.dt.float32
F32R = mybir.dt.float32r
BF16 = mybir.dt.bfloat16
AF = mybir.ActivationFunctionType
ALU = mybir.AluOpType
AXX = mybir.AxisListType.X
BF = ml_dtypes.bfloat16

DM = 4096
TOK = 1024
TQ = 1027
TKV = 1664
KC = 32
NEG = -1e30
FCONSTS = (1e-5, -0.5, 0.5, 2.0)

ENGINES = ("sync", "scalar", "vector", "tensor", "gpsimd")


class Sched:
    def __init__(self):
        self.steps = {e: [] for e in ENGINES}

    def on(self, eng, fn):
        self.steps[eng].append(fn)

    def emit(self, nc):
        with nc.Block() as block:
            for name in ENGINES:
                def mk(nm):
                    def body(eng):
                        for fn in self.steps[nm]:
                            fn(eng)
                    return body
                getattr(block, name)(mk(name))


class Sem:
    def __init__(self, ctx, nc, name):
        self.h = ctx.enter_context(nc.semaphore(name))
        self.n = 0

    def next(self, k=1):
        self.n += k
        return self.n


class Prog:

    def __init__(self, nc, ctx, sched):
        self.nc = nc
        self.sched = sched
        self.ld = Sem(ctx, nc, "s_ld")
        self.st = Sem(ctx, nc, "s_st")
        self.mm = Sem(ctx, nc, "s_mm")
        self.dv = Sem(ctx, nc, "s_dv")
        self.ac = Sem(ctx, nc, "s_ac")
        self.po = Sem(ctx, nc, "s_po")
        self.ph = Sem(ctx, nc, "s_ph")
        self.phk = 0

    def _wrap(self, fn, waits, incs, drain=False):
        def g(eng):
            for (s, v) in waits:
                eng.wait_ge(s.h, v)
            if drain:
                eng.drain()
            inst = fn(eng)
            if drain:
                eng.drain()
                for (s, k) in incs:
                    eng.sem_inc(s.h, k)
            else:
                for (s, k) in incs:
                    inst.then_inc(s.h, k)
        return g

    def tensor(self, fn, waits=()):
        v = self.mm.next()
        self.sched.on("tensor", self._wrap(fn, waits, [(self.mm, 1)]))
        return v

    def vector(self, fn, waits=(), ninc=1):
        v = self.dv.next(ninc)
        self.sched.on("vector", self._wrap(fn, waits, [(self.dv, ninc)], drain=True))
        return v

    def scalar(self, fn, waits=(), ninc=1):
        v = self.ac.next(ninc)
        self.sched.on("scalar", self._wrap(fn, waits, [(self.ac, ninc)], drain=True))
        return v

    def pool(self, fn, waits=(), ninc=1):
        v = self.po.next(ninc)
        self.sched.on("gpsimd", self._wrap(fn, waits, [(self.po, ninc)], drain=True))
        return v

    def mksem(self, ctx, name):
        return Sem(ctx, self.nc, name)

    def gdma(self, dst, src, waits=(), sem=None):
        sem = sem or self.ld
        sem._dma = True
        v = sem.next(16)
        def fn(g):
            return g.dma_start(dst, src)
        self.sched.on("gpsimd", self._wrap(fn, waits, [(sem, 16)]))
        return v

    def sdma(self, dst, src, waits=(), sem=None):
        sem = sem or self.st
        sem._dma = True
        v = sem.next(16)
        def fn(s):
            return s.dma_start(dst, src)
        self.sched.on("sync", self._wrap(fn, waits, [(sem, 16)]))
        return v

    def barrier(self):
        self.phk += 1
        tgt = 5 * self.phk
        self.ph.n = tgt
        for e in ENGINES:
            def fn(eng, tgt=tgt):
                eng.sem_inc(self.ph.h, 1)
                eng.wait_ge(self.ph.h, tgt)
            self.sched.on(e, fn)


def ts(i, n):
    return slice(i * n, (i + 1) * n)



def build_lA(debug=False):
    nc = bass.Bass("TRN2", target_bir_lowering=False)
    x0T = nc.dram_tensor("x0T", [DM, TKV], F32, kind="ExternalInput")
    wqT = nc.dram_tensor("wqT", [DM, DM], F32, kind="ExternalInput")
    wkT = nc.dram_tensor("wkT", [DM, 1024], F32, kind="ExternalInput")
    wvT = nc.dram_tensor("wvT", [DM, 1024], F32, kind="ExternalInput")
    woT = nc.dram_tensor("woT", [DM, DM], F32, kind="ExternalInput")
    rwT = nc.dram_tensor("rwT", [DM, 8], F32, kind="ExternalInput")
    rbrow = nc.dram_tensor("rbrow", [1, 8], F32, kind="ExternalInput")
    AallT = nc.dram_tensor("AallT", [DM, 128], F32, kind="ExternalInput")
    BmF = nc.dram_tensor("BmF", [128, DM], F32, kind="ExternalInput")
    i128b = nc.dram_tensor("i128b", [128, 128], BF16, kind="ExternalInput")
    onescol = nc.dram_tensor("onescol", [128, 1], BF16, kind="ExternalInput")
    onesrow = nc.dram_tensor("onesrow", [1, 512], BF16, kind="ExternalInput")
    negones = nc.dram_tensor("negones", [1, 128], F32, kind="ExternalInput")
    onesrow_f = nc.dram_tensor("onesrow_f", [1, 128], F32, kind="ExternalInput")
    maskSelf = nc.dram_tensor("maskSelf", [128, 128], BF16, kind="ExternalInput")
    maskFar = nc.dram_tensor("maskFar", [128, 128], BF16, kind="ExternalInput")
    fullmaskT = nc.dram_tensor("fullmaskT", [1, 13 * 128], BF16, kind="ExternalInput")
    maskHalo = nc.dram_tensor("maskHalo", [128, 15], F32, kind="ExternalInput")
    cntQ = nc.dram_tensor("cntQ", [1, 1024], F32, kind="ExternalInput")
    cntH = nc.dram_tensor("cntH", [1, 3], F32, kind="ExternalInput")
    fconsts = nc.dram_tensor("fconsts", [128, 4], F32, kind="ExternalInput")
    i8exp = nc.dram_tensor("i8exp", [8, 128], BF16, kind="ExternalInput")

    x1T = nc.dram_tensor("x1T", [DM, TQ], BF16, kind="ExternalOutput")

    xn0_d = nc.dram_tensor("xn0_d", [DM, TKV], BF16)
    qT_d = nc.dram_tensor("qT_d", [DM, TQ], BF16)
    moe_d = nc.dram_tensor("moe_d", [DM, 1152], BF16)

    sched = Sched()
    NT4 = [(0, 512), (512, 512), (1024, 512), (1536, 128)]
    NTQ = [(0, 512), (512, 512), (1024, 3)]

    with contextlib.ExitStack() as ctx:
        E = ctx.enter_context
        P = Prog(nc, ctx, sched)
        sm_x0 = P.ld
        sm_wb = P.ld
        sm_sp = P.st
        sm_qb = P.st
        sm_wo = P.ld
        sm_xr = P.ld
        sm_ms = P.st
        sm_xs = P.st

        i128_sb = E(nc.sbuf_tensor("i128_sb", [128, 128], BF16))
        ones_c = E(nc.sbuf_tensor("ones_c", [128, 1], BF16))
        ones_r = E(nc.sbuf_tensor("ones_r", [1, 512], BF16))
        nones_r = E(nc.sbuf_tensor("nones_r", [1, 128], F32R))
        nones_f = E(nc.sbuf_tensor("nones_f", [1, 128], F32))
        ones_rf = E(nc.sbuf_tensor("ones_rf", [1, 128], F32R))
        mself_sb = E(nc.sbuf_tensor("mself_sb", [128, 128], BF16))
        mfar_sb = E(nc.sbuf_tensor("mfar_sb", [128, 128], BF16))
        fmask_sb = E(nc.sbuf_tensor("fmask_sb", [1, 13 * 128], BF16))
        mhalo_sb = E(nc.sbuf_tensor("mhalo_sb", [128, 15], F32))
        cntq_sb = E(nc.sbuf_tensor("cntq_sb", [1, 1024], F32))
        cnth_sb = E(nc.sbuf_tensor("cnth_sb", [1, 3], F32))
        qmax2 = E(nc.sbuf_tensor("qmax2", [1, 32], F32))
        kmax2 = E(nc.sbuf_tensor("kmax2", [1, 8], F32))
        negC_cols = E(nc.sbuf_tensor("negC_cols", [128, 32], F32))
        negC_row = E(nc.sbuf_tensor("negC_row", [1, 32], F32R))
        encrow = E(nc.sbuf_tensor("encrow", [1, 32], F32))
        fc_sb = E(nc.sbuf_tensor("fc_sb", [128, 4], F32))
        i8e_sb = E(nc.sbuf_tensor("i8e_sb", [8, 128], BF16))
        for i, val in enumerate(FCONSTS):
            nc.const_aps.register(F32, val, fc_sb[:, i:i + 1])

        for dst, src in [(i128_sb, i128b), (ones_c, onescol), (ones_r, onesrow),
                         (nones_r, negones), (ones_rf, onesrow_f),
                         (mself_sb, maskSelf), (mfar_sb, maskFar),
                         (fmask_sb, fullmaskT), (mhalo_sb, maskHalo),
                         (cntq_sb, cntQ), (cnth_sb, cntH), (fc_sb, fconsts),
                         (i8e_sb, i8exp), (nones_f, negones)]:
            P.gdma(dst[:], src[:, :])

        with contextlib.ExitStack() as p1:
            Ep = p1.enter_context
            x0slab = Ep(nc.sbuf_tensor("x0slab", [128, 2 * TKV], BF16))
            sqslab = Ep(nc.sbuf_tensor("sqslab", [128, 2 * TKV], BF16))
            rows = Ep(nc.sbuf_tensor("rows", [1, 4 * TKV], F32))
            rstd_r = Ep(nc.sbuf_tensor("rstd_r", [1, TKV], F32R))
            negmr_r = Ep(nc.sbuf_tensor("negmr_r", [1, TKV], F32R))
            rstd_b = Ep(nc.sbuf_tensor("rstd_b", [128, TKV], F32))
            negmr_b = Ep(nc.sbuf_tensor("negmr_b", [128, TKV], F32))

            mean_r = rows[:, 0:TKV]
            m2_r = rows[:, TKV:2 * TKV]
            var_r = rows[:, 2 * TKV:3 * TKV]
            lnv_r = rows[:, 3 * TKV:4 * TKV]

            with contextlib.ExitStack() as p1s:
                msum_p = p1s.enter_context(nc.psum_tensor("msum_p", [1, TKV], F32))
                sqsum_p = p1s.enter_context(nc.psum_tensor("sqsum_p", [1, TKV], F32))

                mm_hist = {}
                for kc in range(KC):
                    buf = x0slab[:, ts(kc % 2, TKV)]
                    sq = sqslab[:, ts(kc % 2, TKV)]
                    w = [(P.mm, mm_hist[kc - 2])] if kc >= 2 else []
                    v_ld = P.gdma(buf, x0T[ts(kc, 128), :], waits=w, sem=sm_x0)
                    v_sq = P.scalar(lambda s, buf=buf, sq=sq: s.activation(sq, buf, AF.Square),
                                    waits=[(sm_x0, v_ld)] + w)
                    def f(t, kc=kc, buf=buf, sq=sq):
                        for (o, n) in NT4:
                            t.matmul(msum_p[:, o:o + n], ones_c[:], buf[:, o:o + n],
                                     start=(kc == 0), stop=(kc == KC - 1))
                        inst = None
                        for (o, n) in NT4:
                            inst = t.matmul(sqsum_p[:, o:o + n], ones_c[:], sq[:, o:o + n],
                                            start=(kc == 0), stop=(kc == KC - 1))
                        return inst
                    mm_hist[kc] = P.tensor(f, waits=[(sm_x0, v_ld), (P.ac, v_sq)])
                v_stats = P.mm.n

                def f(v):
                    v.tensor_scalar(out=mean_r, in0=msum_p[:], scalar1=1.0 / DM, scalar2=None, op0=ALU.mult)
                    v.drain()
                    v.tensor_tensor(m2_r, mean_r, mean_r, op=ALU.mult)
                    v.drain()
                    return v.scalar_tensor_tensor(out=var_r, in0=sqsum_p[:], scalar=1.0 / DM, in1=m2_r,
                                                  op0=ALU.mult, op1=ALU.subtract)
                v_rows = P.vector(f, waits=[(P.mm, v_stats)])
                def f(s):
                    s.activation(lnv_r, var_r, AF.Ln, bias=1e-5)
                    s.drain()
                    return s.activation(rstd_r[:], lnv_r, AF.Exp, scale=-0.5)
                v_rstd = P.scalar(f, waits=[(P.dv, v_rows)])
                v_negmr = P.vector(
                    lambda v: v.scalar_tensor_tensor(out=negmr_r[:], in0=mean_r, scalar=-1.0,
                                                     in1=rstd_r[:], op0=ALU.mult, op1=ALU.mult),
                    waits=[(P.ac, v_rstd)])

            with contextlib.ExitStack() as p1b:
                bc_p = p1b.enter_context(nc.psum_tensor("bc_p", [128, 512], F32))
                prev_bc = None
                for row, bcast in [(rstd_r, rstd_b), (negmr_r, negmr_b)]:
                    for (o, n) in NT4:
                        wb_ = [(P.dv, prev_bc)] if prev_bc is not None else []
                        v_b = P.tensor(
                            lambda t, row=row, o=o, n=n: t.matmul(
                                bc_p[:, 0:n], ones_rf[:, :], row[:, o:o + n].bitcast(F32R),
                                start=True, stop=True),
                            waits=[(P.dv, v_negmr)] + wb_)
                        prev_bc = P.vector(
                            lambda v, bcast=bcast, o=o, n=n: v.tensor_copy(bcast[:, o:o + n], bc_p[:, 0:n]),
                            waits=[(P.mm, v_b)])
                v_bc = P.dv.n

                dv_hist = {}
                st_hist = {}
                for kc in range(KC):
                    buf = x0slab[:, ts(kc % 2, TKV)]
                    sq = sqslab[:, ts(kc % 2, TKV)]
                    w = [(P.dv, dv_hist[kc - 2]), (sm_sp, st_hist[kc - 2])] if kc >= 2 else [(P.mm, v_stats)]
                    v_ld = P.gdma(buf, x0T[ts(kc, 128), :], waits=w, sem=sm_x0)
                    def f(v, buf=buf, sq=sq):
                        v.tensor_tensor(sq, buf, rstd_b[:], op=ALU.mult)
                        v.drain()
                        return v.tensor_tensor(buf, sq, negmr_b[:], op=ALU.add)
                    dv_hist[kc] = P.vector(f, waits=[(sm_x0, v_ld), (P.dv, v_bc)])
                    st_hist[kc] = P.sdma(xn0_d[ts(kc, 128), :], buf, waits=[(P.dv, dv_hist[kc])],
                                         sem=sm_sp)

        P.barrier()

        kT_sb = E(nc.sbuf_tensor("kT_sb", [128, 8 * TKV], BF16))
        v_sb = E(nc.sbuf_tensor("v_sb", [128, 13 * 1024], BF16))

        with contextlib.ExitStack() as p2:
            Ep = p2.enter_context
            xn0_sb = Ep(nc.sbuf_tensor("xn0_sb", [128, KC * TKV], BF16))
            oslab = Ep(nc.sbuf_tensor("oslab", [128, TKV], BF16))
            sqs = Ep(nc.sbuf_tensor("sqs", [128, TKV], BF16))

            v_sp1 = sm_sp.n
            for kc in range(KC):
                P.sdma(xn0_sb[:, ts(kc, TKV)], xn0_d[ts(kc, 128), :], waits=[(sm_sp, v_sp1)])
            v_xn0 = P.st.n

            p2w = contextlib.ExitStack()
            wbuf = p2w.enter_context(nc.sbuf_tensor("wbuf", [128, 2 * KC * 128], BF16))
            out_p = p2w.enter_context(nc.psum_tensor("out_p", [128, TKV], F32))
            tp_p = p2w.enter_context(nc.psum_tensor("tp_p", [128, 128], BF16))
            tp4_p = p2w.enter_context(nc.psum_tensor("tp4_p", [128, 512], BF16))
            sqr_p = p2w.enter_context(nc.psum_tensor("sqr_p", [1, 512], F32))
            mxp = p2w.enter_context(nc.sbuf_tensor("mxp", [1, 8], F32))
            kweight_hist = {}
            prev_ev = None
            prev_sq = None
            prev_sqm = None
            for hk in range(8):
                wb = wbuf[:, ts(hk % 2, KC * 128)]
                w = [(P.mm, kweight_hist[hk - 2])] if hk >= 2 else []
                v_w = P.gdma(wb.rearrange("p (c n) -> p c n", n=128),
                             wkT[:, ts(hk, 128)].rearrange("(c p) n -> p c n", p=128), waits=w,
                             sem=sm_wb)
                waits = [(sm_wb, v_w), (P.st, v_xn0)]
                if prev_ev is not None:
                    waits.append((P.dv, prev_ev))
                def f(t, wb=wb):
                    inst = None
                    for kc in range(KC):
                        for (o, n) in NT4:
                            inst = t.matmul(out_p[:, o:o + n], wb[:, ts(kc, 128)],
                                            xn0_sb[:, kc * TKV + o: kc * TKV + o + n],
                                            start=(kc == 0), stop=(kc == KC - 1))
                    return inst
                kweight_hist[hk] = v_k = P.tensor(f, waits=waits)
                prev_ev = v_ev = P.vector(
                    lambda v, hk=hk: v.tensor_copy(kT_sb[:, ts(hk, TKV)], out_p[:]),
                    waits=[(P.mm, v_k)])
                wsq = [(P.mm, prev_sqm)] if prev_sqm is not None else []
                v_sq = P.scalar(lambda s, hk=hk: s.activation(sqs[:], kT_sb[:, ts(hk, TKV)], AF.Square),
                                waits=[(P.dv, v_ev)] + wsq)
                for i, (o, n) in enumerate(NT4):
                    wsm = [(P.ac, v_sq)]
                    if prev_sq is not None:
                        wsm.append((P.dv, prev_sq))
                    v_sqm = P.tensor(
                        lambda t, o=o, n=n: t.matmul(sqr_p[:, 0:n], ones_c[:], sqs[:, o:o + n],
                                                     start=True, stop=True),
                        waits=wsm)
                    prev_sqm = v_sqm
                    prev_sq = P.vector(
                        lambda v, i=i, n=n: v.reduce_max(mxp[:, i:i + 1], sqr_p[:, 0:n], axis=AXX),
                        waits=[(P.mm, v_sqm)])
                prev_sq = P.vector(
                    lambda v, hk=hk: v.reduce_max(kmax2[:, hk:hk + 1], mxp[:, 0:4], axis=AXX),
                    waits=[(P.dv, prev_sq)])

            vweight_hist = {}
            prev_tp = None
            for hv in range(8):
                wb = wbuf[:, ts(hv % 2, KC * 128)]
                w = [(P.mm, vweight_hist[hv - 2])] if hv >= 2 else [(P.mm, kweight_hist[7])]
                v_w = P.gdma(wb.rearrange("p (c n) -> p c n", n=128),
                             wvT[:, ts(hv, 128)].rearrange("(c p) n -> p c n", p=128), waits=w,
                             sem=sm_wb)
                waits = [(sm_wb, v_w)]
                if prev_ev is not None:
                    waits.append((P.dv, prev_ev))
                    prev_ev = None
                def f(t, wb=wb):
                    inst = None
                    for kc in range(KC):
                        for (o, n) in NT4:
                            inst = t.matmul(out_p[:, o:o + n], wb[:, ts(kc, 128)],
                                            xn0_sb[:, kc * TKV + o: kc * TKV + o + n],
                                            start=(kc == 0), stop=(kc == KC - 1))
                    return inst
                vweight_hist[hv] = v_m = P.tensor(f, waits=waits)
                v_ev = P.vector(lambda v: v.tensor_copy(oslab[:], out_p[:]), waits=[(P.mm, v_m)])
                for tg in range(4):
                    tcs = list(range(tg * 4, min(tg * 4 + 4, 13)))
                    wtp = [(P.dv, prev_tp)] if prev_tp is not None else []
                    def ft(t, tcs=tcs):
                        inst = None
                        for i, tc in enumerate(tcs):
                            inst = t.transpose(tp4_p[:, ts(i, 128)], oslab[:, ts(tc, 128)],
                                               i128_sb[:])
                        return inst
                    v_t = P.tensor(ft, waits=[(P.dv, v_ev)] + wtp)
                    def fe(v, tcs=tcs, hv=hv):
                        inst = None
                        for i, tc in enumerate(tcs):
                            inst = v.tensor_copy(
                                v_sb[:, tc * 1024 + hv * 128: tc * 1024 + (hv + 1) * 128],
                                tp4_p[:, ts(i, 128)])
                        return inst
                    prev_tp = P.vector(fe, waits=[(P.mm, v_t)])

            qweight_hist = {}
            prev_evq = None
            for hq in range(KC):
                wb = wbuf[:, ts(hq % 2, KC * 128)]
                w = [(P.mm, qweight_hist[hq - 2])] if hq >= 2 else [(P.mm, vweight_hist[7])]
                v_w = P.gdma(wb.rearrange("p (c n) -> p c n", n=128),
                             wqT[:, ts(hq, 128)].rearrange("(c p) n -> p c n", p=128), waits=w,
                             sem=sm_wb)
                waits = [(sm_wb, v_w)]
                if prev_tp is not None:
                    waits.append((P.dv, prev_tp))
                    prev_tp = None
                if prev_evq is not None:
                    waits.append((P.dv, prev_evq))
                def f(t, wb=wb):
                    inst = None
                    for kc in range(KC):
                        for (o, n) in NTQ:
                            inst = t.matmul(out_p[:, o:o + n], wb[:, ts(kc, 128)],
                                            xn0_sb[:, kc * TKV + 637 + o: kc * TKV + 637 + o + n],
                                            start=(kc == 0), stop=(kc == KC - 1))
                    return inst
                qweight_hist[hq] = v_m = P.tensor(f, waits=waits)
                v_ev = P.vector(lambda v: v.tensor_scalar(out=oslab[:, 0:TQ], in0=out_p[:, 0:TQ],
                                                          scalar1=1.0 / math.sqrt(128.0),
                                                          scalar2=None, op0=ALU.mult),
                                waits=[(P.mm, v_m)])
                P.sdma(qT_d[ts(hq, 128), :], oslab[:, 0:TQ], waits=[(P.dv, v_ev)])
                v_sq = P.scalar(lambda s: s.activation(sqs[:, 0:TQ], oslab[:, 0:TQ], AF.Square),
                                waits=[(P.dv, v_ev)])
                def f(t):
                    inst = None
                    for (o, n) in NTQ:
                        inst = t.matmul(out_p[0:1, o:o + n], ones_c[:], sqs[:, o:o + n],
                                        start=True, stop=True)
                    return inst
                v_sqm = P.tensor(f, waits=[(P.ac, v_sq)])
                prev_evq = P.vector(
                    lambda v, hq=hq: v.reduce_max(qmax2[:, hq:hq + 1], out_p[0:1, 0:TQ], axis=AXX),
                    waits=[(P.mm, v_sqm)])

            p2w.close()
            P.barrier()

            with contextlib.ExitStack() as p3:
                E3 = p3.enter_context
                rw_sb = E3(nc.sbuf_tensor("rw_sb", [128, KC * 8], BF16))
                rb_sb = E3(nc.sbuf_tensor("rb_sb", [1, 8], BF16))
                aall_sb = E3(nc.sbuf_tensor("aall_sb", [128, KC * 128], BF16))
                bmf_sb = E3(nc.sbuf_tensor("bmf_sb", [128, DM], BF16))
                h_sb = E3(nc.sbuf_tensor("h_sb", [128, 1152], BF16))
                hw_sb = E3(nc.sbuf_tensor("hw_sb", [128, 1152], BF16))
                wtT_sb = E3(nc.sbuf_tensor("wtT_sb", [8, 128], BF16))
                w_sb = E3(nc.sbuf_tensor("w_sb", [128, 8], BF16))
                e8 = E3(nc.sbuf_tensor("e8", [128, 8], F32))
                eqf = E3(nc.sbuf_tensor("eqf", [128, 8], F32))
                s8f = E3(nc.sbuf_tensor("s8f", [128, 8], F32))
                se8 = E3(nc.sbuf_tensor("se8", [128, 8], F32))
                s8 = E3(nc.sbuf_tensor("s8", [128, 8], BF16))
                sc1 = E3(nc.sbuf_tensor("sc1", [128, 8], F32))
                logit_p = E3(nc.psum_tensor("logit_p", [128, 8], F32))
                wdbg_sb = E3(nc.sbuf_tensor("wdbg_sb", [128, 72], BF16))
                wt_p = E3(nc.psum_tensor("wt_p", [8, 128], BF16))
                wexp_p = E3(nc.psum_tensor("wexp_p", [128, 128], F32))
                h_p = E3(nc.psum_tensor("h_p", [128, 128], F32))
                moe_p = E3(nc.psum_tensor("moe_p", [128, 1152], F32))

                P.gdma(rw_sb[:].rearrange("p (c n) -> p c n", n=8),
                       rwT[:, :].rearrange("(c p) n -> p c n", p=128))
                P.gdma(rb_sb[:], rbrow[:, :])
                P.gdma(aall_sb[:].rearrange("p (c n) -> p c n", n=128),
                       AallT[:, :].rearrange("(c p) n -> p c n", p=128))
                P.gdma(bmf_sb[:], BmF[:, :])
                v_wmoe = P.ld.n

                m1c, nm1c, m2c, ntc, tltc, denc, recc = [sc1[:, i:i + 1] for i in range(7)]
                prev_lg = None
                prev_wexp = None
                prev_h = None
                for tt in range(9):
                    col0 = 512 + tt * 128
                    waits = [(P.ld, v_wmoe), (P.st, v_xn0)]
                    if prev_lg is not None:
                        waits.append((P.dv, prev_lg))
                    def f(t, col0=col0):
                        for kc in range(KC):
                            t.matmul(logit_p[:], xn0_sb[:, kc * TKV + col0: kc * TKV + col0 + 128],
                                     rw_sb[:, ts(kc, 8)], start=(kc == 0), stop=False)
                        return t.matmul(logit_p[:], ones_r[:, 0:128], rb_sb[:],
                                        start=False, stop=True)
                    v_lg = P.tensor(f, waits=waits)
                    def f(v):
                        v.reduce_max(m1c, logit_p[:], axis=AXX)
                        v.reduce_max(nm1c, logit_p[:], axis=AXX, negate=True)
                        v.drain()
                        v.tensor_scalar(out=eqf[:], in0=logit_p[:], scalar1=m1c, scalar2=None,
                                        op0=ALU.is_equal)
                        v.drain()
                        v.reduce_sum(ntc, eqf[:], axis=AXX)
                        v.scalar_tensor_tensor(out=e8[:], in0=eqf[:], scalar=NEG, in1=logit_p[:],
                                               op0=ALU.mult, op1=ALU.add)
                        v.drain()
                        v.reduce_max(m2c, e8[:], axis=AXX)
                        v.tensor_scalar(out=tltc, in0=ntc, scalar1=2.0, scalar2=None,
                                        op0=ALU.is_lt)
                        v.drain()
                        v.tensor_scalar(out=s8f[:], in0=logit_p[:], scalar1=m2c, scalar2=None,
                                        op0=ALU.is_ge)
                        v.drain()
                        return v.scalar_tensor_tensor(out=s8f[:], in0=s8f[:], scalar=tltc,
                                                      in1=eqf[:], op0=ALU.mult, op1=ALU.max)
                    v_tw = P.vector(f, waits=[(P.mm, v_lg)])
                    v_exp8 = P.scalar(lambda s: s.activation(e8[:], logit_p[:], AF.Exp, bias=nm1c),
                                      waits=[(P.dv, v_tw)])
                    def f(v, tt=tt):
                        v.tensor_tensor(se8[:], s8f[:], e8[:], op=ALU.mult)
                        v.drain()
                        v.reduce_sum(denc, se8[:], axis=AXX)
                        v.drain()
                        v.tensor_scalar(out=denc, in0=denc, scalar1=2.0, scalar2=None, op0=ALU.mult)
                        v.drain()
                        v.reciprocal(recc, denc)
                        v.drain()
                        v.scalar_tensor_tensor(out=w_sb[:], in0=e8[:], scalar=recc,
                                               in1=s8f[:], op0=ALU.mult, op1=ALU.mult)
                        v.drain()
                        return v.tensor_copy(wdbg_sb[:, ts(tt, 8)], w_sb[:])
                    prev_lg = v_wfin = P.vector(f, waits=[(P.ac, v_exp8)])
                    wtw = [(P.dv, prev_wexp)] if prev_wexp is not None else []
                    v_wt = P.tensor(lambda t: t.transpose(wt_p[:], w_sb[:], i128_sb[:]),
                                    waits=[(P.dv, v_wfin)] + wtw)
                    v_wtev = P.vector(lambda v: v.tensor_copy(wtT_sb[:], wt_p[:]),
                                      waits=[(P.mm, v_wt)])
                    v_we = P.tensor(
                        lambda t: t.matmul(wexp_p[:], i8e_sb[:], wtT_sb[:],
                                           start=True, stop=True),
                        waits=[(P.dv, v_wtev)] + wtw)
                    wh = [(P.dv, prev_h)] if prev_h is not None else []
                    def f(t, col0=col0):
                        inst = None
                        for kc in range(KC):
                            inst = t.matmul(h_p[:], aall_sb[:, ts(kc, 128)],
                                            xn0_sb[:, kc * TKV + col0: kc * TKV + col0 + 128],
                                            start=(kc == 0), stop=(kc == KC - 1))
                        return inst
                    v_h = P.tensor(f, waits=[(P.ld, v_wmoe)] + wh)
                    def f(v, tt=tt):
                        v.tensor_copy(h_sb[:, ts(tt, 128)], h_p[:])
                        v.drain()
                        return v.tensor_tensor(hw_sb[:, ts(tt, 128)], h_sb[:, ts(tt, 128)],
                                               wexp_p[:], op=ALU.mult)
                    prev_h = prev_wexp = P.vector(f, waits=[(P.mm, max(v_h, v_we))])
                v_hwall = P.dv.n

                NT9 = [(0, 512), (512, 512), (1024, 128)]
                prev_ev = None
                moe_st = {}
                if debug:
                    wdbg_o = nc.dram_tensor("wdbg_o", [128, 72], BF16, kind="ExternalOutput")
                    hdbg_o = nc.dram_tensor("hdbg_o", [128, 1152], BF16, kind="ExternalOutput")
                    hwdbg_o = nc.dram_tensor("hwdbg_o", [128, 1152], BF16, kind="ExternalOutput")
                    P.sdma(wdbg_o[:, :], wdbg_sb[:], waits=[(P.dv, v_hwall)])
                    P.sdma(hdbg_o[:, :], h_sb[:], waits=[(P.dv, v_hwall)])
                    P.sdma(hwdbg_o[:, :], hw_sb[:], waits=[(P.dv, v_hwall)])
                for dc in range(KC):
                    waits = [(P.dv, v_hwall)]
                    if prev_ev is not None:
                        waits.append((P.dv, prev_ev))
                    def f(t, dc=dc):
                        inst = None
                        for (o, n) in NT9:
                            inst = t.matmul(moe_p[:, o:o + n], bmf_sb[:, ts(dc, 128)],
                                            hw_sb[:, o:o + n], start=True, stop=True)
                        return inst
                    v_m = P.tensor(f, waits=waits)
                    wst = [(sm_sp, moe_st[dc - 1])] if dc >= 1 else []
                    prev_ev = P.vector(lambda v: v.tensor_copy(oslab[:, 0:1152], moe_p[:]),
                                       waits=[(P.mm, v_m)] + wst)
                    moe_st[dc] = P.sdma(moe_d[ts(dc, 128), :], oslab[:, 0:1152],
                                        waits=[(P.dv, prev_ev)], sem=sm_sp)

            P.barrier()

        o_sb = E(nc.sbuf_tensor("o_sb", [128, KC * TQ], BF16))

        with contextlib.ExitStack() as p4:
            E4 = p4.enter_context
            qbuf = E4(nc.sbuf_tensor("qbuf", [128, 2 * 512], BF16))
            qhalo = E4(nc.sbuf_tensor("qhalo", [128, KC * 3], BF16))
            pT_sb = E4(nc.sbuf_tensor("pT_sb", [128, 2 * 4096], BF16))
            ph_sb = E4(nc.sbuf_tensor("ph_sb", [128, 15], BF16))
            urow = E4(nc.sbuf_tensor("urow", [1, 512], F32))
            rrow = E4(nc.sbuf_tensor("rrow", [1, 512], F32R))
            rrow3 = E4(nc.sbuf_tensor("rrow3", [1, 4], F32))
            recb_sb = E4(nc.sbuf_tensor("recb_sb", [128, 512], F32))
            s_pA = E4(nc.psum_tensor("s_pA", [128, 1024], F32))
            s_pB = E4(nc.psum_tensor("s_pB", [128, 1024], F32))
            den_p = E4(nc.psum_tensor("den_p", [1, 512], F32))
            o_p = E4(nc.psum_tensor("o_p", [128, 512], F32))
            rb_p = E4(nc.psum_tensor("rb_p", [128, 512], F32))

            v_c0 = P.vector(
                lambda v: v.tensor_tensor(qmax2[:], qmax2[:],
                                          kmax2[:].unsqueeze(2).broadcast_to([1, 8, 4]),
                                          op=ALU.mult))
            v_c1 = P.scalar(lambda s: s.activation(encrow[:], qmax2[:], AF.Sqrt),
                            waits=[(P.dv, v_c0)])
            v_c2 = P.vector(
                lambda v: v.tensor_scalar(out=negC_row[:], in0=encrow[:],
                                          scalar1=-1.0 / math.sqrt(128.0), scalar2=None,
                                          op0=ALU.mult),
                waits=[(P.ac, v_c1)])
            v_c3 = P.tensor(
                lambda t: t.matmul(rb_p[:, 0:32], ones_rf[:, :], negC_row[:],
                                   start=True, stop=True),
                waits=[(P.dv, v_c2)])
            v_c4 = P.vector(lambda v: v.tensor_copy(negC_cols[:], rb_p[:, 0:32]),
                            waits=[(P.mm, v_c3)])
            v_c5 = P.scalar(lambda s: s.activation(encrow[:], negC_row[:], AF.Exp),
                            waits=[(P.dv, v_c2)])

            v_sp2 = sm_sp.n
            P.sdma(qhalo[:].rearrange("p (c n) -> p c n", n=3),
                   qT_d[:, 0:3].rearrange("(c p) n -> p c n", p=128), waits=[(sm_sp, v_sp2)],
                   sem=sm_qb)
            v_qh = sm_qb.n

            qpv = {}
            prevA = prevB = None
            prev_den = prev_o = prev_rb = None
            for h in range(32):
                hk = h // 4
                ncol = negC_cols[:, h:h + 1]
                waits = [(P.st, v_qh), (P.dv, v_c4)]
                if prevB is not None:
                    waits.append((P.ac, prevB))
                def f(t, hk=hk, h=h):
                    inst = None
                    for kb in range(5):
                        inst = t.matmul(s_pB[:, kb * 3:(kb + 1) * 3],
                                        kT_sb[:, hk * TKV + kb * 128: hk * TKV + (kb + 1) * 128],
                                        qhalo[:, ts(h, 3)], start=True, stop=True)
                    return inst
                v_hm = P.tensor(f, waits=waits)
                v_hmm = P.vector(
                    lambda v: v.tensor_tensor(s_pB[:, 0:15], s_pB[:, 0:15], mhalo_sb[:], op=ALU.add),
                    waits=[(P.mm, v_hm)])
                prevB = v_hx = P.scalar(
                    lambda s, ncol=ncol: s.activation(ph_sb[:], s_pB[:, 0:15], AF.Exp, bias=ncol),
                    waits=[(P.dv, v_hmm), (P.ac, v_c5)])
                wd = [(P.dv, prev_den)] if prev_den is not None else []
                def f(t):
                    inst = None
                    for kb in range(5):
                        inst = t.matmul(den_p[:, 0:3], ones_c[:], ph_sb[:, kb * 3:(kb + 1) * 3],
                                        start=(kb == 0), stop=(kb == 4))
                    return inst
                v_hd = P.tensor(f, waits=[(P.ac, v_hx)] + wd)
                def f(v, h=h):
                    v.scalar_tensor_tensor(out=urow[:, 0:3], in0=cnth_sb[:], scalar=encrow[:, h:h + 1],
                                           in1=den_p[:, 0:3], op0=ALU.mult, op1=ALU.subtract)
                    v.drain()
                    v.tensor_scalar(out=urow[:, 0:3], in0=urow[:, 0:3], scalar1=-1e-30, scalar2=None,
                                    op0=ALU.min)
                    v.drain()
                    return v.reciprocal(rrow3[:, 0:3], urow[:, 0:3])
                prev_den = v_hu = P.vector(f, waits=[(P.mm, v_hd)])
                wrb = [(P.dv, prev_rb)] if prev_rb is not None else [(P.dv, v_c4)]
                v_hrb = P.tensor(
                    lambda t: t.matmul(rb_p[:, 0:3], nones_f[:, :], rrow3[:, 0:3], start=True, stop=True),
                    waits=[(P.dv, v_hu)] + wrb)
                prev_rb = v_hrbe = P.vector(lambda v: v.tensor_copy(recb_sb[:, 0:3], rb_p[:, 0:3]),
                                            waits=[(P.mm, v_hrb)])
                wo_ = [(P.dv, prev_o)] if prev_o is not None else []
                def f(t, hk=hk):
                    inst = None
                    for kb in range(5):
                        inst = t.matmul(o_p[:, 0:3],
                                        v_sb[:, kb * 1024 + hk * 128: kb * 1024 + (hk + 1) * 128],
                                        ph_sb[:, kb * 3:(kb + 1) * 3], start=(kb == 0), stop=(kb == 4))
                    return inst
                v_ho = P.tensor(f, waits=[(P.ac, v_hx)] + wo_)
                prev_o = P.vector(
                    lambda v, h=h: v.tensor_tensor(o_sb[:, h * TQ: h * TQ + 3], o_p[:, 0:3],
                                                   recb_sb[:, 0:3], op=ALU.mult),
                    waits=[(P.mm, v_ho), (P.dv, v_hrbe)])

                for qt in range(2):
                    qb_ = qbuf[:, ts(qt, 512)]
                    wq_ = [(P.mm, qpv[(h - 1, qt)])] if (h - 1, qt) in qpv else []
                    v_q = P.sdma(qb_, qT_d[ts(h, 128), 3 + qt * 512: 3 + qt * 512 + 512],
                                 waits=wq_ + [(sm_sp, v_sp2)], sem=sm_qb)
                    kbs = list(range(4 * qt + 1, 4 * qt + 9))
                    spans = []
                    for kb in kbs:
                        lo = max(kb - 5, 4 * qt)
                        hi = min(kb - 1, 4 * qt + 3)
                        spans.append((kb, lo * 128 - qt * 512, (hi - lo + 1) * 128))
                    pslab = pT_sb[:, ts(qt, 4096)]
                    exps = []
                    for half in range(4):
                        sp = s_pA if half % 2 == 0 else s_pB
                        prevexp = prevA if half % 2 == 0 else prevB
                        wx = [(P.ac, prevexp)] if prevexp is not None else []
                        sub = spans[half * 2:(half + 1) * 2]
                        def f(t, sub=sub, sp=sp, qt=qt, hk=hk, qb_=qb_):
                            inst = None
                            for j, (kb, qo, wdt) in enumerate(sub):
                                mms = [("s", qo, wdt)]
                                qbself = kb - 5
                                if 4 * qt <= qbself <= 4 * qt + 3:
                                    mms.append(("self", (qbself - 4 * qt) * 128, 128))
                                qbfar = kb - 1
                                if 4 * qt <= qbfar <= 4 * qt + 3:
                                    mms.append(("far", (qbfar - 4 * qt) * 128, 128))
                                if kb <= 4:
                                    mms.append(("full", qo, wdt))
                                for idx, (kind, o_, n_) in enumerate(mms):
                                    st_ = (idx == 0)
                                    sp_ = (idx == len(mms) - 1)
                                    dst = sp[:, j * 512 + o_: j * 512 + o_ + n_]
                                    if kind == "s":
                                        inst = t.matmul(dst, kT_sb[:, hk * TKV + kb * 128: hk * TKV + (kb + 1) * 128],
                                                        qb_[:, o_:o_ + n_], start=st_, stop=sp_)
                                    elif kind == "self":
                                        inst = t.matmul(dst, i128_sb[:], mself_sb[:], start=st_, stop=sp_)
                                    elif kind == "far":
                                        inst = t.matmul(dst, i128_sb[:], mfar_sb[:], start=st_, stop=sp_)
                                    else:
                                        inst = t.matmul(dst, fmask_sb[:, ts(kb, 128)],
                                                        ones_r[:, 0:n_], start=st_, stop=sp_)
                            return inst
                        wpv = [(P.mm, qpv[(h - 1, qt)])] if (h - 1, qt) in qpv else []
                        v_sm = P.tensor(f, waits=[(P.st, v_q)] + wx + wpv)
                        v_x = P.scalar(
                            lambda s, half=half, sp=sp, ncol=ncol, pslab=pslab:
                                s.activation(pslab[:, half * 1024:(half + 1) * 1024], sp[:],
                                             AF.Exp, bias=ncol),
                            waits=[(P.mm, v_sm), (P.ac, v_c5), (P.dv, v_c4)] + wpv)
                        exps.append(v_x)
                        if half % 2 == 0:
                            prevA = v_x
                        else:
                            prevB = v_x
                    wd = [(P.dv, prev_den)] if prev_den is not None else []
                    def f(t, spans=spans, pslab=pslab):
                        inst = None
                        for j, (kb, qo, wdt) in enumerate(spans):
                            inst = t.matmul(den_p[:, qo:qo + wdt], ones_c[:],
                                            pslab[:, j * 512 + qo: j * 512 + qo + wdt],
                                            start=(j == 0), stop=(j == 7))
                        return inst
                    v_dm = P.tensor(f, waits=[(P.ac, exps[-1])] + wd)
                    def f(v, qt=qt, h=h):
                        v.scalar_tensor_tensor(out=urow[:], in0=cntq_sb[:, ts(qt, 512)],
                                               scalar=encrow[:, h:h + 1], in1=den_p[:],
                                               op0=ALU.mult, op1=ALU.subtract)
                        v.drain()
                        v.tensor_scalar(out=urow[:], in0=urow[:], scalar1=-1e-30, scalar2=None,
                                        op0=ALU.min)
                        v.drain()
                        return v.reciprocal(rrow[:], urow[:])
                    prev_den = v_u = P.vector(f, waits=[(P.mm, v_dm)])
                    wrb = [(P.dv, prev_rb)] if prev_rb is not None else []
                    v_rbm = P.tensor(
                        lambda t: t.matmul(rb_p[:], nones_r[:, :], rrow[:], start=True, stop=True),
                        waits=[(P.dv, v_u)] + wrb)
                    prev_rb = v_rbe = P.vector(lambda v: v.tensor_copy(recb_sb[:], rb_p[:]),
                                               waits=[(P.mm, v_rbm)])
                    wo_ = [(P.dv, prev_o)] if prev_o is not None else []
                    def f(t, spans=spans, pslab=pslab, hk=hk):
                        inst = None
                        for j, (kb, qo, wdt) in enumerate(spans):
                            inst = t.matmul(o_p[:, qo:qo + wdt],
                                            v_sb[:, kb * 1024 + hk * 128: kb * 1024 + (hk + 1) * 128],
                                            pslab[:, j * 512 + qo: j * 512 + qo + wdt],
                                            start=(j == 0), stop=(j == 7))
                        return inst
                    v_pv = P.tensor(f, waits=[(P.ac, exps[-1])] + wo_)
                    qpv[(h, qt)] = v_pv
                    prev_o = P.vector(
                        lambda v, h=h, qt=qt: v.tensor_tensor(
                            o_sb[:, h * TQ + 3 + qt * 512: h * TQ + 3 + qt * 512 + 512],
                            o_p[:], recb_sb[:], op=ALU.mult),
                        waits=[(P.mm, v_pv), (P.dv, v_rbe)])

        if debug:
            o_o = nc.dram_tensor("o_o", [128, KC * TQ], BF16, kind="ExternalOutput")
            P.sdma(o_o[:, :], o_sb[:])
        P.barrier()

        v_sp3 = P.st.n
        v_osb = P.dv.n
        with contextlib.ExitStack() as p5:
            E5 = p5.enter_context
            wobuf = E5(nc.sbuf_tensor("wobuf", [128, 2 * KC * 128], BF16))
            x0re = E5(nc.sbuf_tensor("x0re", [128, 2 * TQ], BF16))
            moesl = E5(nc.sbuf_tensor("moesl", [128, 2 * TQ], BF16))
            x1sl = E5(nc.sbuf_tensor("x1sl", [128, 2 * TQ], BF16))
            wo_p = [E5(nc.psum_tensor(f"wo_p{i}", [128, TQ], F32)) for i in range(2)]

            hist = {}
            for dc in range(KC):
                wb = wobuf[:, ts(dc % 2, KC * 128)]
                x0b = x0re[:, ts(dc % 2, TQ)]
                moeb = moesl[:, ts(dc % 2, TQ)]
                x1b = x1sl[:, ts(dc % 2, TQ)]
                wob = wo_p[dc % 2]
                wv_ = [(P.mm, hist[dc - 2])] if dc >= 2 else []
                wv2 = [(P.dv, hist[('e', dc - 2)])] if dc >= 2 else []
                wv3 = [(sm_xs, hist[('s', dc - 2)])] if dc >= 2 else []
                v_w = P.gdma(wb.rearrange("p (c n) -> p c n", n=128),
                             woT[:, ts(dc, 128)].rearrange("(c p) n -> p c n", p=128), waits=wv_,
                             sem=sm_wo)
                v_x0 = P.gdma(x0b, x0T[ts(dc, 128), 637:1664], waits=wv2, sem=sm_xr)
                v_moe = P.sdma(moeb, moe_d[ts(dc, 128), 125:1152], waits=wv2 + [(sm_sp, v_sp3)],
                               sem=sm_ms)
                wprev = [(P.dv, hist[('e', dc - 2)])] if dc >= 2 else []
                def f(t, wb=wb, wob=wob):
                    inst = None
                    for hh in range(KC):
                        for (o, n) in NTQ:
                            inst = t.matmul(wob[:, o:o + n], wb[:, ts(hh, 128)],
                                            o_sb[:, hh * TQ + o: hh * TQ + o + n],
                                            start=(hh == 0), stop=(hh == KC - 1))
                    return inst
                hist[dc] = v_m = P.tensor(f, waits=[(sm_wo, v_w), (P.dv, v_osb)] + wprev)
                def f(v, x0b=x0b, moeb=moeb, x1b=x1b, wob=wob):
                    v.tensor_tensor(x1b, wob[:], x0b, op=ALU.add)
                    v.drain()
                    return v.tensor_tensor(x1b, x1b, moeb, op=ALU.add)
                hist[('e', dc)] = v_e = P.vector(
                    f, waits=[(P.mm, v_m), (sm_xr, v_x0), (sm_ms, v_moe)] + wv3)
                hist[('s', dc)] = P.sdma(x1T[ts(dc, 128), :], x1b, waits=[(P.dv, v_e)], sem=sm_xs)

            if debug:
                xn0_o = nc.dram_tensor("xn0_o", [DM, TKV], BF16, kind="ExternalOutput")
                qT_o = nc.dram_tensor("qT_o", [DM, TQ], BF16, kind="ExternalOutput")
                moe_o = nc.dram_tensor("moe_o", [DM, 1152], BF16, kind="ExternalOutput")
                P.sdma(xn0_o[:, :], xn0_d[:, :])
                P.sdma(qT_o[:, :], qT_d[:, :])
                P.sdma(moe_o[:, :], moe_d[:, :])
            vfin = sm_xs.n
            vfin2 = P.st.n
            def fend(s, vfin=vfin, vfin2=vfin2):
                s.wait_ge(sm_xs.h, vfin)
                s.wait_ge(P.st.h, vfin2)
            sched.on("sync", fend)

        with nc.allow_low_precision("bf16 operands / f32r broadcasts by design"):
            sched.emit(nc)
    return nc



def _l0_host_consts():
    i = np.arange(128)[:, None]
    j = np.arange(128)[None, :]
    mself = np.where(i <= j, 0.0, NEG).astype(np.float32)
    mfar = np.where(i > j, 0.0, NEG).astype(np.float32)
    mhalo = np.zeros((128, 5, 3), np.float32)
    for kb in range(5):
        for kj in range(128):
            g = -640 + kb * 128 + kj
            for ii in range(3):
                p_ = ii - 3
                ok = (g <= p_) and (p_ - g < 512)
                mhalo[kj, kb, ii] = 0.0 if ok else NEG
    return {
        "i128b": np.eye(128, dtype=BF),
        "onescol": np.ones((128, 1), BF),
        "onesrow": np.ones((1, 512), BF),
        "negones": -np.ones((1, 128), np.float32),
        "onesrow_f": np.ones((1, 128), np.float32),
        "maskSelf": mself.astype(BF),
        "maskFar": mfar.astype(BF),
        "maskHalo": mhalo.reshape(128, 15),
        "fconsts": np.tile(np.array([FCONSTS], np.float32), (128, 1)),
        "i8exp": np.repeat(np.eye(8, dtype=BF), 16, axis=1),
    }


def _l0_percore(core, x):
    b, cc = core // 4, core % 4
    Q0 = cc * 1024
    lo = Q0 - 640
    pad = max(0, -lo)
    sl = x[b, max(0, lo):Q0 + 1024]
    x0s = np.concatenate([np.zeros((pad, DM), np.float32), sl], 0)
    fm = np.zeros((13, 128), np.float32)
    cntq = np.zeros(1024, np.float32)
    cnth = np.zeros(3, np.float32)
    if cc == 0:
        fm[:5] = NEG
        for ii in range(3):
            g = np.arange(-640, 0)
            p_ = ii - 3
            valid = (g <= p_) & (p_ - g < 512)
            cnth[ii] = ((g < 0) & valid).sum()
    return {
        "x0T": np.ascontiguousarray(x0s.T),
        "fullmaskT": np.repeat(fm, 1, 0).reshape(1, 13 * 128).astype(BF),
        "cntQ": cntq.reshape(1, 1024),
        "cntH": cnth.reshape(1, 3),
    }


def host_inputs_lA(inp):
    consts = _l0_host_consts()
    shared = {
        "wqT": np.ascontiguousarray(inp["L0_attn_wq"].T),
        "wkT": np.ascontiguousarray(inp["L0_attn_wk"].T),
        "wvT": np.ascontiguousarray(inp["L0_attn_wv"].T),
        "woT": np.ascontiguousarray(inp["L0_attn_wo"].T),
        "rwT": np.ascontiguousarray(inp["L0_ffn_router_w"].T),
        "rbrow": inp["L0_ffn_router_b"].reshape(1, 8),
        "AallT": np.ascontiguousarray(inp["L0_ffn_A"].reshape(128, DM).T),
        "BmF": np.ascontiguousarray(inp["L0_ffn_Bm"].transpose(0, 2, 1).reshape(128, DM)),
    }
    maps = []
    for core in range(8):
        m = dict(consts)
        m.update(shared)
        m.update(_l0_percore(core, inp["L0_attn_x"]))
        maps.append(m)
    return maps




def build_lB():
    nc = bass.Bass("TRN2", target_bir_lowering=False)
    x1T = nc.dram_tensor("x1T", [DM, TQ], BF16, kind="ExternalInput")
    wipT = nc.dram_tensor("wipT", [DM, 18560], F32, kind="ExternalInput")
    convW = nc.dram_tensor("convW", [128, 80 * 4], F32, kind="ExternalInput")
    convB = nc.dram_tensor("convB", [128, 80], F32, kind="ExternalInput")
    dtBias = nc.dram_tensor("dtBias", [128, 1], F32, kind="ExternalInput")
    Alog = nc.dram_tensor("Alog", [128, 1], F32, kind="ExternalInput")
    onescol = nc.dram_tensor("onescol", [128, 1], BF16, kind="ExternalInput")
    onesrow_f = nc.dram_tensor("onesrow_f", [1, 128], F32, kind="ExternalInput")
    fconsts = nc.dram_tensor("fconsts", [128, 4], F32, kind="ExternalInput")

    xn1T = nc.dram_tensor("xn1T", [DM, TQ], BF16, kind="ExternalOutput")
    gateT = nc.dram_tensor("gateT", [8192, 1024], BF16, kind="ExternalOutput")
    xT = nc.dram_tensor("xT", [8192, 1024], BF16, kind="ExternalOutput")
    BT = nc.dram_tensor("BT", [1024, 1024], BF16, kind="ExternalOutput")
    CT = nc.dram_tensor("CT", [1024, 1024], BF16, kind="ExternalOutput")
    dtF = nc.dram_tensor("dtF", [128, 1024], F32, kind="ExternalOutput")
    dtB = nc.dram_tensor("dtB", [128, 1024], BF16, kind="ExternalOutput")
    AdtF = nc.dram_tensor("AdtF", [128, 1024], F32, kind="ExternalOutput")
    cumrelF = nc.dram_tensor("cumrelF", [128, 1024], F32, kind="ExternalOutput")

    sched = Sched()
    NTQ = [(0, 512), (512, 512), (1024, 3)]

    with contextlib.ExitStack() as ctx:
        E = ctx.enter_context
        P = Prog(nc, ctx, sched)

        ones_c = E(nc.sbuf_tensor("ones_c", [128, 1], BF16))
        ones_rf = E(nc.sbuf_tensor("ones_rf", [1, 128], F32R))
        ones_ff = E(nc.sbuf_tensor("ones_ff", [1, 128], F32))
        cw_sb = E(nc.sbuf_tensor("cw_sb", [128, 80 * 4], F32))
        cb_sb = E(nc.sbuf_tensor("cb_sb", [128, 80], F32))
        dtb_sb = E(nc.sbuf_tensor("dtb_sb", [128, 1], F32))
        alog_sb = E(nc.sbuf_tensor("alog_sb", [128, 1], F32))
        nega_sb = E(nc.sbuf_tensor("nega_sb", [128, 1], F32))
        fc_sb = E(nc.sbuf_tensor("fc_sb", [128, 4], F32))
        for i, val in enumerate(FCONSTS):
            nc.const_aps.register(F32, val, fc_sb[:, i:i + 1])
        for dst, srcd in [(ones_c, onescol), (ones_rf, onesrow_f), (ones_ff, onesrow_f),
                          (cw_sb, convW),
                          (cb_sb, convB), (dtb_sb, dtBias), (alog_sb, Alog),
                          (fc_sb, fconsts)]:
            P.gdma(dst[:], srcd[:, :])
        v_const = P.ld.n
        v_na0 = P.scalar(lambda s: s.activation(nega_sb[:], alog_sb[:], AF.Exp),
                         waits=[(P.ld, v_const)])
        v_na = P.vector(
            lambda v: v.tensor_scalar(out=nega_sb[:], in0=nega_sb[:], scalar1=-1.0,
                                      scalar2=None, op0=ALU.mult),
            waits=[(P.ac, v_na0)])

        xn1_sb = E(nc.sbuf_tensor("xn1_sb", [128, KC * TQ], BF16))

        with contextlib.ExitStack() as b1:
            Ep = b1.enter_context
            sqslab = Ep(nc.sbuf_tensor("sqslab", [128, 2 * TQ], BF16))
            rows = Ep(nc.sbuf_tensor("rows", [1, 4 * TQ], F32))
            rstd_r = Ep(nc.sbuf_tensor("rstd_r", [1, TQ], F32R))
            negmr_r = Ep(nc.sbuf_tensor("negmr_r", [1, TQ], F32R))
            rstd_b = Ep(nc.sbuf_tensor("rstd_b", [128, TQ], F32))
            negmr_b = Ep(nc.sbuf_tensor("negmr_b", [128, TQ], F32))

            mean_r = rows[:, 0:TQ]
            m2_r = rows[:, TQ:2 * TQ]
            var_r = rows[:, 2 * TQ:3 * TQ]
            lnv_r = rows[:, 3 * TQ:4 * TQ]

            with contextlib.ExitStack() as b1s:
                msum_p = b1s.enter_context(nc.psum_tensor("msum_p", [1, TQ], F32))
                sqsum_p = b1s.enter_context(nc.psum_tensor("sqsum_p", [1, TQ], F32))
                mm_hist = {}
                for kc in range(KC):
                    buf = xn1_sb[:, ts(kc, TQ)]
                    sq = sqslab[:, ts(kc % 2, TQ)]
                    v_ld = P.sdma(buf, x1T[ts(kc, 128), :])
                    w = [(P.mm, mm_hist[kc - 2])] if kc >= 2 else []
                    v_sq = P.scalar(lambda s, buf=buf, sq=sq: s.activation(sq, buf, AF.Square),
                                    waits=[(P.st, v_ld)] + w)
                    def f(t, kc=kc, buf=buf, sq=sq):
                        inst = None
                        for (o, n) in NTQ:
                            t.matmul(msum_p[:, o:o + n], ones_c[:], buf[:, o:o + n],
                                     start=(kc == 0), stop=(kc == KC - 1))
                        for (o, n) in NTQ:
                            inst = t.matmul(sqsum_p[:, o:o + n], ones_c[:], sq[:, o:o + n],
                                            start=(kc == 0), stop=(kc == KC - 1))
                        return inst
                    mm_hist[kc] = P.tensor(f, waits=[(P.st, v_ld), (P.ac, v_sq)])
                v_stats = P.mm.n

                def f(v):
                    v.tensor_scalar(out=mean_r, in0=msum_p[:], scalar1=1.0 / DM, scalar2=None, op0=ALU.mult)
                    v.drain()
                    v.tensor_tensor(m2_r, mean_r, mean_r, op=ALU.mult)
                    v.drain()
                    return v.scalar_tensor_tensor(out=var_r, in0=sqsum_p[:], scalar=1.0 / DM, in1=m2_r,
                                                  op0=ALU.mult, op1=ALU.subtract)
                v_rows = P.vector(f, waits=[(P.mm, v_stats)])
                def f(s):
                    s.activation(lnv_r, var_r, AF.Ln, bias=1e-5)
                    s.drain()
                    return s.activation(rstd_r[:], lnv_r, AF.Exp, scale=-0.5)
                v_rstd = P.scalar(f, waits=[(P.dv, v_rows)])
                v_negmr = P.vector(
                    lambda v: v.scalar_tensor_tensor(out=negmr_r[:], in0=mean_r, scalar=-1.0,
                                                     in1=rstd_r[:], op0=ALU.mult, op1=ALU.mult),
                    waits=[(P.ac, v_rstd)])

            with contextlib.ExitStack() as b1b:
                bc_p = b1b.enter_context(nc.psum_tensor("bc_p", [128, 512], F32))
                prev_bc = None
                for row, bcast in [(rstd_r, rstd_b), (negmr_r, negmr_b)]:
                    for (o, n) in NTQ:
                        wb_ = [(P.dv, prev_bc)] if prev_bc is not None else []
                        if n >= 256:
                            v_b = P.tensor(
                                lambda t, row=row, o=o, n=n: t.matmul(
                                    bc_p[:, 0:n], ones_rf[:, :], row[:, o:o + n].bitcast(F32R),
                                    start=True, stop=True),
                                waits=[(P.dv, v_negmr)] + wb_)
                        else:
                            v_b = P.tensor(
                                lambda t, row=row, o=o, n=n: t.matmul(
                                    bc_p[:, 0:n], ones_ff[:, :], row[:, o:o + n].bitcast(F32),
                                    start=True, stop=True),
                                waits=[(P.dv, v_negmr)] + wb_)
                        prev_bc = P.vector(
                            lambda v, bcast=bcast, o=o, n=n: v.tensor_copy(bcast[:, o:o + n], bc_p[:, 0:n]),
                            waits=[(P.mm, v_b)])
                v_bc = P.dv.n

                for kc in range(KC):
                    buf = xn1_sb[:, ts(kc, TQ)]
                    sq = sqslab[:, ts(kc % 2, TQ)]
                    def f(v, buf=buf, sq=sq):
                        v.tensor_tensor(sq, buf, rstd_b[:], op=ALU.mult)
                        v.drain()
                        return v.tensor_tensor(buf, sq, negmr_b[:], op=ALU.add)
                    v_n = P.vector(f, waits=[(P.dv, v_bc)])
                    P.sdma(xn1T[ts(kc, 128), :], buf, waits=[(P.dv, v_n)])
        v_xn1 = P.dv.n

        P.barrier()

        with contextlib.ExitStack() as b2:
            Ep = b2.enter_context
            wbuf = Ep(nc.sbuf_tensor("wbuf", [128, 2 * KC * 128], BF16))
            ev = Ep(nc.sbuf_tensor("ev", [128, 2 * TQ], BF16))
            cacc = Ep(nc.sbuf_tensor("cacc", [128, 1024], F32))
            csil = Ep(nc.sbuf_tensor("csil", [128, 2 * 1024], BF16))
            dtraw = Ep(nc.sbuf_tensor("dtraw", [128, TQ], F32))
            dts = Ep(nc.sbuf_tensor("dts", [128, 5 * 1024], F32))
            ip_p = Ep(nc.psum_tensor("ip_p", [128, TQ], F32))

            hist = {}
            for oc in range(145):
                wb = wbuf[:, ts(oc % 2, KC * 128)]
                evb = ev[:, ts(oc % 2, TQ)]
                w = [(P.mm, hist[oc - 2])] if oc >= 2 else []
                v_w = P.gdma(wb.rearrange("p (c n) -> p c n", n=128),
                             wipT[:, ts(oc, 128)].rearrange("(c p) n -> p c n", p=128), waits=w)
                wprev = [(P.dv, hist[('e', oc - 1)])] if oc >= 1 else []
                def f(t, wb=wb):
                    inst = None
                    for kc in range(KC):
                        for (o, n) in NTQ:
                            inst = t.matmul(ip_p[:, o:o + n], wb[:, ts(kc, 128)],
                                            xn1_sb[:, kc * TQ + o: kc * TQ + o + n],
                                            start=(kc == 0), stop=(kc == KC - 1))
                    return inst
                hist[oc] = v_m = P.tensor(f, waits=[(P.ld, v_w), (P.dv, v_xn1)] + wprev)

                wev = []
                if ('s', oc - 2) in hist:
                    wev.append((P.st, hist[('s', oc - 2)]))
                if ('cv', oc - 2) in hist:
                    wev.append((P.dv, hist[('cv', oc - 2)]))
                if oc < 64:
                    hist[('e', oc)] = v_e = P.vector(
                        lambda v, evb=evb: v.tensor_copy(evb[:, 0:1024], ip_p[:, 3:TQ]),
                        waits=[(P.mm, v_m)] + wev)
                    hist[('s', oc)] = P.sdma(gateT[ts(oc, 128), :], evb[:, 0:1024], waits=[(P.dv, v_e)])
                elif oc < 144:
                    cc = oc - 64
                    hist[('e', oc)] = v_e = P.vector(
                        lambda v, evb=evb: v.tensor_copy(evb[:, 0:TQ], ip_p[:, 0:TQ]),
                        waits=[(P.mm, v_m)] + wev)
                    def fpool(g, evb=evb, cc=cc):
                        g.tensor_scalar(out=cacc[:], in0=evb[:, 0:1024],
                                        scalar1=cw_sb[:, cc * 4: cc * 4 + 1], scalar2=None,
                                        op0=ALU.mult)
                        inst = None
                        for tap in range(1, 4):
                            g.drain()
                            inst = g.scalar_tensor_tensor(out=cacc[:], in0=evb[:, tap:tap + 1024],
                                                          scalar=cw_sb[:, cc * 4 + tap: cc * 4 + tap + 1],
                                                          in1=cacc[:], op0=ALU.mult, op1=ALU.add)
                        return inst
    
                    wcv = [(P.dv, v_e), (P.ld, v_const)]
                    if ('c', oc - 1) in hist:
                        wcv.append((P.ac, hist[('c', oc - 1)]))
                    hist[('cv', oc)] = v_cv = P.vector(fpool, waits=wcv)
                    csb = csil[:, ts(cc % 2, 1024)]
                    wsi = [(P.dv, v_cv)]
                    if ('s', oc - 2) in hist:
                        wsi.append((P.st, hist[('s', oc - 2)]))
                    v_si = P.scalar(
                        lambda s, csb=csb, cc=cc: s.activation(csb, cacc[:], AF.Silu,
                                                               bias=cb_sb[:, cc:cc + 1]),
                        waits=wsi)
                    hist[('c', oc)] = v_si
                    if cc < 64:
                        v_s = P.sdma(xT[ts(cc, 128), :], csb, waits=[(P.ac, v_si)])
                    elif cc < 72:
                        v_s = P.sdma(BT[ts(cc - 64, 128), :], csb, waits=[(P.ac, v_si)])
                    else:
                        v_s = P.sdma(CT[ts(cc - 72, 128), :], csb, waits=[(P.ac, v_si)])
                    hist[('s', oc)] = v_s
                else:
                    v_e = P.vector(lambda v: v.tensor_copy(dtraw[:], ip_p[:, 0:TQ]),
                                   waits=[(P.mm, v_m)])
                    hist[('e', oc)] = v_e
                    rb = dts[:, 0:1024]
                    ab = dts[:, 1024:2048]
                    eb = dts[:, 2048:3072]
                    lb = dts[:, 3072:4096]
                    dt_ = dts[:, 4096:5120]
                    v_rb = P.vector(lambda v: v.tensor_scalar(out=rb, in0=dtraw[:, 3:TQ],
                                                              scalar1=dtb_sb[:, 0:1], scalar2=None,
                                                              op0=ALU.add),
                                    waits=[(P.dv, v_e), (P.ld, v_const)])
                    v_abs = P.scalar(lambda s: s.activation(ab, rb, AF.Abs),
                                     waits=[(P.dv, v_rb)])
                    def f(s):
                        s.activation(eb, ab, AF.Exp, scale=-1.0)
                        s.drain()
                        return s.activation(lb, eb, AF.Ln, bias=1.0)
                    v_ln = P.scalar(f, waits=[(P.ac, v_abs)])
                    def fv(v):
                        v.tensor_scalar(out=dt_, in0=rb, scalar1=0.0, scalar2=None, op0=ALU.max)
                        v.drain()
                        v.tensor_tensor(dt_, dt_, lb, op=ALU.add)
                        v.drain()
                        inst = v.tensor_scalar(out=ab, in0=dt_, scalar1=nega_sb[:, 0:1],
                                               scalar2=None, op0=ALU.mult)
                        return inst
                    v_dt = P.vector(fv, waits=[(P.ac, v_ln), (P.dv, v_na)])
                    def fv(v):
                        inst = None
                        for c in range(4):
                            inst = v.tensor_tensor_scan(out=eb[:, ts(c, 256)], data0=ab[:, ts(c, 256)],
                                                        data1=ab[:, ts(c, 256)], initial=0.0,
                                                        op0=ALU.add, op1=ALU.bypass)
                        return inst
                    v_cum = P.vector(fv, waits=[(P.dv, v_dt)])
                    v_dtb = P.vector(lambda v: v.tensor_copy(csil[:, 0:1024], dt_),
                                     waits=[(P.dv, v_dt), (P.ac, hist.get(('c', 143), 0)),
                                            (P.st, hist.get(('s', 142), 0))])
                    P.sdma(dtF[:, :], dt_, waits=[(P.dv, v_dt)])
                    P.sdma(dtB[:, :], csil[:, 0:1024], waits=[(P.dv, v_dtb)])
                    P.sdma(AdtF[:, :], ab, waits=[(P.dv, v_cum)])
                    P.sdma(cumrelF[:, :], eb, waits=[(P.dv, v_cum)])

            vfin = P.st.n
            sched.on("sync", lambda s, vfin=vfin: s.wait_ge(P.st.h, vfin))

        with nc.allow_low_precision("bf16 by design"):
            sched.emit(nc)
    return nc


def host_inputs_lB(inp, x1_list):
    shared = {
        "wipT": np.ascontiguousarray(inp["L1_attn_in_proj_w"].T),
        "convW": np.ascontiguousarray(inp["L1_attn_conv_w"][:, 0, :].reshape(80, 128, 4)
                                      .transpose(1, 0, 2).reshape(128, 320)),
        "convB": np.ascontiguousarray(inp["L1_attn_conv_b"].reshape(80, 128).T),
        "dtBias": inp["L1_attn_dt_bias"].reshape(128, 1),
        "Alog": inp["L1_attn_A_log"].reshape(128, 1),
        "onescol": np.ones((128, 1), BF),
        "onesrow_f": np.ones((1, 128), np.float32),
        "fconsts": np.tile(np.array([FCONSTS], np.float32), (128, 1)),
    }
    maps = []
    for core in range(8):
        m = dict(shared)
        m["x1T"] = x1_list[core]
        maps.append(m)
    return maps



CS = 256


def build_lC():
    nc = bass.Bass("TRN2", target_bir_lowering=False)
    ins = {}
    for p in range(2):
        ins[f"xTM{p}"] = nc.dram_tensor(f"xTM{p}", [4096, 1024], BF16, kind="ExternalInput")
        ins[f"Bfm{p}"] = nc.dram_tensor(f"Bfm{p}", [128, 4096], BF16, kind="ExternalInput")
        ins[f"Cfm{p}"] = nc.dram_tensor(f"Cfm{p}", [128, 4096], BF16, kind="ExternalInput")
        ins[f"dtTM{p}"] = nc.dram_tensor(f"dtTM{p}", [4096, 16], BF16, kind="ExternalInput")
        ins[f"AdtTM{p}"] = nc.dram_tensor(f"AdtTM{p}", [4096, 16], F32, kind="ExternalInput")
        ins[f"cumF{p}"] = nc.dram_tensor(f"cumF{p}", [16, 4096], F32, kind="ExternalInput")
    sel16 = nc.dram_tensor("sel16", [16, 16 * 256], F32, kind="ExternalInput")
    i128b = nc.dram_tensor("i128b", [128, 128], BF16, kind="ExternalInput")
    trilS = nc.dram_tensor("trilS", [128, 128], F32, kind="ExternalInput")
    triuI = nc.dram_tensor("triuI", [128, 128], F32, kind="ExternalInput")
    ones2 = nc.dram_tensor("ones2", [128, 128], F32, kind="ExternalInput")
    gmask = nc.dram_tensor("gmask", [128, 512], BF16, kind="ExternalInput")
    onesrow_f = nc.dram_tensor("onesrow_f", [1, 128], F32, kind="ExternalInput")
    i16f = nc.dram_tensor("i16f", [16, 16], F32, kind="ExternalInput")
    outs = {}
    for p in range(2):
        outs[p] = nc.dram_tensor(f"yTM{p}", [4096, 1024], BF16, kind="ExternalOutput")

    sched = Sched()
    with contextlib.ExitStack() as ctx:
        E = ctx.enter_context
        P = Prog(nc, ctx, sched)

        sel_r = E(nc.sbuf_tensor("sel_r", [16, 16 * 256], F32R))
        i128_sb = E(nc.sbuf_tensor("i128_sb", [128, 128], BF16))
        tril_sb = E(nc.sbuf_tensor("tril_sb", [128, 128], F32))
        triu_sb = E(nc.sbuf_tensor("triu_sb", [128, 128], F32))
        ones2_sb = E(nc.sbuf_tensor("ones2_sb", [128, 128], F32))
        gmask_sb = E(nc.sbuf_tensor("gmask_sb", [128, 512], BF16))
        ones_rf = E(nc.sbuf_tensor("ones_rf", [1, 128], F32R))
        i16_sb = E(nc.sbuf_tensor("i16_sb", [16, 16], F32))

        xtm_sb = E(nc.sbuf_tensor("xtm_sb", [128, 32 * 1024], BF16))
        bfm_sb = E(nc.sbuf_tensor("bfm_sb", [128, 4096], BF16))
        cfm_sb = E(nc.sbuf_tensor("cfm_sb", [128, 4096], BF16))
        dttm_sb = E(nc.sbuf_tensor("dttm_sb", [128, 32 * 16], BF16))
        adt_sb = E(nc.sbuf_tensor("adt_sb", [128, 32 * 16], F32))
        cum_r = E(nc.sbuf_tensor("cum_r", [16, 4096], F32R))
        ncum_r = E(nc.sbuf_tensor("ncum_r", [16, 4096], F32R))
        R_sb = E(nc.sbuf_tensor("R_sb", [128, 1024], F32))
        Rbf_sb = E(nc.sbuf_tensor("Rbf_sb", [128, 1024], BF16))
        xs_sb = E(nc.sbuf_tensor("xs_sb", [128, 2 * 1024], BF16))
        xsd_sb = E(nc.sbuf_tensor("xsd_sb", [128, 2 * 1024], BF16))
        dec_sb = E(nc.sbuf_tensor("dec_sb", [128, 64], BF16))
        btm_sb = E(nc.sbuf_tensor("btm_sb", [128, 2 * 128], BF16))
        gms_sb = E(nc.sbuf_tensor("gms_sb", [128, 512], BF16))
        tmin_sb = E(nc.sbuf_tensor("tmin_sb", [128, 1024], F32))
        lbf_sb = E(nc.sbuf_tensor("lbf_sb", [128, 1024], BF16))
        sc_sb = E(nc.sbuf_tensor("sc_sb", [128, 16 * 512], BF16))
        ysl_sb = E(nc.sbuf_tensor("ysl_sb", [128, 2 * 1024], BF16))
        ea_sb = E(nc.sbuf_tensor("ea_sb", [1, 16], F32))
        eahd_sb = E(nc.sbuf_tensor("eahd_sb", [1, 1024], F32R))
        da_sb = E(nc.sbuf_tensor("da_sb", [128, 1024], BF16))

        gm_p = E(nc.psum_tensor("gm_p", [128, 512], F32))
        tp2A = E(nc.psum_tensor("tp2A", [128, 512], F32))
        tp2B = E(nc.psum_tensor("tp2B", [128, 512], F32))
        y_p = E(nc.psum_tensor("y_p", [128, 1024], F32))
        s_p = E(nc.psum_tensor("s_p", [128, 1024], F32))
        tpb_p = E(nc.psum_tensor("tpb_p", [128, 128], F32))
        dec_p = tpb_p[:, 0:64]

        for dst, srcd in [(sel_r, sel16), (i128_sb, i128b), (tril_sb, trilS),
                          (triu_sb, triuI), (ones2_sb, ones2), (gmask_sb, gmask),
                          (ones_rf, onesrow_f), (i16_sb, i16f)]:
            P.gdma(dst[:], srcd[:, :])

        carry_yst = {}
        for p in range(2):
            for tc in range(32):
                P.sdma(xtm_sb[:, ts(tc, 1024)], ins[f"xTM{p}"][ts(tc, 128), :])
            P.sdma(bfm_sb[:], ins[f"Bfm{p}"][:, :])
            P.sdma(cfm_sb[:], ins[f"Cfm{p}"][:, :])
            P.sdma(dttm_sb[:].rearrange("p (c n) -> p c n", n=16),
                   ins[f"dtTM{p}"][:, :].rearrange("(c p) n -> p c n", p=128))
            v_ld1 = P.gdma(adt_sb[:].rearrange("p (c n) -> p c n", n=16),
                           ins[f"AdtTM{p}"][:, :].rearrange("(c p) n -> p c n", p=128))
            v_ld2 = P.gdma(cum_r[:], ins[f"cumF{p}"][:, :])
            v_lds = P.st.n
            v_ldg = P.ld.n
            v_neg = P.vector(
                lambda v: v.tensor_scalar(out=ncum_r[:], in0=cum_r[:], scalar1=-1.0,
                                          scalar2=None, op0=ALU.mult),
                waits=[(P.ld, v_ldg)])
            def f(v):
                v.memset(R_sb[:], 0.0)
                v.drain()
                return v.memset(Rbf_sb[:], 0.0)
            v_r0 = P.vector(f, waits=[(P.dv, v_neg)])

            hist = {}
            prev = dict(dec=None, gms=None, sc=None, y=None, s=None, tpb=None,
                        xs=None, da=None, rup=v_r0, yst=dict(carry_yst), tmin=None)
            for c in range(16):
                tok0 = c * CS
                t0c = tok0
                wdec = [(P.ld, v_ldg), (P.st, v_lds)]
                if prev["dec"] is not None:
                    wdec.append((P.ac, prev["dec"]))
                if prev.get("ae") is not None:
                    wdec.append((P.dv, prev["ae"]))
                if prev["tpb"] is not None:
                    wdec.append((P.dv, prev["tpb"]))
                adt0 = adt_sb[:, (tok0 // 128) * 16: (tok0 // 128) * 16 + 16]
                adt1 = adt_sb[:, (tok0 // 128 + 1) * 16: (tok0 // 128 + 1) * 16 + 16]
                def f(t, adt0=adt0, adt1=adt1):
                    t.matmul(dec_p[:, 0:16], tril_sb[:], adt0, start=True, stop=False)
                    t.matmul(dec_p[:, 0:16], ones2_sb[:], adt1, start=False, stop=True)
                    t.matmul(dec_p[:, 16:32], tril_sb[:], adt1, start=False, stop=True)
                    t.matmul(dec_p[:, 32:48], triu_sb[:], adt0, start=False, stop=True)
                    t.matmul(dec_p[:, 48:64], ones2_sb[:], adt0, start=False, stop=False)
                    return t.matmul(dec_p[:, 48:64], triu_sb[:], adt1, start=False, stop=True)
                v_dm = P.tensor(f, waits=wdec)
                prev["dec"] = v_de = P.scalar(
                    lambda s: s.activation(dec_sb[:], dec_p[:], AF.Exp),
                    waits=[(P.mm, v_dm)])
                wxs = [(P.ac, v_de)]
                if prev["xs"] is not None:
                    wxs.append((P.mm, prev["xs"]))
                def f(v, tok0=tok0):
                    inst = None
                    for ti in range(2):
                        tc = tok0 // 128 + ti
                        dtv = dttm_sb[:, tc * 16:(tc + 1) * 16].unsqueeze(2).broadcast_to([128, 16, 64])
                        v.tensor_tensor(xs_sb[:, ts(ti, 1024)], xtm_sb[:, ts(tc, 1024)], dtv,
                                        op=ALU.mult)
                        v.drain()
                        dcv = dec_sb[:, ti * 16:(ti + 1) * 16].unsqueeze(2).broadcast_to([128, 16, 64])
                        inst = v.tensor_tensor(xsd_sb[:, ts(ti, 1024)], xs_sb[:, ts(ti, 1024)], dcv,
                                               op=ALU.mult)
                        v.drain()
                    return inst
                v_xs = P.vector(f, waits=wxs)
                wg = [(P.st, v_lds)]
                if prev["gms"] is not None:
                    wg.append((P.dv, prev["gms"]))
                def f(t, t0c=t0c):
                    t.matmul(gm_p[:, 0:256], bfm_sb[:, t0c:t0c + 128], cfm_sb[:, t0c:t0c + 256],
                             start=True, stop=True)
                    return t.matmul(gm_p[:, 256:512], bfm_sb[:, t0c + 128:t0c + 256],
                                    cfm_sb[:, t0c:t0c + 256], start=False, stop=True)
                v_gm = P.tensor(f, waits=wg)
                prev["gms"] = v_gms = P.vector(
                    lambda v: v.tensor_tensor(gms_sb[:], gm_p[:], gmask_sb[:], op=ALU.mult),
                    waits=[(P.mm, v_gm)])
                wb = [(P.dv, prev["tpb"])] if prev["tpb"] is not None else []
                wb.append((P.ac, v_de))
                v_bt0 = P.tensor(lambda t, t0c=t0c: t.transpose(
                    tpb_p[:].bitcast(BF16)[:, 0:128], bfm_sb[:, t0c:t0c + 128], i128_sb[:]),
                    waits=[(P.st, v_lds)] + wb)
                v_be0 = P.vector(lambda v: v.tensor_copy(btm_sb[:, 0:128],
                                                         tpb_p[:].bitcast(BF16)[:, 0:128]),
                                 waits=[(P.mm, v_bt0)])
                v_bt1 = P.tensor(lambda t, t0c=t0c: t.transpose(
                    tpb_p[:].bitcast(BF16)[:, 0:128], bfm_sb[:, t0c + 128:t0c + 256], i128_sb[:]),
                    waits=[(P.dv, v_be0)])
                prev["tpb"] = v_be1 = P.vector(
                    lambda v: v.tensor_copy(btm_sb[:, 128:256], tpb_p[:].bitcast(BF16)[:, 0:128]),
                    waits=[(P.mm, v_bt1)])
                tmin_hist = {}
                for hh in range(16):
                    par = hh % 2
                    tp2 = tp2A if par == 0 else tp2B
                    wt = [(P.dv, tmin_hist[hh - 2])] if hh >= 2 else (
                        [(P.dv, prev["tmin"])] if prev["tmin"] is not None else [(P.dv, v_neg)])
                    def f(t, hh=hh, t0c=t0c, tp2=tp2):
                        sel0 = sel_r[:, hh * 256: hh * 256 + 128]
                        sel1 = sel_r[:, hh * 256: hh * 256 + 256]
                        t.matmul(tp2[:, 0:256], sel0, cum_r[:, t0c:t0c + 256], start=True, stop=False)
                        t.matmul(tp2[:, 0:256], ncum_r[:, t0c:t0c + 128], sel1, start=False, stop=True)
                        t.matmul(tp2[:, 256:512], sel0, cum_r[:, t0c:t0c + 256], start=False, stop=False)
                        return t.matmul(tp2[:, 256:512], ncum_r[:, t0c + 128:t0c + 256], sel1,
                                        start=False, stop=True)
                    v_t2 = P.tensor(f, waits=wt)
                    tmn = tmin_sb[:, ts(par, 512)]
                    lbf = lbf_sb[:, ts(par, 512)]
                    v_tm = P.vector(
                        lambda v, tp2=tp2, tmn=tmn: v.tensor_scalar(out=tmn, in0=tp2[:], scalar1=0.0,
                                                                    scalar2=None, op0=ALU.min),
                        waits=[(P.mm, v_t2)])
                    tmin_hist[hh] = v_tm
                    v_lx = P.scalar(lambda s, tmn=tmn, lbf=lbf: s.activation(lbf, tmn, AF.Exp),
                                    waits=[(P.dv, v_tm)])
                    hist[('sc', hh)] = P.vector(
                        lambda v, hh=hh, lbf=lbf: v.tensor_tensor(sc_sb[:, ts(hh, 512)], lbf,
                                                                  gms_sb[:], op=ALU.mult),
                        waits=[(P.ac, v_lx), (P.dv, v_gms)])
                prev["tmin"] = tmin_hist[15]
                v_scall = P.dv.n
                for ih in range(2):
                    wy = [(P.dv, prev["rup"]), (P.dv, v_scall), (P.dv, v_xs)]
                    if prev["y"] is not None:
                        wy.append((P.dv, prev["y"]))
                    def f(t, ih=ih, t0c=t0c):
                        inst = None
                        for hh in range(16):
                            inst = t.matmul(y_p[:, hh * 64:(hh + 1) * 64],
                                            cfm_sb[:, t0c + ih * 128: t0c + ih * 128 + 128],
                                            Rbf_sb[:, hh * 64:(hh + 1) * 64],
                                            start=(hh % 8 == 0), stop=False)
                        return inst
                    v_yo = P.tensor(f, waits=wy)
                    def f(v, ih=ih):
                        sdv = dec_sb[:, 32 + ih * 16: 48 + ih * 16].unsqueeze(2).broadcast_to([128, 16, 64])
                        return v.tensor_tensor(y_p[:], y_p[:], sdv, op=ALU.mult)
                    v_sd = P.vector(f, waits=[(P.mm, v_yo), (P.ac, v_de)])
                    def f(t, ih=ih):
                        inst = None
                        for hh in range(16):
                            base = hh * 512
                            if ih == 0:
                                inst = t.matmul(y_p[:, hh * 64:(hh + 1) * 64],
                                                sc_sb[:, base: base + 128],
                                                xs_sb[:, hh * 64: hh * 64 + 64],
                                                start=False, stop=False)
                            else:
                                t.matmul(y_p[:, hh * 64:(hh + 1) * 64],
                                         sc_sb[:, base + 128: base + 256],
                                         xs_sb[:, hh * 64: hh * 64 + 64],
                                         start=False, stop=False)
                                inst = t.matmul(y_p[:, hh * 64:(hh + 1) * 64],
                                                sc_sb[:, base + 384: base + 512],
                                                xs_sb[:, 1024 + hh * 64: 1024 + hh * 64 + 64],
                                                start=False, stop=True)
                        return inst
                    v_yd = P.tensor(f, waits=[(P.dv, v_sd)])
                    ysl = ysl_sb[:, ts(ih, 1024)]
                    wev = [(P.st, prev["yst"].get(ih, 0))] if prev["yst"].get(ih) else []
                    prev["y"] = v_ye = P.vector(
                        lambda v, ysl=ysl: v.tensor_copy(ysl, y_p[:]),
                        waits=[(P.mm, v_yd)] + wev)
                    prev["yst"][ih] = P.sdma(outs[p][tok0 + ih * 128: tok0 + ih * 128 + 128, :],
                                             ysl, waits=[(P.dv, v_ye)])
                ws = [(P.dv, prev["rup"]), (P.dv, v_xs), (P.dv, v_be1)]
                if prev["s"] is not None:
                    ws.append((P.dv, prev["s"]))
                def f(t):
                    inst = None
                    for hh in range(16):
                        for kh in range(2):
                            inst = t.matmul(s_p[:, hh * 64:(hh + 1) * 64],
                                            btm_sb[:, ts(kh, 128)],
                                            xsd_sb[:, kh * 1024 + hh * 64: kh * 1024 + hh * 64 + 64],
                                            start=(hh % 8 == 0 and kh == 0), stop=(kh == 1))
                    return inst
                v_sm = P.tensor(f, waits=ws)
                wda = [(P.dv, prev["y"])]
                v_at = P.tensor(
                    lambda t, t0c=t0c: t.matmul(tpb_p[0:1, 0:16],
                                                cum_r[:, t0c + 255: t0c + 256].bitcast(F32),
                                                i16_sb[:], start=True, stop=True),
                    waits=[(P.dv, prev["tpb"])])
                v_ae = P.vector(lambda v: v.tensor_copy(ea_sb[:], tpb_p[0:1, 0:16]),
                                waits=[(P.mm, v_at)])
                prev["ae"] = v_ae
                v_ax = P.scalar(lambda s: s.activation(ea_sb[:], ea_sb[:], AF.Exp),
                                waits=[(P.dv, v_ae)])
                v_ah = P.vector(
                    lambda v: v.tensor_copy(eahd_sb[:],
                                            ea_sb[:].unsqueeze(2).broadcast_to([1, 16, 64])),
                    waits=[(P.ac, v_ax)])
                def f(t):
                    t.matmul(y_p[:, 0:512], ones_rf[:, :], eahd_sb[:, 0:512], start=True, stop=True)
                    return t.matmul(y_p[:, 512:1024], ones_rf[:, :], eahd_sb[:, 512:1024],
                                    start=True, stop=True)
                v_dam = P.tensor(f, waits=[(P.dv, v_ah)] + wda)
                v_dae = P.vector(lambda v: v.tensor_copy(da_sb[:], y_p[:]),
                                 waits=[(P.mm, v_dam)])
                prev["da"] = v_dae
                def f(v):
                    v.tensor_tensor(R_sb[:], R_sb[:], da_sb[:], op=ALU.mult)
                    v.drain()
                    v.tensor_tensor(R_sb[:], R_sb[:], s_p[:], op=ALU.add)
                    v.drain()
                    return v.tensor_copy(Rbf_sb[:], R_sb[:])
                prev["rup"] = P.vector(f, waits=[(P.dv, v_dae), (P.mm, v_sm)])
                prev["y"] = prev["rup"]
            carry_yst = dict(prev["yst"])
            P.barrier()

        vfin = P.st.n
        sched.on("sync", lambda s, vfin=vfin: s.wait_ge(P.st.h, vfin))
        with nc.allow_low_precision("bf16/f32r by design"):
            sched.emit(nc)
    return nc


def host_inputs_lC(lb_outs):
    xs = {}
    for b in range(2):
        cores = [b * 4 + i for i in range(4)]
        xs[('x', b)] = np.concatenate([lb_outs[c]["xT"] for c in cores], 1)
        xs[('B', b)] = np.concatenate([lb_outs[c]["BT"] for c in cores], 1)
        xs[('C', b)] = np.concatenate([lb_outs[c]["CT"] for c in cores], 1)
        xs[('dt', b)] = np.concatenate([lb_outs[c]["dtB"] for c in cores], 1)
        xs[('Adt', b)] = np.concatenate([lb_outs[c]["AdtF"] for c in cores], 1)
        xs[('cum', b)] = np.concatenate([lb_outs[c]["cumrelF"] for c in cores], 1)
    pair_table = []
    sel = np.zeros((16, 16 * 256), np.float32)
    for hh in range(16):
        sel[hh, hh * 256:(hh + 1) * 256] = 1.0
    i = np.arange(128)[:, None]; j = np.arange(128)[None, :]
    tril_s = (i > j).astype(np.float32)
    triu_i = (i <= j).astype(np.float32)
    jj = np.arange(128)[:, None]; ii = np.arange(256)[None, :]
    g0 = (jj <= ii).astype(np.float32)
    g1 = ((jj + 128) <= ii).astype(np.float32)
    consts = {
        "sel16": sel,
        "i128b": np.eye(128, dtype=BF),
        "trilS": tril_s, "triuI": triu_i,
        "ones2": np.ones((128, 128), np.float32),
        "gmask": np.concatenate([g0, g1], 1).astype(BF),
        "onesrow_f": np.ones((1, 128), np.float32),
        "i16f": np.eye(16, dtype=np.float32),
    }
    maps = []
    for core in range(8):
        pairs = [(core // 4, (core % 4) * 2), (core // 4, (core % 4) * 2 + 1)]
        pair_table.append(pairs)
        m = dict(consts)
        for p, (b, g) in enumerate(pairs):
            m[f"xTM{p}"] = np.ascontiguousarray(xs[('x', b)][g * 1024:(g + 1) * 1024, :].T)
            m[f"Bfm{p}"] = np.ascontiguousarray(xs[('B', b)][g * 128:(g + 1) * 128, :])
            m[f"Cfm{p}"] = np.ascontiguousarray(xs[('C', b)][g * 128:(g + 1) * 128, :])
            m[f"dtTM{p}"] = np.ascontiguousarray(xs[('dt', b)][g * 16:(g + 1) * 16, :].T)
            m[f"AdtTM{p}"] = np.ascontiguousarray(xs[('Adt', b)][g * 16:(g + 1) * 16, :].T)
            m[f"cumF{p}"] = np.ascontiguousarray(xs[('cum', b)][g * 16:(g + 1) * 16, :])
        maps.append(m)
    return maps, pair_table



def build_lD():
    nc = bass.Bass("TRN2", target_bir_lowering=False)
    yT = nc.dram_tensor("yT", [8192, 1024], BF16, kind="ExternalInput")
    xT = nc.dram_tensor("xT", [8192, 1024], BF16, kind="ExternalInput")
    gateT = nc.dram_tensor("gateT", [8192, 1024], BF16, kind="ExternalInput")
    xn1T = nc.dram_tensor("xn1T", [DM, TQ], BF16, kind="ExternalInput")
    x1T = nc.dram_tensor("x1T", [DM, TQ], BF16, kind="ExternalInput")
    wopT = nc.dram_tensor("wopT", [8192, DM], F32, kind="ExternalInput")
    U1w = nc.dram_tensor("U1w", [DM, 1024], F32, kind="ExternalInput")
    V1w = nc.dram_tensor("V1w", [1024, 22016], F32, kind="ExternalInput")
    U2w = nc.dram_tensor("U2w", [11008, 1024], F32, kind="ExternalInput")
    V2w = nc.dram_tensor("V2w", [1024, DM], F32, kind="ExternalInput")
    b1c = nc.dram_tensor("b1c", [128, 172], F32, kind="ExternalInput")
    b2c = nc.dram_tensor("b2c", [128, 32], F32, kind="ExternalInput")
    nwc = nc.dram_tensor("nwc", [128, 64], F32, kind="ExternalInput")
    Dc = nc.dram_tensor("Dc", [128, 64], F32, kind="ExternalInput")
    onescol = nc.dram_tensor("onescol", [128, 1], BF16, kind="ExternalInput")
    onesrow_f = nc.dram_tensor("onesrow_f", [1, 128], F32, kind="ExternalInput")
    fconsts = nc.dram_tensor("fconsts", [128, 4], F32, kind="ExternalInput")

    x2T = nc.dram_tensor("x2T", [DM, 1024], F32, kind="ExternalOutput")

    sched = Sched()
    with contextlib.ExitStack() as ctx:
        E = ctx.enter_context
        P = Prog(nc, ctx, sched)
        ones_c = E(nc.sbuf_tensor("ones_c", [128, 1], BF16))
        ones_rf = E(nc.sbuf_tensor("ones_rf", [1, 128], F32R))
        b1_sb = E(nc.sbuf_tensor("b1_sb", [128, 172], F32))
        b2_sb = E(nc.sbuf_tensor("b2_sb", [128, 32], F32))
        nw_sb = E(nc.sbuf_tensor("nw_sb", [128, 64], F32))
        d_sb = E(nc.sbuf_tensor("d_sb", [128, 64], F32))
        fc_sb = E(nc.sbuf_tensor("fc_sb", [128, 4], F32))
        for i, val in enumerate(FCONSTS):
            nc.const_aps.register(F32, val, fc_sb[:, i:i + 1])
        for dst, srcd in [(ones_c, onescol), (ones_rf, onesrow_f), (b1_sb, b1c),
                          (b2_sb, b2c), (nw_sb, nwc), (d_sb, Dc), (fc_sb, fconsts)]:
            P.gdma(dst[:], srcd[:, :])
        v_const = P.ld.n

        with contextlib.ExitStack() as d1:
            Ep = d1.enter_context
            yn_sb = Ep(nc.sbuf_tensor("yn_sb", [128, 64 * 1024], BF16))
            ysl = Ep(nc.sbuf_tensor("ysl", [128, 2 * 1024], BF16))
            xsl = Ep(nc.sbuf_tensor("xsl", [128, 2 * 1024], BF16))
            gsl = Ep(nc.sbuf_tensor("gsl", [128, 2 * 1024], BF16))
            sgs = Ep(nc.sbuf_tensor("sgs", [128, 2 * 1024], BF16))
            sqs = Ep(nc.sbuf_tensor("sqs", [128, 2 * 1024], BF16))
            rows = Ep(nc.sbuf_tensor("rows", [1, 2 * 1024], F32))
            rstd_r = Ep(nc.sbuf_tensor("rstd_r", [1, 1024], F32R))
            ss_p = Ep(nc.psum_tensor("ss_p", [1, 1024], F32))
            rb_p = Ep(nc.psum_tensor("rb_p", [128, 1024], F32))

            hist = {}
            for g in range(8):
                for k in range(8):
                    cc = g * 8 + k
                    yb = ysl[:, ts(cc % 2, 1024)]
                    xb = xsl[:, ts(cc % 2, 1024)]
                    gb = gsl[:, ts(cc % 2, 1024)]
                    sg = sgs[:, ts(cc % 2, 1024)]
                    sq = sqs[:, ts(cc % 2, 1024)]
                    w = [(P.dv, hist[cc - 2])] if cc >= 2 else []
                    v_y = P.sdma(yb, yT[ts(cc, 128), :], waits=w)
                    v_x = P.sdma(xb, xT[ts(cc, 128), :], waits=w)
                    v_g = P.sdma(gb, gateT[ts(cc, 128), :], waits=w)
                    wsi = [(P.mm, hist[('q', cc - 2)])] if cc >= 2 else []
                    v_si = P.scalar(lambda s, gb=gb, sg=sg: s.activation(sg, gb, AF.Silu),
                                    waits=[(P.st, v_g)] + wsi)
                    def f(v, yb=yb, xb=xb, sg=sg, cc=cc):
                        v.scalar_tensor_tensor(out=yb, in0=xb, scalar=d_sb[:, cc:cc + 1],
                                               in1=yb, op0=ALU.mult, op1=ALU.add)
                        v.drain()
                        return v.tensor_tensor(yn_sb[:, ts(cc, 1024)], yb, sg, op=ALU.mult)
                    hist[cc] = v_yg = P.vector(
                        f, waits=[(P.st, max(v_y, v_x)), (P.ac, v_si), (P.ld, v_const)])
                    v_sq = P.scalar(lambda s, sq=sq, cc=cc: s.activation(sq, yn_sb[:, ts(cc, 1024)],
                                                                         AF.Square),
                                    waits=[(P.dv, v_yg)])
                    def f(t, k=k, sq=sq):
                        t.matmul(ss_p[:, 0:512], ones_c[:], sq[:, 0:512],
                                 start=(k == 0), stop=(k == 7))
                        return t.matmul(ss_p[:, 512:1024], ones_c[:], sq[:, 512:1024],
                                        start=(k == 0), stop=(k == 7))
                    hist[('q', cc)] = P.tensor(f, waits=[(P.ac, v_sq)])
                v_ssall = P.mm.n
                def f(v):
                    return v.tensor_scalar(out=rows[:, 0:1024], in0=ss_p[:], scalar1=1.0 / 1024.0,
                                           scalar2=None, op0=ALU.mult)
                v_var = P.vector(f, waits=[(P.mm, v_ssall)])
                def f(s):
                    s.activation(rows[:, 1024:2048], rows[:, 0:1024], AF.Ln, bias=1e-5)
                    s.drain()
                    return s.activation(rstd_r[:], rows[:, 1024:2048], AF.Exp, scale=-0.5)
                v_rst = P.scalar(f, waits=[(P.dv, v_var)])
                wrb = [(P.dv, hist[('rb', g - 1)])] if g >= 1 else []
                def f(t):
                    t.matmul(rb_p[:, 0:512], ones_rf[:, :], rstd_r[:, 0:512], start=True, stop=True)
                    return t.matmul(rb_p[:, 512:1024], ones_rf[:, :], rstd_r[:, 512:1024],
                                    start=True, stop=True)
                v_rbm = P.tensor(f, waits=[(P.ac, v_rst)] + wrb)
                last = None
                for k in range(8):
                    cc = g * 8 + k
                    def f(v, cc=cc):
                        return v.scalar_tensor_tensor(out=yn_sb[:, ts(cc, 1024)],
                                                      in0=yn_sb[:, ts(cc, 1024)],
                                                      scalar=nw_sb[:, cc:cc + 1],
                                                      in1=rb_p[:], op0=ALU.mult, op1=ALU.mult)
                    last = P.vector(f, waits=[(P.mm, v_rbm)])
                hist[('rb', g)] = last
            v_ynall = P.dv.n

            P.barrier()

            with contextlib.ExitStack() as d2:
                E2 = d2.enter_context
                wbuf = E2(nc.sbuf_tensor("wbuf", [128, 2 * 64 * 128], BF16))
                x1sl = E2(nc.sbuf_tensor("x1sl", [128, 2 * 1024], BF16))
                x2sl = E2(nc.sbuf_tensor("x2sl", [128, 2 * 1024], F32))
                op_p = [E2(nc.psum_tensor(f"op_p{i}", [128, 1024], F32)) for i in range(2)]
                h2 = {}
                for dc in range(KC):
                    wb = wbuf[:, ts(dc % 2, 64 * 128)]
                    x1b = x1sl[:, ts(dc % 2, 1024)]
                    x2b = x2sl[:, ts(dc % 2, 1024)]
                    wv_ = [(P.mm, h2[dc - 2])] if dc >= 2 else []
                    wv2 = [(P.dv, h2[('e', dc - 2)])] if dc >= 2 else []
                    wv3 = [(P.st, h2[('s', dc - 2)])] if dc >= 2 else []
                    v_w = P.gdma(wb.rearrange("p (c n) -> p c n", n=128),
                                 wopT[:, ts(dc, 128)].rearrange("(c p) n -> p c n", p=128),
                                 waits=wv_)
                    v_x1 = P.sdma(x1b, x1T[ts(dc, 128), 3:TQ], waits=wv2)
                    wprev = [(P.dv, h2[('e', dc - 2)])] if dc >= 2 else []
                    opb = op_p[dc % 2]
                    def f(t, wb=wb, opb=opb):
                        inst = None
                        for kk in range(64):
                            for (o, n) in [(0, 512), (512, 512)]:
                                inst = t.matmul(opb[:, o:o + n], wb[:, ts(kk, 128)],
                                                yn_sb[:, kk * 1024 + o: kk * 1024 + o + n],
                                                start=(kk == 0), stop=(kk == 63))
                        return inst
                    h2[dc] = v_m = P.tensor(f, waits=[(P.ld, v_w), (P.dv, v_ynall)] + wprev)
                    h2[('e', dc)] = v_e = P.vector(
                        lambda v, x1b=x1b, x2b=x2b, opb=opb: v.tensor_tensor(x2b, opb[:], x1b,
                                                                             op=ALU.add),
                        waits=[(P.mm, v_m), (P.st, v_x1)] + wv3)
                    h2[('s', dc)] = P.sdma(x2T[ts(dc, 128), :], x2b, waits=[(P.dv, v_e)])

        P.barrier()

        with contextlib.ExitStack() as d3:
            E3 = d3.enter_context
            p1_sb = E3(nc.sbuf_tensor("p1_sb", [128, 8 * 1024], BF16))

            with contextlib.ExitStack() as dp1:
                xn1_sb = dp1.enter_context(nc.sbuf_tensor("xn1_sb", [128, KC * 1024], BF16))
                wu1 = dp1.enter_context(nc.sbuf_tensor("wu1", [128, 2 * KC * 128], BF16))
                p1_p = dp1.enter_context(nc.psum_tensor("p1_p", [128, 1024], F32))
                for kc in range(KC):
                    P.sdma(xn1_sb[:, ts(kc, 1024)], xn1T[ts(kc, 128), 3:TQ])
                v_xn1 = P.st.n
                h3 = {}
                for rc in range(8):
                    wb = wu1[:, ts(rc % 2, KC * 128)]
                    wv_ = [(P.mm, h3[rc - 2])] if rc >= 2 else []
                    v_w = P.gdma(wb.rearrange("p (c n) -> p c n", n=128),
                                 U1w[:, ts(rc, 128)].rearrange("(c p) n -> p c n", p=128),
                                 waits=wv_)
                    wprev = [(P.dv, h3[('e', rc - 1)])] if rc >= 1 else []
                    def f(t, wb=wb):
                        inst = None
                        for kc in range(KC):
                            for (o, n) in [(0, 512), (512, 512)]:
                                inst = t.matmul(p1_p[:, o:o + n], wb[:, ts(kc, 128)],
                                                xn1_sb[:, kc * 1024 + o: kc * 1024 + o + n],
                                                start=(kc == 0), stop=(kc == KC - 1))
                        return inst
                    h3[rc] = v_m = P.tensor(f, waits=[(P.ld, v_w), (P.st, v_xn1)] + wprev)
                    h3[('e', rc)] = P.vector(
                        lambda v, rc=rc: v.tensor_copy(p1_sb[:, ts(rc, 1024)], p1_p[:]),
                        waits=[(P.mm, v_m)])
                v_p1 = P.dv.n

            P.barrier()

            act_sb = E3(nc.sbuf_tensor("act_sb", [128, 86 * 512], BF16))
            z_sb = E3(nc.sbuf_tensor("z_sb", [128, 8 * 512], BF16))
            wv1 = E3(nc.sbuf_tensor("wv1", [128, 4 * 8 * 128], BF16))
            wu2 = E3(nc.sbuf_tensor("wu2", [128, 2 * 8 * 128], BF16))
            wv2_ = E3(nc.sbuf_tensor("wv2_", [128, 2 * 8 * 128], BF16))
            sgf = E3(nc.sbuf_tensor("sgf", [128, 1024], BF16))
            x2sl2 = E3(nc.sbuf_tensor("x2sl2", [128, 2 * 512], F32))
            x2rl = E3(nc.sbuf_tensor("x2rl", [128, 2 * 512], F32))

            for tt in range(2):
                to = tt * 512
                with contextlib.ExitStack() as dh:
                    hg_p = [dh.enter_context(nc.psum_tensor(f"hg_p{tt}_{i}", [128, 512], F32))
                            for i in range(2)]
                    hu_p = [dh.enter_context(nc.psum_tensor(f"hu_p{tt}_{i}", [128, 512], F32))
                            for i in range(2)]
                    sgf2 = [sgf[:, 0:512] if False else None for _ in range(2)]
                    h4 = {}
                    for fc in range(86):
                        par = fc % 2
                        hg = hg_p[par]; hu = hu_p[par]
                        wbg = wv1[:, ts((2 * fc) % 4, 8 * 128)]
                        wbu = wv1[:, ts((2 * fc + 1) % 4, 8 * 128)]
                        wv_ = [(P.mm, h4[fc - 2])] if fc >= 2 else []
                        v_wg = P.gdma(wbg.rearrange("p (c n) -> p c n", n=128),
                                      V1w[:, ts(fc, 128)].rearrange("(c p) n -> p c n", p=128),
                                      waits=wv_)
                        v_wu = P.gdma(wbu.rearrange("p (c n) -> p c n", n=128),
                                      V1w[:, ts(fc + 86, 128)].rearrange("(c p) n -> p c n", p=128),
                                      waits=wv_)
                        wprev = [(P.ac, h4[('a', fc - 2)])] if fc >= 2 else []
                        wprev2 = [(P.dv, h4[('v', fc - 2)])] if fc >= 2 else []
                        def f(t, wbg=wbg, wbu=wbu, to=to, hg=hg, hu=hu):
                            inst = None
                            for rc in range(8):
                                t.matmul(hg[:], wbg[:, ts(rc, 128)],
                                         p1_sb[:, rc * 1024 + to: rc * 1024 + to + 512],
                                         start=(rc == 0), stop=(rc == 7))
                            for rc in range(8):
                                inst = t.matmul(hu[:], wbu[:, ts(rc, 128)],
                                                p1_sb[:, rc * 1024 + to: rc * 1024 + to + 512],
                                                start=(rc == 0), stop=(rc == 7))
                            return inst
                        h4[fc] = v_m = P.tensor(f, waits=[(P.ld, max(v_wg, v_wu)), (P.dv, v_p1)]
                                                + wprev + wprev2)
                        sgb = sgf[:, ts(par, 512)] if sgf.shape[1] >= 1024 else sgf[:]
                        h4[('a', fc)] = v_sg = P.scalar(
                            lambda s, fc=fc, hg=hg, sgb=sgb: s.activation(sgb, hg[:], AF.Silu,
                                                                          bias=b1_sb[:, fc:fc + 1]),
                            waits=[(P.mm, v_m)])
                        h4[('v', fc)] = P.vector(
                            lambda v, fc=fc, hu=hu, sgb=sgb: v.scalar_tensor_tensor(
                                out=act_sb[:, ts(fc, 512)], in0=hu[:],
                                scalar=b1_sb[:, 86 + fc: 87 + fc], in1=sgb,
                                op0=ALU.add, op1=ALU.mult),
                            waits=[(P.ac, v_sg)])
                    v_act = P.dv.n

                P.barrier()

                with contextlib.ExitStack() as dz:
                    z_p = dz.enter_context(nc.psum_tensor(f"z_p{tt}", [128, 8 * 512], F32))
                    h5 = {}
                    for fc in range(86):
                        wb = wu2[:, ts(fc % 2, 8 * 128)]
                        wv_ = [(P.mm, h5[fc - 2])] if fc >= 2 else []
                        v_w = P.gdma(wb.rearrange("p (c n) -> p c n", n=128),
                                     U2w[ts(fc, 128), :].rearrange("p (c n) -> p c n", n=128),
                                     waits=wv_)
                        def f(t, wb=wb, fc=fc):
                            inst = None
                            for rc in range(8):
                                inst = t.matmul(z_p[:, ts(rc, 512)], wb[:, ts(rc, 128)],
                                                act_sb[:, ts(fc, 512)],
                                                start=(fc == 0), stop=(fc == 85))
                            return inst
                        h5[fc] = P.tensor(f, waits=[(P.ld, v_w), (P.dv, v_act)])
                    v_zmm = P.mm.n
                    v_ze = P.vector(lambda v: v.tensor_copy(z_sb[:], z_p[:]),
                                    waits=[(P.mm, v_zmm)])

                with contextlib.ExitStack() as dz2:
                    op2_p = dz2.enter_context(nc.psum_tensor(f"op2_{tt}", [128, 512], F32))
                    v_allst = P.st.n
                    h6 = {}
                    for dc in range(KC):
                        wb = wv2_[:, ts(dc % 2, 8 * 128)]
                        x2b = x2sl2[:, ts(dc % 2, 512)]
                        x2r = x2rl[:, ts(dc % 2, 512)]
                        wv_ = [(P.mm, h6[dc - 2])] if dc >= 2 else []
                        wv2x = [(P.dv, h6[('e', dc - 2)])] if dc >= 2 else []
                        wv3x = [(P.st, h6[('s', dc - 2)])] if dc >= 2 else []
                        v_w = P.gdma(wb.rearrange("p (c n) -> p c n", n=128),
                                     V2w[:, ts(dc, 128)].rearrange("(c p) n -> p c n", p=128),
                                     waits=wv_)
                        v_xr = P.sdma(x2r, x2T[ts(dc, 128), to:to + 512],
                                      waits=wv2x + [(P.st, v_allst)])
                        wprev = [(P.dv, h6[('e', dc - 1)])] if dc >= 1 else []
                        def f(t, wb=wb):
                            inst = None
                            for rc in range(8):
                                inst = t.matmul(op2_p[:], wb[:, ts(rc, 128)],
                                                z_sb[:, ts(rc, 512)],
                                                start=(rc == 0), stop=(rc == 7))
                            return inst
                        h6[dc] = v_m = P.tensor(f, waits=[(P.ld, v_w), (P.dv, v_ze)] + wprev)
                        def f(v, x2b=x2b, x2r=x2r, dc=dc):
                            return v.scalar_tensor_tensor(out=x2b, in0=op2_p[:],
                                                          scalar=b2_sb[:, dc:dc + 1], in1=x2r,
                                                          op0=ALU.add, op1=ALU.add)
                        h6[('e', dc)] = v_e = P.vector(f, waits=[(P.mm, v_m), (P.st, v_xr),
                                                                 (P.ld, v_const)] + wv3x)
                        h6[('s', dc)] = P.sdma(x2T[ts(dc, 128), to:to + 512], x2b,
                                               waits=[(P.dv, v_e)])
                P.barrier()

        vfin = P.st.n
        sched.on("sync", lambda s, vfin=vfin: s.wait_ge(P.st.h, vfin))
        with nc.allow_low_precision("bf16 by design"):
            sched.emit(nc)
    return nc


def host_inputs_lD(inp, lb_outs, y_fm_list):
    shared = {
        "wopT": np.ascontiguousarray(inp["L1_attn_out_proj_w"].T),
        "U1w": inp["L1_ffn_U1"],
        "V1w": inp["L1_ffn_V1"],
        "U2w": inp["L1_ffn_U2"],
        "V2w": inp["L1_ffn_V2"],
        "b1c": np.ascontiguousarray(inp["L1_ffn_b1"].reshape(172, 128).T),
        "b2c": np.ascontiguousarray(inp["L1_ffn_b2"].reshape(32, 128).T),
        "nwc": np.ascontiguousarray(inp["L1_attn_norm_w"].reshape(64, 128).T),
        "Dc": np.ascontiguousarray(np.repeat(inp["L1_attn_D"], 64).reshape(64, 128).T),
        "onescol": np.ones((128, 1), BF),
        "onesrow_f": np.ones((1, 128), np.float32),
        "fconsts": np.tile(np.array([FCONSTS], np.float32), (128, 1)),
    }
    maps = []
    for core in range(8):
        m = dict(shared)
        m["yT"] = y_fm_list[core]
        m["xT"] = lb_outs[core]["xT"]
        m["gateT"] = lb_outs[core]["gateT"]
        m["xn1T"] = lb_outs[core]["xn1T"]
        m["x1T"] = lb_outs[core]["x1T"]
        maps.append(m)
    return maps



def kernel(**inputs):
    inp = {k: np.asarray(v) for k, v in inputs.items()}

    ncA = build_lA()
    resA = run_bass_kernel_spmd(ncA, host_inputs_lA(inp), core_ids=list(range(8)))
    x1_list = [np.asarray(resA.results[c]["x1T"]) for c in range(8)]

    ncB = build_lB()
    resB = run_bass_kernel_spmd(ncB, host_inputs_lB(inp, x1_list), core_ids=list(range(8)))
    keysB = ["xn1T", "gateT", "xT", "BT", "CT", "dtF", "dtB", "AdtF", "cumrelF"]
    lb = [{k: np.asarray(resB.results[c][k]) for k in keysB} for c in range(8)]
    for c in range(8):
        lb[c]["x1T"] = x1_list[c]

    ncC = build_lC()
    mapsC, pair_table = host_inputs_lC(lb)
    resC = run_bass_kernel_spmd(ncC, mapsC, core_ids=list(range(8)))

    y_fm_list = []
    for core in range(8):
        b, cc = core // 4, core % 4
        y_fm = np.empty((8192, 1024), BF)
        for g in range(8):
            owner = b * 4 + g // 2
            p = g % 2
            ytm = np.asarray(resC.results[owner][f"yTM{p}"])
            y_fm[g * 1024:(g + 1) * 1024, :] = ytm[cc * 1024:(cc + 1) * 1024, :].T
        y_fm_list.append(np.ascontiguousarray(y_fm))

    ncD = build_lD()
    resD = run_bass_kernel_spmd(ncD, host_inputs_lD(inp, lb, y_fm_list), core_ids=list(range(8)))

    out = np.empty((2, 4096, DM), np.float32)
    for core in range(8):
        b, cc = core // 4, core % 4
        out[b, cc * 1024:(cc + 1) * 1024, :] = np.asarray(resD.results[core]["x2T"]).T
    return out

